# version 09
# baseline (speedup 1.0000x reference)
import math
import numpy as np
from contextlib import ExitStack

import concourse.bass as bass
import concourse.mybir as mybir
from concourse.bass_utils import run_bass_kernel_spmd

F32 = mybir.dt.float32
BF16 = mybir.dt.bfloat16
AF = mybir.ActivationFunctionType
ALU = mybir.AluOpType
AX = mybir.AxisListType

T, D, NS, L, M, FF = 2048, 1024, 64, 128, 16, 4096
DA = 3
DBd = 4
NK = DA + DBd + 1
NMOM = DA + DBd + DA + 1
LAM0 = math.exp(-8.0)
TSn = 2048.0
EPS = 1e-5

_DEBUG_TAPS = False
_STOP_AFTER = None

_DRAIN_OPS = {
    "tensor_copy", "tensor_add", "tensor_sub", "tensor_mul", "tensor_scalar",
    "tensor_scalar_mul", "tensor_scalar_add", "tensor_scalar_max",
    "tensor_scalar_min", "tensor_reduce", "reduce_sum", "reduce_max",
    "scalar_tensor_tensor", "reciprocal", "memset", "activation", "select",
}


class _DR:

    def __init__(self, eng):
        self._e = eng

    def __getattr__(self, nm):
        f = getattr(self._e, nm)
        if nm in _DRAIN_OPS:
            def g(*a, **k):
                r = f(*a, **k)
                self._e.drain()
                return r
            return g
        return f


def _host_constants():
    c = {}
    ii = np.arange(L, dtype=np.float64)
    dif = ii[None, :] - ii[:, None]
    mask = dif >= 0
    cP = np.zeros((128, (DA + 1) * 128), dtype=np.float64)
    for k in range(DA + 1):
        cP[:, k * 128:(k + 1) * 128] = np.where(
            mask, np.exp(-LAM0 * np.maximum(dif, 0)) *
            (np.maximum(dif, 0) / TSn) ** k / math.factorial(k), 0.0)
    c["cP"] = cP.astype(np.float32)

    r1 = (L - 1 - ii)
    qw = np.stack([np.exp(-LAM0 * r1) * (r1 / TSn) ** a / math.factorial(a)
                   for a in range(DA + 1)], axis=1)
    cQW = np.zeros((128, M * 128), dtype=np.float64)
    for cc in range(M):
        for a in range(DA + 1):
            cQW[:, cc * 128 + a * 32 + cc] = qw[:, a]
    c["cQW"] = cQW.astype(np.float32)

    ccs = np.arange(M, dtype=np.float64)
    dcc = ccs[None, :] - ccs[:, None] - 1.0
    mc = dcc >= 0
    r2 = np.maximum(dcc, 0.0) * L
    cTC = np.zeros((128, 256), dtype=np.float64)
    for a in range(DA + 1):
        for b in range(DBd + 1):
            blk = np.where(mc, np.exp(-LAM0 * r2) * (r2 / TSn) ** b / math.factorial(b), 0.0)
            kap = a + b
            base = (kap % 2) * 128 + (kap // 2) * 32
            for cc in range(M):
                cTC[a * 32:a * 32 + 16, base + cc] = blk[:, cc]
    c["cTC"] = cTC.astype(np.float32)

    r3 = ii + 1.0
    we = np.stack([np.exp(-LAM0 * r3) * (r3 / TSn) ** e / math.factorial(e)
                   for e in range(DA + 1)])
    cWE = np.zeros((128, M * 128), dtype=np.float64)
    for cc in range(M):
        for e2 in range(DA + 1):
            cWE[e2 * 32 + cc, cc * 128:(cc + 1) * 128] = we[e2]
    c["cWE"] = cWE.astype(np.float32)

    c["cID"] = np.eye(128, dtype=np.float32)
    return c


def build_nc(debug_taps=False, stop_after=None):
    nc = bass.Bass("TRN2", target_bir_lowering=False)

    def din(name, shape, dtype=F32):
        return nc.dram_tensor(name, shape, dtype, kind="ExternalInput")

    x_in = din("x", [T, D])
    logA = din("L0_attn_log_A", [D, NS])
    Bp = din("L0_attn_B_param", [D, NS])
    Cp = din("L0_attn_C_param", [D, NS])
    Dp = din("L0_attn_D_param", [D])
    logdt = din("L0_attn_log_dt", [D])
    gam = din("L0_attn_ln_gamma", [D])
    bet = din("L0_attn_ln_beta", [D])
    Wout = din("L0_attn_W_out", [D, D])
    bout = din("L0_attn_b_out", [D])
    W1 = din("L0_ffn_W1", [FF, D])
    b1 = din("L0_ffn_b1", [FF])
    W2 = din("L0_ffn_W2", [D, FF])
    b2 = din("L0_ffn_b2", [D])
    cP_in = din("cP", [128, (DA + 1) * 128])
    cQW_in = din("cQW", [128, M * 128])
    cTC_in = din("cTC", [128, 256])
    cWE_in = din("cWE", [128, M * 128])
    cID_in = din("cID", [128, 128])

    out_t = nc.dram_tensor("out", [T, D], F32, kind="ExternalOutput")

    scr_mrow = nc.dram_tensor("scr_mrow", [NMOM, D], F32)
    scr_rows = nc.dram_tensor("scr_rows", [2, D], F32)
    scr_h = nc.dram_tensor("scr_h", [32, 128, T], BF16)

    taps = {}
    if debug_taps:
        for nm, shp in [("tap_xn", [T, D]), ("tap_xn2", [T, D]), ("tap_y", [T, D]),
                        ("tap_q", [128, D]), ("tap_V", [128, D]),
                        ("tap_mom", [128, NMOM * 8]), ("tap_invs", [128, 2])]:
            taps[nm] = nc.dram_tensor(nm, shp, BF16 if nm == "tap_y" else F32,
                                      kind="ExternalOutput")

    st = ExitStack()
    stX = ExitStack()
    stA = ExitStack()
    stB = ExitStack()
    stC = ExitStack()
    stD = ExitStack()
    en = st.enter_context
    with st:
        cst = en(nc.sbuf_tensor("cst", [128, 14], F32))
        (C_EPS, C_1, C_NHALF, C_PHALF, C_HALF, C_INVD, C_N1,
         C_LAM0, C_NT, C_2, C_INVW, C_W8, C_0, C_SP) = range(14)
        onesc = en(nc.sbuf_tensor("onesc", [128, 1], F32))
        for _v, _c in [(0.0, C_0), (1.0, C_1), (2.0, C_2), (0.5, C_HALF), (-0.5, C_NHALF)]:
            nc.const_aps.register(F32, _v, cst[:, _c:_c + 1])
        invs = en(nc.sbuf_tensor("invs", [128, 2], F32))
        b1sb = en(nc.sbuf_tensor("b1sb", [128, 32], F32))
        stat_a = en(nc.sbuf_tensor("stat_a", [128, M], F32))
        stat_b = en(nc.sbuf_tensor("stat_b", [128, M], F32))
        tiny = en(nc.sbuf_tensor("tiny", [128, 16], F32))
        (TY_SX, TY_SXX, TY_MU, TY_EX2, TY_MSQ, TY_VAR, TY_RSTD,
         TY_T1, TY_Y2, TY_RSTD2, TY_MR, TY_MU2, TY_VAR2, TY_RSTDB) = range(14)
        cID = en(nc.sbuf_tensor("sb_cID", [128, 128], BF16))
        wchunk = [en(nc.sbuf_tensor(f"wch{i}", [128, 1024], F32)) for i in range(2)]
        W2T = en(nc.sbuf_tensor("W2T", [128, 32 * 1024], BF16))
        yT = en(nc.sbuf_tensor("yT", [128, 8 * T], BF16))

        xnT = stX.enter_context(nc.sbuf_tensor("xnT", [128, 8 * T], BF16))
        Vt = stX.enter_context(nc.sbuf_tensor("Vt", [128, D], BF16))
        xn2sb = stX.enter_context(nc.sbuf_tensor("xn2sb", [128, M * 1024], BF16))

        enA = stA.enter_context
        cQW = enA(nc.sbuf_tensor("sb_cQW", [128, M * 128], BF16))
        cTC = enA(nc.sbuf_tensor("sb_cTC", [128, 256], BF16))
        gbc = enA(nc.sbuf_tensor("gbc", [128, D], BF16))
        bbc = enA(nc.sbuf_tensor("bbc", [128, D], BF16))
        mom = enA(nc.sbuf_tensor("mom", [128, NMOM * 8], F32))
        dt8 = enA(nc.sbuf_tensor("dt8", [128, 8], F32))
        dte = enA(nc.sbuf_tensor("dte", [128, 8], F32))
        mrow_s = enA(nc.sbuf_tensor("mrow_s", [NMOM * 8, 128], F32))
        rowA = enA(nc.sbuf_tensor("rowA", [8, 128], F32))
        rowB = enA(nc.sbuf_tensor("rowB", [8, 128], F32))
        xt = [enA(nc.sbuf_tensor(f"xt{i}", [128, D], F32)) for i in range(2)]
        xn_f = enA(nc.sbuf_tensor("xn_f", [128, D], F32))
        xsq = enA(nc.sbuf_tensor("xsq", [128, D], F32))
        xn_b = [enA(nc.sbuf_tensor(f"xnb{i}", [128, D], BF16)) for i in range(2)]
        mstk1 = xn_b
        zrep = enA(nc.sbuf_tensor("zrep", [128, D], BF16))
        q64 = enA(nc.sbuf_tensor("q64", [128, D], BF16))

        cIDf = xn_f[:, 0:128]
        pA = xt[0][:, 0:512]
        pB = xt[0][:, 512:1024]
        pC = xt[1][:, 0:512]
        dtx = xt[1][:, 512:1024]
        b1row = xsq[0:32, 0:128]
        dtrow = xsq[32:40, 0:128]

        ps_a = en(nc.psum_tensor("ps_a", [128, 512], F32))
        ps_b = en(nc.psum_tensor("ps_b", [128, 512], F32))
        ps_c = en(nc.psum_tensor("ps_c", [128, 1024], F32))
        ps_d = en(nc.psum_tensor("ps_d", [128, 1024], BF16))
        ps_e = en(nc.psum_tensor("ps_e", [128, 1024], F32))

        s_dv = en(nc.semaphore("s_dv"))
        s_te = en(nc.semaphore("s_te"))
        s_a2 = en(nc.semaphore("s_a2"))
        s_wa = en(nc.semaphore("s_wa"))
        s_wm = en(nc.semaphore("s_wm"))
        s_wv = en(nc.semaphore("s_wv"))
        SEM = {"dv": s_dv, "te": s_te, "a2": s_a2}
        ctr = {k: 0 for k in SEM}

        def bump(inst, which, n=1):
            inst.then_inc(SEM[which], n)
            ctr[which] += n
            return ctr[which]

        streams = {}
        for _n in ["ld", "cst", "iv", "mk", "b3l",
                   "x0", "x1", "xn0", "xn1", "wp0", "wp1", "wq0", "wq1",
                   "x20", "x21", "w1l0", "w1l1", "w1w0", "w1w1",
                   "w1c0", "w1c1", "w2l0", "w2l1", "hw0", "hw1",
                   "hge0", "hge1", "hgo0", "hgo1", "xr0", "xr1",
                   "ow0", "ow1", "tap", "tapg", "tg0", "tg1"]:
            streams[_n] = {"sem": en(nc.semaphore("st_" + _n)), "n": 0}

        def sdma(q, name, out, in_):
            s = streams[name]
            q.dma_start(out, in_).then_inc(s["sem"], 16)
            s["n"] += 16
            return s["n"]

        def swt(eng, name, val=None):
            s = streams[name]
            eng.wait_ge(s["sem"], s["n"] if val is None else val)

        csl = lambda c: cst[:, c:c + 1]
        ps_d3 = ps_d.ap().rearrange("p (b f) -> p b f", f=128)
        xnT3 = xnT.ap().rearrange("p (b t) -> p b t", t=T)
        yT3 = yT.ap().rearrange("p (b t) -> p b t", t=T)
        W2T3 = W2T.ap().rearrange("p (f c) -> p f c", c=1024)

        def bc_ap(dt_, n=128, width=D, off=0):
            return bass.AP(dt_, off, [[0, n], [1, width]])

        def _early_out(open_stacks):
            with nc.Block() as blk2:
                @blk2.sync
                def _(sy):
                    sdma(sy, "ld", out_t.ap(), x_in.ap())
                    swt(sy, "ld")
                    for nm2, s2 in streams.items():
                        sy.wait_ge(s2["sem"], s2["n"]) if False else None

                @blk2.scalar
                def _(ac):
                    for nm2 in ["xn0", "xn1", "hw0", "hw1", "hgo0", "hgo1",
                                "xr0", "xr1", "tap"]:
                        swt(ac, nm2, streams[nm2]["n"])

                @blk2.sync
                def _(sy):
                    for nm2 in ["ld", "x0", "x1", "iv", "x20", "x21", "w1w0", "w1w1",
                                "w1c0", "w1c1", "hge0", "hge1", "wq0", "wq1"]:
                        swt(sy, nm2, streams[nm2]["n"])

                @blk2.gpsimd
                def _(gp):
                    for nm2 in ["cst", "mk", "b3l", "wp0", "wp1", "w1l0", "w1l1",
                                "w2l0", "w2l1", "tapg", "tg0", "tg1"]:
                        swt(gp, nm2, streams[nm2]["n"])
            for s2 in open_stacks:
                s2.close()


        with nc.Block() as blk:
            @blk.vector
            def _(dv):
                dv = _DR(dv)
                bump(dv.memset(xsq[:, :], 0.0), "dv")

            @blk.sync
            def _(sy):
                sdma(sy, "ld", cIDf, cID_in.ap())
                sy.wait_ge(s_dv, 1)
                sdma(sy, "ld", b1row, b1.ap().rearrange("(c p) -> c p", p=128))
                sdma(sy, "ld", dtrow, logdt.ap().rearrange("(b p) -> b p", p=128))
                ap_dn = [[NS, 128], [128 * NS, 8], [1, NS]]
                sdma(sy, "ld", pA, bass.AP(logA, 0, ap_dn))
                sdma(sy, "ld", pB, bass.AP(Bp, 0, ap_dn))
                sdma(sy, "ld", pC, bass.AP(Cp, 0, ap_dn))
                sdma(sy, "ld", rowA[:, :], bass.AP(bout, 0, [[128, 8], [1, 128]]))
                sdma(sy, "ld", rowB[:, :], bass.AP(b2, 0, [[128, 8], [1, 128]]))
            b0_ld = streams["ld"]["n"]

            @blk.gpsimd
            def _(gp):
                sdma(gp, "cst", cQW[:, :], cQW_in.ap())
                sdma(gp, "cst", gbc[:, :], bc_ap(gam))
                sdma(gp, "cst", bbc[:, :], bc_ap(bet))
                sdma(gp, "cst", cTC[:, :], cTC_in.ap())
                sdma(gp, "cst", cID[:, :], cID_in.ap())
            b0_cst = streams["cst"]["n"]

            @blk.tensor
            def _(te):
                swt(te, "ld", b0_ld)
                te.wait_ge(s_dv, 1)
                bump(te.transpose(ps_c[:, 0:128], xsq[:, 0:128], cIDf), "te")
            b0_te = ctr["te"]

            @blk.vector
            def _(dv):
                dv = _DR(dv)
                for col, val in [(C_EPS, EPS), (C_1, 1.0), (C_NHALF, -0.5), (C_PHALF, 0.5),
                                 (C_HALF, 0.5), (C_INVD, 1.0 / 1024.0), (C_N1, -1.0),
                                 (C_LAM0, LAM0), (C_NT, -TSn), (C_2, 2.0),
                                 (C_INVW, 1.0 / (FF * D)), (C_W8, 1e-8),
                                 (C_0, 0.0), (C_SP, 0.0)]:
                    dv.memset(cst[:, col:col + 1], val)
                dv.memset(onesc[:, :], 1.0)
                dv.wait_ge(s_te, b0_te)
                dv.tensor_copy(b1sb[:, :], ps_c[:, 0:32])
                bump(dv.tensor_copy(dt8[:, :], ps_c[:, 32:40]), "dv")
                swt(dv, "ld", b0_ld)
                bump(dv.tensor_add(rowA[:, :], rowA[:, :], rowB[:, :]), "dv")
            b0_dv = ctr["dv"]

            @blk.scalar
            def _(ac):
                ac = _DR(ac)
                ac.wait_ge(s_dv, b0_dv - 1)
                bump(ac.activation(dte[:, :], dt8[:, :], AF.Exp), "a2")
                swt(ac, "ld", b0_ld)
                bump(ac.activation(pA, pA, AF.Exp), "a2")
            b0_a2 = ctr["a2"]

            @blk.sync
            def _(sy):
                sy.wait_ge(s_dv, b0_dv)
                sdma(sy, "ld", bass.AP(scr_rows, 0, [[128, 8], [1, 128]]),
                     rowA[:, :])
            b0_ld2 = streams["ld"]["n"]

        if stop_after == "B0":
            _early_out([stA, stX])
            return nc

        with nc.Block() as blk:
            @blk.vector
            def _(dv):
                dv = _DR(dv)
                dv.wait_ge(s_a2, b0_a2)
                dtx3 = dtx.rearrange("p (b n) -> p b n", n=64)
                for n in range(64):
                    dv.tensor_copy(dtx3[:, :, n:n + 1], dte[:, :])
                dv.tensor_mul(pA, pA, dtx)
                dv.tensor_mul(pB, pB, dtx)
                dv.tensor_mul(pC, pC, pB)
                dv.tensor_scalar(dtx, pA, csl(C_LAM0), csl(C_NT),
                                 ALU.subtract, ALU.mult)
                for k in range(NMOM):
                    dv.tensor_reduce(mom[:, k * 8:(k + 1) * 8],
                                     pC.rearrange("p (b n) -> p b n", n=64),
                                     AX.X, ALU.add)
                    if k < NMOM - 1:
                        dv.tensor_mul(pC, pC, dtx)
                bump(dv.tensor_copy(tiny[:, TY_MR:TY_MR + 1], csl(C_1)), "dv")
            b0b_dv = ctr["dv"]

            @blk.tensor
            def _(te):
                te.wait_ge(s_dv, b0b_dv)
                bump(te.transpose(ps_c[0:NMOM * 8, 0:128], mom[:, :], cIDf), "te")
            b0b_te = ctr["te"]

            @blk.vector
            def _(dv):
                dv = _DR(dv)
                dv.wait_ge(s_te, b0b_te)
                bump(dv.tensor_copy(mrow_s[:, :], ps_c[0:NMOM * 8, 0:128]), "dv")
            b0b_dv2 = ctr["dv"]

            @blk.sync
            def _(sy):
                sy.wait_ge(s_dv, b0b_dv2)
                sdma(sy, "ld", scr_mrow.ap().rearrange("k (b p) -> (k b) p", p=128),
                     mrow_s[:, :])
            b0b_ld = streams["ld"]["n"]

            b0b_cst = streams["cst"]["n"]

            if debug_taps:
                @blk.scalar
                def _(ac):
                    ac.wait_ge(s_dv, b0b_dv)
                    sdma(ac, "tap", taps["tap_mom"].ap(), mom[:, :])
        tap_b1 = streams["tap"]["n"]
        if stop_after == "B0b":
            _early_out([stA, stX])
            return nc

        wp1 = [("W1", k) for k in range(32)] + [("W2", k) for k in range(32)]

        def wp1_ap(which, k, half=None):
            lo = 0 if half is None else half * 512
            hi = 1024 if half is None else (half + 1) * 512
            if which == "W1":
                return W1.ap()[k * 128:(k + 1) * 128, lo:hi]
            dblk, fq = divmod(k, 4)
            return W2.ap()[dblk * 128:(dblk + 1) * 128,
                           fq * 1024 + lo:fq * 1024 + hi]

        a20 = ctr["a2"]; dv0 = ctr["dv"]; te0 = ctr["te"]
        E_sy_x = lambda tc: 16 * (tc // 2 + 1)
        E_a2_sq = lambda k: (a20 + 1) if k == 0 else (a20 + 5 * k - 3)
        E_a2_rstd = lambda tc: a20 + 5 * tc + 3
        E_a2_xnb = lambda tc: a20 + 5 * tc + 4
        E_a2_rstd2 = lambda tc: a20 + 5 * tc + 5
        E_a2_evT = lambda k: (a20 + 5 * k + 11) if k < M - 1 else (a20 + 5 * M + 2)
        E_dv_var = lambda tc: (dv0 + 1) if tc == 0 else (dv0 + 4 * tc - 2)
        E_dv_apply = lambda tc: dv0 + 4 * tc + 3
        E_dv_y2 = lambda tc: dv0 + 4 * tc + 4
        E_dv_xn2 = lambda tc: dv0 + 4 * tc + 5
        E_te_tr = lambda tc: te0 + 2 * tc + 1
        E_te_q = lambda tc: te0 + 2 * tc + 2
        MUc = lambda t: TY_MU if t % 2 == 0 else TY_MU2
        VARc = lambda t: TY_VAR if t % 2 == 0 else TY_VAR2
        RSTDc = lambda t: TY_RSTD if t % 2 == 0 else TY_RSTDB

        with nc.Block() as blk:
            @blk.sync
            def _(sy):
                for tc in range(M):
                    if tc >= 2:
                        sy.wait_ge(s_dv, E_dv_apply(tc - 2))
                    sdma(sy, f"x{tc % 2}", xt[tc % 2][:, :],
                         x_in.ap()[tc * 128:(tc + 1) * 128, :])

            @blk.scalar
            def _(ac):
                ac = _DR(ac)
                swt(ac, "x0", E_sy_x(0))
                bump(ac.activation(xsq[:, :], xt[0][:, :], AF.Square), "a2")
                for tc in range(M):
                    if tc + 1 < M:
                        t1 = tc + 1
                        swt(ac, f"x{t1 % 2}", E_sy_x(t1))
                        ac.wait_ge(s_dv, E_dv_xn2(tc - 1) if tc >= 1
                                   else E_dv_var(0))
                        bump(ac.activation(xsq[:, :], xt[t1 % 2][:, :], AF.Square),
                             "a2")
                    else:
                        bump(ac.activation(tiny[:, TY_MR:TY_MR + 1], csl(C_1),
                                           AF.Copy, bias=0.0, scale=1.0), "a2")
                    ac.wait_ge(s_dv, E_dv_var(tc))
                    bump(ac.activation(tiny[:, RSTDc(tc):RSTDc(tc) + 1],
                                       tiny[:, VARc(tc):VARc(tc) + 1], AF.Sqrt,
                                       bias=csl(C_EPS), scale=csl(C_1)), "a2")
                    ac.wait_ge(s_dv, E_dv_apply(tc))
                    if tc >= 2:
                        ac.wait_ge(s_te, E_te_tr(tc - 2))
                    bump(ac.activation(xn_b[tc % 2][:, :], xn_f[:, :], AF.Copy,
                                       bias=0.0, scale=1.0), "a2")
                    ac.wait_ge(s_dv, E_dv_y2(tc))
                    bump(ac.activation(tiny[:, TY_RSTD2:TY_RSTD2 + 1],
                                       tiny[:, TY_Y2:TY_Y2 + 1], AF.Sqrt,
                                       bias=csl(C_EPS), scale=csl(C_1)), "a2")
                    if tc >= 1:
                        ac.wait_ge(s_te, E_te_tr(tc - 1))
                        bump(ac.activation(xnT3[:, :, (tc - 1) * 128:tc * 128], ps_d3,
                                           AF.Copy, bias=0.0, scale=1.0), "a2")
                    else:
                        bump(ac.activation(tiny[:, TY_MR:TY_MR + 1], csl(C_1),
                                           AF.Copy, bias=0.0, scale=1.0), "a2")
                    if debug_taps:
                        sdma(ac, "tap", taps["tap_xn"].ap()[tc * 128:(tc + 1) * 128, :],
                             xn_f[:, :])
                ac.wait_ge(s_te, E_te_tr(M - 1))
                bump(ac.activation(xnT3[:, :, (M - 1) * 128:M * 128], ps_d3,
                                   AF.Copy, bias=0.0, scale=1.0), "a2")
                assert ctr["a2"] == a20 + 5 * M + 2

            @blk.vector
            def _(dv):
                dv = _DR(dv)
                swt(dv, "ld", b0_ld)
                swt(dv, "cst", b0b_cst)

                def stageA(t):
                    p2 = t % 2
                    swt(dv, f"x{p2}", E_sy_x(t))
                    dv.tensor_reduce(tiny[:, TY_SX:TY_SX + 1], xt[p2][:, :],
                                     AX.X, ALU.add)
                    dv.wait_ge(s_a2, E_a2_sq(t))
                    dv.tensor_reduce(tiny[:, TY_SXX:TY_SXX + 1], xsq[:, :],
                                     AX.X, ALU.add)
                    dv.tensor_scalar_mul(tiny[:, MUc(t):MUc(t) + 1],
                                         tiny[:, TY_SX:TY_SX + 1], csl(C_INVD))
                    dv.tensor_scalar_mul(tiny[:, TY_EX2:TY_EX2 + 1],
                                         tiny[:, TY_SXX:TY_SXX + 1], csl(C_INVD))
                    dv.tensor_mul(tiny[:, TY_MSQ:TY_MSQ + 1],
                                  tiny[:, MUc(t):MUc(t) + 1],
                                  tiny[:, MUc(t):MUc(t) + 1])
                    bump(dv.tensor_sub(tiny[:, VARc(t):VARc(t) + 1],
                                       tiny[:, TY_EX2:TY_EX2 + 1],
                                       tiny[:, TY_MSQ:TY_MSQ + 1]), "dv")

                def stageB(t):
                    p2 = t % 2
                    dv.wait_ge(s_a2, E_a2_rstd(t))
                    dv.reciprocal(tiny[:, RSTDc(t):RSTDc(t) + 1],
                                  tiny[:, RSTDc(t):RSTDc(t) + 1])
                    if debug_taps and t >= 1:
                        swt(dv, "tap", tap_b1 + 16 * t)
                    bump(dv.tensor_scalar(xn_f[:, :], xt[p2][:, :],
                                          tiny[:, MUc(t):MUc(t) + 1],
                                          tiny[:, RSTDc(t):RSTDc(t) + 1],
                                          ALU.subtract, ALU.mult), "dv")
                    dv.tensor_mul(tiny[:, TY_T1:TY_T1 + 1],
                                  tiny[:, VARc(t):VARc(t) + 1],
                                  tiny[:, RSTDc(t):RSTDc(t) + 1])
                    bump(dv.tensor_mul(tiny[:, TY_Y2:TY_Y2 + 1],
                                       tiny[:, TY_T1:TY_T1 + 1],
                                       tiny[:, RSTDc(t):RSTDc(t) + 1]), "dv")
                    dv.tensor_scalar_add(stat_a[:, t:t + 1],
                                         tiny[:, RSTDc(t):RSTDc(t) + 1], csl(C_1))
                    dv.tensor_scalar(stat_b[:, t:t + 1], tiny[:, MUc(t):MUc(t) + 1],
                                     tiny[:, RSTDc(t):RSTDc(t) + 1], csl(C_N1),
                                     ALU.mult, ALU.mult)
                    dv.wait_ge(s_a2, E_a2_rstd2(t))
                    dv.reciprocal(tiny[:, TY_RSTD2:TY_RSTD2 + 1],
                                  tiny[:, TY_RSTD2:TY_RSTD2 + 1])
                    dv.scalar_tensor_tensor(xsq[:, :], xn_f[:, :],
                                            tiny[:, TY_RSTD2:TY_RSTD2 + 1], gbc[:, :],
                                            ALU.mult, ALU.mult)
                    bump(dv.tensor_add(xn2sb[:, t * 1024:(t + 1) * 1024],
                                       xsq[:, :], bbc[:, :]), "dv")

                stageA(0)
                for tc in range(M):
                    if tc + 1 < M:
                        stageA(tc + 1)
                    else:
                        bump(dv.tensor_copy(tiny[:, TY_MR:TY_MR + 1], csl(C_1)), "dv")
                    stageB(tc)
                assert ctr["dv"] == dv0 + 4 * M + 1

            @blk.tensor
            def _(te):
                swt(te, "cst", b0b_cst)
                for tc in range(M):
                    p = tc % 2
                    te.wait_ge(s_a2, E_a2_xnb(tc))
                    if tc >= 1:
                        te.wait_ge(s_a2, E_a2_evT(tc - 1))
                    for dblk in range(8):
                        inst = te.transpose(ps_d[:, dblk * 128:(dblk + 1) * 128],
                                            xn_b[p][:, dblk * 128:(dblk + 1) * 128],
                                            cID[:, :])
                    bump(inst, "te")
                    te.wait_ge(s_dv, E_dv_xn2(tc))
                    te.matmul(ps_c[:, 0:512], cQW[:, tc * 128:(tc + 1) * 128],
                              xn2sb[:, tc * 1024:tc * 1024 + 512],
                              start=(tc == 0), stop=(tc == M - 1))
                    bump(te.matmul(ps_c[:, 512:1024], cQW[:, tc * 128:(tc + 1) * 128],
                                   xn2sb[:, tc * 1024 + 512:(tc + 1) * 1024],
                                   start=(tc == 0), stop=(tc == M - 1)), "te")
                assert ctr["te"] == te0 + 2 * M

        if stop_after == "B1":
            _early_out([stA, stX])
            return nc

        with nc.Block() as blk:
            @blk.vector
            def _(dv):
                dv = _DR(dv)
                dv.tensor_copy(q64[:, :], ps_c[:, :])
                dv.memset(zrep[:, :], 0.0)
                dv.memset(mstk1[0][:, :], 0.0)
                bump(dv.memset(mstk1[1][:, :], 0.0), "dv")
            b2_dv1 = ctr["dv"]

            @blk.tensor
            def _(te):
                te.wait_ge(s_dv, b2_dv1)
                for par, tgt in ((0, ps_e), (1, ps_c)):
                    te.matmul(tgt[:, 0:512], cTC[:, par * 128:(par + 1) * 128],
                              q64[:, 0:512], start=True, stop=True)
                    inst = te.matmul(tgt[:, 512:1024], cTC[:, par * 128:(par + 1) * 128],
                                     q64[:, 512:1024], start=True, stop=True)
                bump(inst, "te")
            b2_te1 = ctr["te"]

            dv_b2 = ctr["dv"]
            E_dv_v = lambda kap: dv_b2 + kap + 1
            E_gp_mk = lambda kap: 64 * (kap + 1)

            @blk.gpsimd
            def _(gp):
                swt(gp, "ld", b0b_ld)
                gp.wait_ge(s_dv, b2_dv1)
                for kap in range(NK):
                    if kap >= 2:
                        gp.wait_ge(s_dv, E_dv_v(kap - 2))
                    for e2 in range(DA + 1):
                        sdma(gp, "mk", mstk1[kap % 2][e2 * 32:e2 * 32 + 16, :],
                             bass.AP(scr_mrow, (kap + e2) * D, [[0, 16], [1, D]]))

            @blk.vector
            def _(dv):
                dv = _DR(dv)
                dv.wait_ge(s_te, b2_te1)
                for kap in range(NK):
                    src_ps = ps_e if kap % 2 == 0 else ps_c
                    base = (kap // 2) * 32
                    swt(dv, "mk", E_gp_mk(kap))
                    dv.tensor_copy(zrep[0:16, :], src_ps[base:base + 16, :])
                    for e2 in range(1, 4):
                        dv.tensor_copy(zrep[e2 * 32:e2 * 32 + 16, :], zrep[0:16, :])
                    if kap == 0:
                        bump(dv.tensor_mul(Vt[:, :], zrep[:, :], mstk1[0][:, :]), "dv")
                    else:
                        dv.tensor_mul(xn_f[:, :], zrep[:, :], mstk1[kap % 2][:, :])
                        bump(dv.tensor_add(Vt[:, :], Vt[:, :], xn_f[:, :]), "dv")
                assert ctr["dv"] == E_dv_v(NK - 1)
            b2_dv2 = ctr["dv"]

            if debug_taps:
                @blk.gpsimd
                def _(gp):
                    gp.wait_ge(s_dv, b2_dv2)
                    sdma(gp, "tapg", taps["tap_q"].ap(), q64[:, :])
                    sdma(gp, "tapg", taps["tap_V"].ap(), Vt[:, :])

        if stop_after == "B2":
            _early_out([stA, stX])
            return nc

        stA.close()
        enB = stB.enter_context
        u_b = [enB(nc.sbuf_tensor(f"ub_{k}", [128, D], BF16))
               for k in range(DA + 2)]
        y_b = [enB(nc.sbuf_tensor(f"y_b{p}", [128, D], BF16)) for p in range(2)]
        cP = enB(nc.sbuf_tensor("sb_cP", [128, (DA + 1) * 128], BF16))
        cWE = enB(nc.sbuf_tensor("sb_cWE", [128, M * 128], BF16))
        mtb = enB(nc.sbuf_tensor("mtb", [128, (DA + 1) * D], BF16))
        dbc = enB(nc.sbuf_tensor("dbc", [128, D], BF16))
        wabs = [enB(nc.sbuf_tensor(f"wabs{i}", [128, 512], F32)) for i in range(2)]
        wacc = enB(nc.sbuf_tensor("wacc", [1, 512], F32))
        scal1 = enB(nc.sbuf_tensor("scal1", [1, 8], F32))

        dv0 = ctr["dv"]; te0 = ctr["te"]
        E_dv_uh = lambda c, h: dv0 + 4 * c + 1 + h
        E_dv_yev = lambda k: (dv0 + 4 * (k + 1) + 3) if k < M - 1 else (dv0 + 4 * M + 1)

        def E_dv_ytr(k):
            if k <= M - 3:
                return dv0 + 4 * (k + 2) + 4
            return dv0 + 4 * M + 2 if k == M - 2 else dv0 + 4 * M + 3
        E_te_ymm = lambda c, h: te0 + 3 * c + 1 + h
        E_te_ytr = lambda k: (te0 + 3 * (k + 1) + 3) if k < M - 1 else (te0 + 3 * M + 1)
        tapg_b3 = streams["tapg"]["n"]

        with nc.Block() as blk:
            @blk.gpsimd
            def _(gp):
                swt(gp, "ld", b0b_ld)
                if debug_taps:
                    swt(gp, "tap", streams["tap"]["n"])
                sdma(gp, "b3l", cP[:, :], cP_in.ap())
                sdma(gp, "b3l", cWE[:, :], cWE_in.ap())
                sdma(gp, "b3l", dbc[:, :], bc_ap(Dp))
                for k in range(DA + 1):
                    sdma(gp, "b3l", mtb[:, k * D:(k + 1) * D],
                         bass.AP(scr_mrow, k * D, [[0, 128], [1, D]]))
                for wk in range(64):
                    if wk >= 2:
                        gp.wait_ge(s_wa, 2 * wk - 2)
                    if debug_taps:
                        sdma(gp, f"wp{wk % 2}", wchunk[wk % 2][:, :],
                             wp1_ap(*wp1[wk]))
                    else:
                        sdma(gp, f"wp{wk % 2}", wchunk[wk % 2][:, 512:1024],
                             wp1_ap(*wp1[wk], half=1))
            b3_l = streams["b3l"]["n"]

            if not debug_taps:
                @blk.sync
                def _(sy):
                    for wk in range(64):
                        if wk >= 2:
                            sy.wait_ge(s_wa, 2 * wk - 2)
                        sdma(sy, f"wq{wk % 2}", wchunk[wk % 2][:, 0:512],
                             wp1_ap(*wp1[wk], half=0))

            @blk.scalar
            def _(ac):
                ac = _DR(ac)
                for j in range(128):
                    wk, h = divmod(j, 2)
                    if h == 1 or debug_taps:
                        swt(ac, f"wp{wk % 2}", 16 * (wk // 2 + 1))
                    else:
                        swt(ac, f"wq{wk % 2}", 16 * (wk // 2 + 1))
                    if j >= 2:
                        ac.wait_ge(s_wm, j - 1)
                    ac.activation(wabs[h][:, :],
                                  wchunk[wk % 2][:, h * 512:(h + 1) * 512],
                                  AF.Abs).then_inc(s_wa, 1)

            @blk.vector
            def _(dv):
                dv = _DR(dv)
                swt(dv, "b3l", b3_l)
                for c in range(M):
                    p = c % 2
                    x2sl3 = xn2sb.ap().rearrange("p (c2 h f) -> p c2 h f",
                                                 c2=M, h=2, f=512)
                    for h in range(2):
                        if c >= 1:
                            dv.wait_ge(s_te, E_te_ymm(c - 1, h))
                        hs = slice(h * 512, (h + 1) * 512)
                        for k in range(DA + 1):
                            dv.tensor_mul(u_b[k][:, hs],
                                          xn2sb[:, c * 1024 + h * 512:
                                                c * 1024 + (h + 1) * 512],
                                          mtb[:, k * D + h * 512:k * D + (h + 1) * 512])
                        bump(dv.tensor_mul(u_b[DA + 1][:, hs],
                                           xn2sb[:, c * 1024 + h * 512:
                                                 c * 1024 + (h + 1) * 512],
                                           dbc[:, hs]), "dv")
                    if c >= 1:
                        kk = c - 1
                        dv.wait_ge(s_te, E_te_ymm(kk, 1))
                        if kk >= 1:
                            dv.wait_ge(s_te, E_te_ytr(kk - 1))
                        if debug_taps and kk >= 2:
                            swt(dv, f"tg{kk % 2}", 16 * (kk // 2))
                        psy = ps_c if kk % 2 == 0 else ps_e
                        bump(dv.tensor_copy(y_b[kk % 2][:, :], psy[:, :]), "dv")
                    else:
                        bump(dv.tensor_copy(tiny[:, TY_MR:TY_MR + 1], csl(C_1)), "dv")
                    if c >= 2:
                        kk = c - 2
                        dv.wait_ge(s_te, E_te_ytr(kk))
                        bump(dv.tensor_copy(yT3[:, :, kk * 128:(kk + 1) * 128], ps_d3),
                             "dv")
                    else:
                        bump(dv.tensor_copy(tiny[:, TY_MR:TY_MR + 1], csl(C_1)), "dv")
                    if c % 2 == 1:
                        g = (c - 1) // 2
                        which, gl = (0, g) if g < 4 else (1, g - 4)
                        srcp = ps_a if which == 0 else ps_b
                        dv.wait_ge(s_wm, 8 * c + 8)
                        if gl == 0:
                            dv.tensor_copy(wacc[0:1, :],
                                           srcp[0:1, 0:512]).then_inc(s_wv, 1)
                        else:
                            dv.tensor_add(wacc[0:1, :], wacc[0:1, :],
                                          srcp[0:1, 0:512]).then_inc(s_wv, 1)
                        if gl == 3:
                            n = 256
                            while n >= 1:
                                dv.tensor_add(wacc[0:1, 0:n], wacc[0:1, 0:n],
                                              wacc[0:1, n:2 * n])
                                n //= 2
                            dv.tensor_copy(scal1[0:1, which:which + 1],
                                           wacc[0:1, 0:1])
                dv.wait_ge(s_te, E_te_ymm(M - 1, 1))
                dv.wait_ge(s_te, E_te_ytr(M - 2))
                if debug_taps:
                    swt(dv, f"tg{(M - 1) % 2}", 16 * ((M - 1) // 2))
                psy = ps_c if (M - 1) % 2 == 0 else ps_e
                bump(dv.tensor_copy(y_b[(M - 1) % 2][:, :], psy[:, :]), "dv")
                bump(dv.tensor_copy(yT3[:, :, (M - 2) * 128:(M - 1) * 128], ps_d3), "dv")
                dv.wait_ge(s_te, E_te_ytr(M - 1))
                bump(dv.tensor_copy(yT3[:, :, (M - 1) * 128:M * 128], ps_d3), "dv")
                assert ctr["dv"] == dv0 + 4 * M + 3

            @blk.tensor
            def _(te):
                def wmm(j):
                    te.wait_ge(s_wa, j + 1)
                    which = 0 if j < 64 else 1
                    tgt = ps_a if which == 0 else ps_b
                    gl = (j % 64) // 16
                    if j % 16 == 0 and gl >= 1:
                        te.wait_ge(s_wv, which * 4 + gl)
                    te.matmul(tgt[0:1, 0:512], onesc[:, 0:1], wabs[j % 2][:, :],
                              start=(j % 16 == 0),
                              stop=(j % 16 == 15)).then_inc(s_wm, 1)

                for c in range(M):
                    p = c % 2
                    psy = ps_c if c % 2 == 0 else ps_e
                    for half in range(2):
                        te.wait_ge(s_dv, E_dv_uh(c, half))
                        if half == 0 and c >= 2:
                            te.wait_ge(s_dv, E_dv_yev(c - 2))
                        sl = slice(half * 512, (half + 1) * 512)
                        for k in range(DA + 1):
                            te.matmul(psy[:, sl], cP[:, k * 128:(k + 1) * 128],
                                      u_b[k][:, sl], start=(k == 0), stop=False)
                        te.matmul(psy[:, sl], cID[:, :], u_b[DA + 1][:, sl],
                                  start=False, stop=False)
                        bump(te.matmul(psy[:, sl], cWE[:, c * 128:(c + 1) * 128],
                                       Vt[:, sl], start=False, stop=True), "te")
                        for j in range(8 * c + 4 * half, 8 * c + 4 * half + 4):
                            wmm(j)
                    if c >= 1:
                        kk = c - 1
                        te.wait_ge(s_dv, E_dv_yev(kk))
                        if c >= 2:
                            te.wait_ge(s_dv, E_dv_ytr(kk - 1))
                        for dblk in range(8):
                            inst = te.transpose(ps_d[:, dblk * 128:(dblk + 1) * 128],
                                                y_b[kk % 2][:, dblk * 128:(dblk + 1) * 128],
                                                cID[:, :])
                        bump(inst, "te")
                    else:
                        bump(te.transpose(ps_d[:, 0:128], cID[:, :], cID[:, :]), "te")
                te.wait_ge(s_dv, E_dv_yev(M - 1))
                te.wait_ge(s_dv, E_dv_ytr(M - 2))
                for dblk in range(8):
                    inst = te.transpose(ps_d[:, dblk * 128:(dblk + 1) * 128],
                                        y_b[(M - 1) % 2][:, dblk * 128:(dblk + 1) * 128],
                                        cID[:, :])
                bump(inst, "te")
                assert ctr["te"] == te0 + 3 * M + 1

            if debug_taps:
                @blk.sync
                def _(sy):
                    for kk in range(M):
                        sy.wait_ge(s_dv, E_dv_yev(kk))
                        sdma(sy, f"tg{kk % 2}", taps["tap_y"].ap()[kk * 128:(kk + 1) * 128, :],
                             y_b[kk % 2][:, :])

        if stop_after == "B3":
            _early_out([stB, stX])
            return nc

        with nc.Block() as blk:
            @blk.vector
            def _(dv):
                dv = _DR(dv)
                dv.tensor_scalar(scal1[0:1, 2:4], scal1[0:1, 0:2],
                                 cst[0:1, C_INVW:C_INVW + 1], cst[0:1, C_W8:C_W8 + 1],
                                 ALU.mult, ALU.max)
                bump(dv.reciprocal(invs[0:1, 0:2], scal1[0:1, 2:4]), "dv")
            b3b_dv = ctr["dv"]

            @blk.sync
            def _(sy):
                sy.wait_ge(s_dv, b3b_dv)
                n = 1
                while n < 128:
                    m2 = min(n, 128 - n)
                    sdma(sy, "iv", invs[n:n + m2, :], invs[0:m2, :])
                    swt(sy, "iv")
                    n += m2
            b3b_iv = streams["iv"]["n"]

            if debug_taps:
                @blk.gpsimd
                def _(gp):
                    swt(gp, "iv", b3b_iv)
                    sdma(gp, "tapg", taps["tap_invs"].ap(), invs[:, :])

        stB.close()
        enC = stC.enter_context
        w1st = [enC(nc.sbuf_tensor(f"w1st{i}", [128, 1024], BF16)) for i in range(3)]
        hst = [enC(nc.sbuf_tensor(f"hst{i}", [128, T], BF16)) for i in range(2)]
        wsgA = enC(nc.sbuf_tensor("wsgA", [128, 1024], BF16))
        wsgB = enC(nc.sbuf_tensor("wsgB", [128, 1024], BF16))
        w1tn = enC(nc.sbuf_tensor("w1tn", [128, 1024], BF16))
        w2tn = [enC(nc.sbuf_tensor(f"w2tn{i}", [128, 1024], BF16)) for i in range(2)]
        wchC = [enC(nc.sbuf_tensor(f"wchC{i}", [128, 1024], F32)) for i in range(2)]

        a20 = ctr["a2"]; dv0 = ctr["dv"]; te0 = ctr["te"]
        nA, nD, nT = a20, dv0, te0
        A_s1 = {}; A_g = {}; A_s2 = {}
        D_a1 = {}; D_e1 = {}; D_a2 = {}; D_e2 = {}
        T_t1 = {}; T_h = {}; T_t2 = {}
        nA += 2; A_s1[0] = nA
        nA += 2; A_s1[1] = nA
        nD += 1; D_a1[0] = nD
        nD += 1; D_e1[0] = nD
        nD += 1; D_a1[1] = nD
        nD += 1; D_e1[1] = nD
        nT += 1; T_t1[0] = nT
        nT += 1; T_t1[1] = nT
        for fc in range(32):
            if fc <= 29:
                nA += 2; A_s1[fc + 2] = nA
            for t4 in range(4):
                nA += 1; A_g[(fc, t4)] = nA
            nA += 2; A_s2[fc] = nA
            if fc <= 29:
                nD += 1; D_a1[fc + 2] = nD
                nD += 1; D_e1[fc + 2] = nD
            nD += 1; D_a2[fc] = nD
            if fc >= 1:
                nD += 1; D_e2[fc - 1] = nD
            for t4 in range(4):
                nT += 1; T_h[(fc, t4)] = nT
            if fc <= 29:
                nT += 1; T_t1[fc + 2] = nT
            if fc >= 1:
                nT += 1; T_t2[fc - 1] = nT
        nT += 1; T_t2[31] = nT
        nD += 1; D_e2[31] = nD
        E_ac_hw = lambda fc: 16 * (fc // 2 + 1)

        with nc.Block() as blk:
            @blk.gpsimd
            def _(gp):
                sdma(gp, "w1l0", wchunk[0][:, :], wp1_ap("W1", 0))
                sdma(gp, "w2l0", wchC[0][:, :], wp1_ap("W2", 0))
                sdma(gp, "w1l0", wchunk[1][:, :], wp1_ap("W1", 1))
                for fc in range(32):
                    if fc <= 29:
                        k = fc + 2
                        gp.wait_ge(s_a2, A_s1[fc])
                        sdma(gp, "w1l0", wchunk[k % 2][:, :], wp1_ap("W1", k))
                    if fc <= 30:
                        k = fc + 1
                        if fc >= 1:
                            gp.wait_ge(s_a2, A_s2[fc - 1])
                        sdma(gp, "w2l0", wchC[k % 2][:, :], wp1_ap("W2", k))

            @blk.scalar
            def _(ac):
                ac = _DR(ac)
                swt(ac, "iv", b3b_iv)
                for k in range(2):
                    swt(ac, "w1l0", 16 * (k + 1))
                    if k == 1:
                        ac.wait_ge(s_dv, D_a1[0])
                    bump(ac.activation(wsgA[:, :], wchunk[k][:, :], AF.Sign,
                                       bias=csl(C_NHALF), scale=invs[:, 0:1]), "a2")
                    bump(ac.activation(wsgB[:, :], wchunk[k][:, :], AF.Sign,
                                       bias=csl(C_PHALF), scale=invs[:, 0:1]), "a2")
                for fc in range(32):
                    p = fc % 2
                    if fc <= 29:
                        k = fc + 2
                        swt(ac, "w1l0", 16 * (k + 1))
                        ac.wait_ge(s_dv, D_a2[fc - 1] if fc >= 1 else D_a1[1])
                        bump(ac.activation(wsgA[:, :], wchunk[k % 2][:, :], AF.Sign,
                                           bias=csl(C_NHALF), scale=invs[:, 0:1]),
                             "a2")
                        bump(ac.activation(wsgB[:, :], wchunk[k % 2][:, :], AF.Sign,
                                           bias=csl(C_PHALF), scale=invs[:, 0:1]),
                             "a2")
                    for t4 in range(4):
                        psh = (ps_a[:, :], ps_b[:, :], ps_c[:, 0:512],
                               ps_e[:, 0:512])[t4]
                        ac.wait_ge(s_te, T_h[(fc, t4)])
                        if fc >= 2 and t4 == 0:
                            swt(ac, f"hw{fc % 2}", E_ac_hw(fc - 2))
                        bump(ac.activation(hst[p][:, t4 * 512:(t4 + 1) * 512],
                                           psh, AF.Gelu,
                                           bias=b1sb[:, fc:fc + 1],
                                           scale=csl(C_HALF)), "a2")
                    swt(ac, "w2l0", 16 * (fc + 1))
                    ac.wait_ge(s_dv, D_a1[fc + 2] if fc <= 29 else D_a1[31])
                    bump(ac.activation(wsgA[:, :], wchC[fc % 2][:, :], AF.Sign,
                                       bias=csl(C_NHALF), scale=invs[:, 1:2]), "a2")
                    bump(ac.activation(wsgB[:, :], wchC[fc % 2][:, :], AF.Sign,
                                       bias=csl(C_PHALF), scale=invs[:, 1:2]), "a2")
                    sdma(ac, f"hw{fc % 2}", scr_h.ap()[fc, :, :], hst[p][:, :])
                assert ctr["a2"] == nA

            @blk.vector
            def _(dv):
                dv = _DR(dv)
                for k in range(2):
                    dv.wait_ge(s_a2, A_s1[k])
                    if k == 1:
                        dv.wait_ge(s_te, T_t1[0])
                    bump(dv.tensor_add(w1tn[:, :], wsgA[:, :], wsgB[:, :]), "dv")
                    dv.wait_ge(s_te, T_t1[k])
                    bump(dv.tensor_copy(
                        w1st[k].ap().rearrange("p (b f) -> p b f", f=128), ps_d3),
                        "dv")
                for fc in range(32):
                    if fc <= 29:
                        k = fc + 2
                        dv.wait_ge(s_a2, A_s1[k])
                        dv.wait_ge(s_te, T_t1[k - 1])
                        bump(dv.tensor_add(w1tn[:, :], wsgA[:, :], wsgB[:, :]), "dv")
                        dv.wait_ge(s_te, T_t1[k])
                        if fc >= 1:
                            dv.wait_ge(s_te, T_h[(fc - 1, 3)])
                        bump(dv.tensor_copy(
                            w1st[k % 3].ap().rearrange("p (b f) -> p b f", f=128),
                            ps_d3), "dv")
                    dv.wait_ge(s_a2, A_s2[fc])
                    if fc >= 2:
                        dv.wait_ge(s_te, T_t2[fc - 2])
                    bump(dv.tensor_add(w2tn[fc % 2][:, :], wsgA[:, :], wsgB[:, :]),
                         "dv")
                    if fc >= 1:
                        u = fc - 1
                        dblk, fq = divmod(u, 4)
                        dv.wait_ge(s_te, T_t2[u])
                        bump(dv.tensor_copy(
                            W2T3[:, fq * 8:(fq + 1) * 8, dblk * 128:(dblk + 1) * 128],
                            ps_d3), "dv")
                u = 31
                dblk, fq = divmod(u, 4)
                dv.wait_ge(s_te, T_t2[u])
                bump(dv.tensor_copy(
                    W2T3[:, fq * 8:(fq + 1) * 8, dblk * 128:(dblk + 1) * 128],
                    ps_d3), "dv")
                assert ctr["dv"] == nD

            @blk.tensor
            def _(te):
                for k in range(2):
                    te.wait_ge(s_dv, D_a1[k])
                    if k == 1:
                        te.wait_ge(s_dv, D_e1[0])
                    for dblk in range(8):
                        inst = te.transpose(ps_d[:, dblk * 128:(dblk + 1) * 128],
                                            w1tn[:, dblk * 128:(dblk + 1) * 128],
                                            cID[:, :])
                    bump(inst, "te")
                for fc in range(32):
                    te.wait_ge(s_dv, D_e1[fc])
                    for t4 in range(4):
                        psh = (ps_a[:, :], ps_b[:, :], ps_c[:, 0:512],
                               ps_e[:, 0:512])[t4]
                        gi = fc * 4 + t4
                        if gi >= 4:
                            te.wait_ge(s_a2, A_g[((gi - 4) // 4, (gi - 4) % 4)])
                        for dblk in range(8):
                            inst = te.matmul(psh,
                                             w1st[fc % 3][:, dblk * 128:(dblk + 1) * 128],
                                             xnT[:, dblk * T + t4 * 512:
                                                 dblk * T + (t4 + 1) * 512],
                                             start=(dblk == 0), stop=(dblk == 7))
                        bump(inst, "te")
                    if fc <= 29:
                        k = fc + 2
                        te.wait_ge(s_dv, D_a1[k])
                        te.wait_ge(s_dv, D_e1[k - 1] if k <= 3 else D_e2[k - 4])
                        for dblk in range(8):
                            inst = te.transpose(ps_d[:, dblk * 128:(dblk + 1) * 128],
                                                w1tn[:, dblk * 128:(dblk + 1) * 128],
                                                cID[:, :])
                        bump(inst, "te")
                    if fc >= 1:
                        u = fc - 1
                        te.wait_ge(s_dv, D_a2[u])
                        te.wait_ge(s_dv, D_e1[u + 3] if u + 3 <= 31 else D_e2[u - 1])
                        for dblk in range(8):
                            inst = te.transpose(ps_d[:, dblk * 128:(dblk + 1) * 128],
                                                w2tn[u % 2][:, dblk * 128:(dblk + 1) * 128],
                                                cID[:, :])
                        bump(inst, "te")
                u = 31
                te.wait_ge(s_dv, D_a2[u])
                te.wait_ge(s_dv, D_e2[30])
                for dblk in range(8):
                    inst = te.transpose(ps_d[:, dblk * 128:(dblk + 1) * 128],
                                        w2tn[u % 2][:, dblk * 128:(dblk + 1) * 128],
                                        cID[:, :])
                bump(inst, "te")
                assert ctr["te"] == nT

        if stop_after == "B5":
            _early_out([stC, stX])
            return nc

        stC.close()
        stX.close()
        enD = stD.enter_context
        WoT = enD(nc.sbuf_tensor("WoT", [128, 8 * 1024], BF16))
        WoT3 = WoT.ap().rearrange("p (b c) -> p b c", c=1024)
        hgrp = [enD(nc.sbuf_tensor(f"hgrp{i}", [128, 32 * 256], BF16)) for i in range(2)]
        xr = [enD(nc.sbuf_tensor(f"xr{i}", [128, D], F32)) for i in range(2)]
        osb = [enD(nc.sbuf_tensor(f"osb{i}", [128, D], F32)) for i in range(2)]
        t1f = enD(nc.sbuf_tensor("t1f", [128, D], F32))
        t2f = enD(nc.sbuf_tensor("t2f", [128, D], F32))
        bias1bc = enD(nc.sbuf_tensor("bias1bc", [128, D], F32))

        a20 = ctr["a2"]; te0 = ctr["te"]; dv0 = ctr["dv"]
        E_a2_wo = lambda k: a20 + k + 1
        E_te_wo = lambda k: te0 + k + 1
        E_dv_wo = lambda k: dv0 + k + 1
        b6w_base = [streams["w1c0"]["n"], streams["w1c1"]["n"]]
        with nc.Block() as blk:
            @blk.gpsimd
            def _(gp):
                swt(gp, "ld", b0_ld2)
                sdma(gp, "cst", bias1bc[:, :], bass.AP(scr_rows, 0, [[0, 128], [1, D]]))
            b6_cst = streams["cst"]["n"]

            @blk.sync
            def _(sy):
                for eb in range(8):
                    if eb >= 2:
                        sy.wait_ge(s_a2, E_a2_wo(eb - 2))
                    sdma(sy, f"w1c{eb % 2}", wchunk[eb % 2][:, :],
                         Wout.ap()[eb * 128:(eb + 1) * 128, :])

            @blk.scalar
            def _(ac):
                ac = _DR(ac)
                for eb in range(8):
                    swt(ac, f"w1c{eb % 2}", b6w_base[eb % 2] + 16 * (eb // 2 + 1))
                    if eb >= 2:
                        ac.wait_ge(s_te, E_te_wo(eb - 2))
                    bump(ac.activation(hgrp[eb % 2][:, 0:1024], wchunk[eb % 2][:, :],
                                       AF.Copy, bias=0.0, scale=2.0), "a2")

            @blk.tensor
            def _(te):
                for eb in range(8):
                    te.wait_ge(s_a2, E_a2_wo(eb))
                    if eb >= 1:
                        te.wait_ge(s_dv, E_dv_wo(eb - 1))
                    for dblk in range(8):
                        inst = te.transpose(ps_d[:, dblk * 128:(dblk + 1) * 128],
                                            hgrp[eb % 2][:, dblk * 128:(dblk + 1) * 128],
                                            cID[:, :])
                    bump(inst, "te")

            @blk.vector
            def _(dv):
                dv = _DR(dv)
                for eb in range(8):
                    dv.wait_ge(s_te, E_te_wo(eb))
                    bump(dv.tensor_copy(WoT3[:, :, eb * 128:(eb + 1) * 128], ps_d3), "dv")

        te0 = ctr["te"]; dv0 = ctr["dv"]
        E_te_o = lambda tg, tk: te0 + 2 * tg + tk + 1
        E_dv_oe = lambda tg, tk: dv0 + 2 * tg + tk + 1
        E_gp_ow = lambda tg: 16 * (tg + 1)
        E_ac_xr = lambda tg: 16 * (tg + 1)
        E_hg = lambda tg: 16 * 16 * (tg // 2 + 1)
        b5_hw = [streams["hw0"]["n"], streams["hw1"]["n"]]
        with nc.Block() as blk:
            @blk.sync
            def _(sy):
                swt(sy, "hw0", b5_hw[0])
                swt(sy, "hw1", b5_hw[1])
                for tg in range(8):
                    if tg >= 2:
                        sy.wait_ge(s_te, E_te_o(tg - 2, 1))
                    for fc in range(0, 32, 2):
                        sdma(sy, f"hge{tg % 2}", hgrp[tg % 2][:, fc * 256:(fc + 1) * 256],
                             scr_h.ap()[fc, :, tg * 256:(tg + 1) * 256])

            @blk.scalar
            def _(ac):
                ac = _DR(ac)
                swt(ac, "hw0", b5_hw[0])
                swt(ac, "hw1", b5_hw[1])
                for tg in range(8):
                    if tg >= 2:
                        ac.wait_ge(s_te, E_te_o(tg - 2, 1))
                    for fc in range(1, 32, 2):
                        sdma(ac, f"hgo{tg % 2}", hgrp[tg % 2][:, fc * 256:(fc + 1) * 256],
                             scr_h.ap()[fc, :, tg * 256:(tg + 1) * 256])
                    for tk in range(2):
                        tci = tg * 2 + tk
                        if tg >= 1:
                            ac.wait_ge(s_dv, E_dv_oe(tg - 1, tk))
                        sdma(ac, f"xr{tk}", xr[tk][:, :],
                             x_in.ap()[tci * 128:(tci + 1) * 128, :])

            @blk.tensor
            def _(te):
                for tg in range(8):
                    for tk in range(2):
                        pso = ps_c if tk == 0 else ps_e
                        swt(te, f"hge{tg % 2}", E_hg(tg))
                        swt(te, f"hgo{tg % 2}", E_hg(tg))
                        if tg >= 1:
                            te.wait_ge(s_dv, E_dv_oe(tg - 1, tk))
                        for dh in range(2):
                            sl = slice(dh * 512, (dh + 1) * 512)
                            for fc in range(32):
                                te.matmul(pso[:, sl],
                                          hgrp[tg % 2][:, fc * 256 + tk * 128:
                                                       fc * 256 + (tk + 1) * 128],
                                          W2T[:, fc * 1024 + dh * 512:
                                              fc * 1024 + (dh + 1) * 512],
                                          start=(fc == 0), stop=False)
                            for dblk in range(8):
                                inst = te.matmul(
                                    pso[:, sl],
                                    yT[:, dblk * T + (tg * 2 + tk) * 128:
                                       dblk * T + (tg * 2 + tk + 1) * 128],
                                    WoT[:, dblk * 1024 + dh * 512:
                                        dblk * 1024 + (dh + 1) * 512],
                                    start=False, stop=(dblk == 7))
                        bump(inst, "te")

            @blk.vector
            def _(dv):
                dv = _DR(dv)
                swt(dv, "cst", b6_cst)
                for tg in range(8):
                    for tk in range(2):
                        pso = ps_c if tk == 0 else ps_e
                        tmp = t1f if tk == 0 else t2f
                        tci = tg * 2 + tk
                        swt(dv, f"xr{tk}", E_ac_xr(tg))
                        dv.wait_ge(s_te, E_te_o(tg, tk))
                        if tg >= 1:
                            swt(dv, f"ow{tk}", E_gp_ow(tg - 1))
                        dv.tensor_scalar(tmp[:, :], xr[tk][:, :],
                                         stat_a[:, tci:tci + 1], stat_b[:, tci:tci + 1],
                                         ALU.mult, ALU.add)
                        dv.tensor_add(tmp[:, :], tmp[:, :], bias1bc[:, :])
                        dv.scalar_tensor_tensor(osb[tk][:, 0:512], pso[:, 0:512],
                                                csl(C_HALF), tmp[:, 0:512],
                                                ALU.mult, ALU.add)
                        dv._e.drain()
                        bump(dv.scalar_tensor_tensor(osb[tk][:, 512:1024],
                                                     pso[:, 512:1024], csl(C_HALF),
                                                     tmp[:, 512:1024], ALU.mult, ALU.add),
                             "dv")

            @blk.gpsimd
            def _(gp):
                for tg in range(8):
                    for tk in range(2):
                        tci = tg * 2 + tk
                        gp.wait_ge(s_dv, E_dv_oe(tg, tk))
                        sdma(gp, f"ow{tk}", out_t.ap()[tci * 128:(tci + 1) * 128, :],
                             osb[tk][:, :])
                swt(gp, "ow0", streams["ow0"]["n"])
                swt(gp, "ow1", streams["ow1"]["n"])

        with nc.Block() as blk:
            @blk.sync
            def _(sy):
                for nm in ["ld", "x0", "x1", "iv", "x20", "x21", "w1w0", "w1w1",
                           "w1c0", "w1c1", "hge0", "hge1", "wq0", "wq1"]:
                    swt(sy, nm, streams[nm]["n"])

            @blk.scalar
            def _(ac):
                for nm in ["xn0", "xn1", "hw0", "hw1", "hgo0", "hgo1", "xr0", "xr1",
                           "tap"]:
                    swt(ac, nm, streams[nm]["n"])

            @blk.gpsimd
            def _(gp):
                for nm in ["cst", "mk", "b3l", "wp0", "wp1", "w1l0", "w1l1",
                           "w2l0", "w2l1", "tapg", "tg0", "tg1"]:
                    swt(gp, nm, streams[nm]["n"])

        stD.close()

    return nc


def kernel(**inputs):
    consts = _host_constants()
    nc = build_nc(debug_taps=_DEBUG_TAPS, stop_after=_STOP_AFTER)
    x = np.ascontiguousarray(np.asarray(inputs["L0_attn_x"], dtype=np.float32))
    shared = {k: np.ascontiguousarray(np.asarray(v, dtype=np.float32))
              for k, v in inputs.items() if k != "L0_attn_x"}
    shared.update(consts)
    in_maps = []
    for i in range(8):
        mm = dict(shared)
        mm["x"] = np.ascontiguousarray(x[i])
        in_maps.append(mm)
    res = run_bass_kernel_spmd(nc, in_maps, core_ids=list(range(8)))
    return np.stack([np.asarray(r["out"], dtype=np.float32) for r in res.results])


# version 10
# speedup vs baseline: 1.2047x; 1.2047x over previous
import math
import numpy as np
from contextlib import ExitStack

import concourse.bass as bass
import concourse.mybir as mybir
from concourse.bass_utils import run_bass_kernel_spmd

F32 = mybir.dt.float32
BF16 = mybir.dt.bfloat16
AF = mybir.ActivationFunctionType
ALU = mybir.AluOpType
AX = mybir.AxisListType

T, D, NS, L, M, FF = 2048, 1024, 64, 128, 16, 4096
DA = 3
DBd = 4
NK = DA + DBd + 1
NMOM = DA + DBd + DA + 1
LAM0 = math.exp(-8.0)
TSn = 2048.0
EPS = 1e-5

_DEBUG_TAPS = False
_STOP_AFTER = None

_DRAIN_OPS = {
    "tensor_copy", "tensor_add", "tensor_sub", "tensor_mul", "tensor_scalar",
    "tensor_scalar_mul", "tensor_scalar_add", "tensor_scalar_max",
    "tensor_scalar_min", "tensor_reduce", "reduce_sum", "reduce_max",
    "scalar_tensor_tensor", "reciprocal", "memset", "activation", "select",
}


class _DR:

    def __init__(self, eng):
        self._e = eng

    def __getattr__(self, nm):
        f = getattr(self._e, nm)
        if nm in _DRAIN_OPS:
            def g(*a, **k):
                r = f(*a, **k)
                self._e.drain()
                return r
            return g
        return f


def _host_constants():
    c = {}
    ii = np.arange(L, dtype=np.float64)
    dif = ii[None, :] - ii[:, None]
    mask = dif >= 0
    cP = np.zeros((128, (DA + 1) * 128), dtype=np.float64)
    for k in range(DA + 1):
        cP[:, k * 128:(k + 1) * 128] = np.where(
            mask, np.exp(-LAM0 * np.maximum(dif, 0)) *
            (np.maximum(dif, 0) / TSn) ** k / math.factorial(k), 0.0)
    c["cP"] = cP.astype(np.float32)

    r1 = (L - 1 - ii)
    qw = np.stack([np.exp(-LAM0 * r1) * (r1 / TSn) ** a / math.factorial(a)
                   for a in range(DA + 1)], axis=1)
    cQW = np.zeros((128, M * 128), dtype=np.float64)
    for cc in range(M):
        for a in range(DA + 1):
            cQW[:, cc * 128 + a * 32 + cc] = qw[:, a]
    c["cQW"] = cQW.astype(np.float32)

    ccs = np.arange(M, dtype=np.float64)
    dcc = ccs[None, :] - ccs[:, None] - 1.0
    mc = dcc >= 0
    r2 = np.maximum(dcc, 0.0) * L
    cTC = np.zeros((128, 256), dtype=np.float64)
    for a in range(DA + 1):
        for b in range(DBd + 1):
            blk = np.where(mc, np.exp(-LAM0 * r2) * (r2 / TSn) ** b / math.factorial(b), 0.0)
            kap = a + b
            base = (kap % 2) * 128 + (kap // 2) * 32
            for cc in range(M):
                cTC[a * 32:a * 32 + 16, base + cc] = blk[:, cc]
    c["cTC"] = cTC.astype(np.float32)

    r3 = ii + 1.0
    we = np.stack([np.exp(-LAM0 * r3) * (r3 / TSn) ** e / math.factorial(e)
                   for e in range(DA + 1)])
    cWE = np.zeros((128, M * 128), dtype=np.float64)
    for cc in range(M):
        for e2 in range(DA + 1):
            cWE[e2 * 32 + cc, cc * 128:(cc + 1) * 128] = we[e2]
    c["cWE"] = cWE.astype(np.float32)

    c["cID"] = np.eye(128, dtype=np.float32)
    return c


def build_nc(debug_taps=False, stop_after=None):
    nc = bass.Bass("TRN2", target_bir_lowering=False)

    def din(name, shape, dtype=F32):
        return nc.dram_tensor(name, shape, dtype, kind="ExternalInput")

    x_in = din("x", [T, D])
    logA = din("L0_attn_log_A", [D, NS])
    Bp = din("L0_attn_B_param", [D, NS])
    Cp = din("L0_attn_C_param", [D, NS])
    Dp = din("L0_attn_D_param", [D])
    logdt = din("L0_attn_log_dt", [D])
    gam = din("L0_attn_ln_gamma", [D])
    bet = din("L0_attn_ln_beta", [D])
    Wout = din("L0_attn_W_out", [D, D])
    bout = din("L0_attn_b_out", [D])
    W1 = din("L0_ffn_W1", [FF, D])
    b1 = din("L0_ffn_b1", [FF])
    W2 = din("L0_ffn_W2", [D, FF])
    b2 = din("L0_ffn_b2", [D])
    cP_in = din("cP", [128, (DA + 1) * 128])
    cQW_in = din("cQW", [128, M * 128])
    cTC_in = din("cTC", [128, 256])
    cWE_in = din("cWE", [128, M * 128])
    cID_in = din("cID", [128, 128])

    out_t = nc.dram_tensor("out", [T, D], F32, kind="ExternalOutput")

    scr_mrow = nc.dram_tensor("scr_mrow", [NMOM, D], F32)
    scr_rows = nc.dram_tensor("scr_rows", [2, D], F32)
    scr_h = nc.dram_tensor("scr_h", [32, 128, T], BF16)

    taps = {}
    if debug_taps:
        for nm, shp in [("tap_xn", [T, D]), ("tap_xn2", [T, D]), ("tap_y", [T, D]),
                        ("tap_q", [128, D]), ("tap_V", [128, D]),
                        ("tap_mom", [128, NMOM * 8]), ("tap_invs", [128, 2])]:
            taps[nm] = nc.dram_tensor(nm, shp, BF16 if nm == "tap_y" else F32,
                                      kind="ExternalOutput")

    st = ExitStack()
    stX = ExitStack()
    stA = ExitStack()
    stB = ExitStack()
    stC = ExitStack()
    stD = ExitStack()
    en = st.enter_context
    with st:
        cst = en(nc.sbuf_tensor("cst", [128, 14], F32))
        (C_EPS, C_1, C_NHALF, C_PHALF, C_HALF, C_INVD, C_N1,
         C_LAM0, C_NT, C_2, C_INVW, C_W8, C_0, C_SP) = range(14)
        onesc = en(nc.sbuf_tensor("onesc", [128, 1], F32))
        for _v, _c in [(0.0, C_0), (1.0, C_1), (2.0, C_2), (0.5, C_HALF), (-0.5, C_NHALF)]:
            nc.const_aps.register(F32, _v, cst[:, _c:_c + 1])
        invs = en(nc.sbuf_tensor("invs", [128, 2], F32))
        b1sb = en(nc.sbuf_tensor("b1sb", [128, 32], F32))
        stat_a = en(nc.sbuf_tensor("stat_a", [128, M], F32))
        stat_b = en(nc.sbuf_tensor("stat_b", [128, M], F32))
        tiny = en(nc.sbuf_tensor("tiny", [128, 16], F32))
        (TY_SX, TY_SXX, TY_MU, TY_EX2, TY_MSQ, TY_VAR, TY_RSTD,
         TY_T1, TY_Y2, TY_RSTD2, TY_MR, TY_MU2, TY_VAR2, TY_RSTDB) = range(14)
        cID = en(nc.sbuf_tensor("sb_cID", [128, 128], BF16))
        wchunk = [en(nc.sbuf_tensor(f"wch{i}", [128, 1024], F32)) for i in range(2)]
        W2T = en(nc.sbuf_tensor("W2T", [128, 32 * 1024], BF16))
        yT = en(nc.sbuf_tensor("yT", [128, 8 * T], BF16))

        xnT = stX.enter_context(nc.sbuf_tensor("xnT", [128, 8 * T], BF16))
        Vt = stX.enter_context(nc.sbuf_tensor("Vt", [128, D], BF16))
        xn2sb = stX.enter_context(nc.sbuf_tensor("xn2sb", [128, M * 1024], BF16))

        enA = stA.enter_context
        cQW = enA(nc.sbuf_tensor("sb_cQW", [128, M * 128], BF16))
        cTC = enA(nc.sbuf_tensor("sb_cTC", [128, 256], BF16))
        gbc = enA(nc.sbuf_tensor("gbc", [128, D], BF16))
        bbc = enA(nc.sbuf_tensor("bbc", [128, D], BF16))
        mom = enA(nc.sbuf_tensor("mom", [128, NMOM * 8], F32))
        dt8 = enA(nc.sbuf_tensor("dt8", [128, 8], F32))
        dte = enA(nc.sbuf_tensor("dte", [128, 8], F32))
        rowA = enA(nc.sbuf_tensor("rowA", [8, 128], F32))
        rowB = enA(nc.sbuf_tensor("rowB", [8, 128], F32))
        xt = [enA(nc.sbuf_tensor(f"xt{i}", [128, D], F32)) for i in range(2)]
        xn_f = enA(nc.sbuf_tensor("xn_f", [128, D], F32))
        xsq = enA(nc.sbuf_tensor("xsq", [128, D], F32))
        xn_b = [enA(nc.sbuf_tensor(f"xnb{i}", [128, D], BF16)) for i in range(2)]
        mstk1 = xn_b
        zrep = enA(nc.sbuf_tensor("zrep", [128, D], BF16))
        q64 = enA(nc.sbuf_tensor("q64", [128, D], BF16))

        cIDf = xn_f[:, 0:128]
        pA = xt[0][:, 0:512]
        pB = xt[0][:, 512:1024]
        pC = xt[1][:, 0:512]
        dtx = xt[1][:, 512:1024]
        b1row = xsq[0:32, 0:128]

        ps_a = en(nc.psum_tensor("ps_a", [128, 512], F32))
        ps_b = en(nc.psum_tensor("ps_b", [128, 512], F32))
        ps_c = en(nc.psum_tensor("ps_c", [128, 1024], F32))
        ps_d = en(nc.psum_tensor("ps_d", [128, 1024], BF16))
        ps_e = en(nc.psum_tensor("ps_e", [128, 1024], F32))

        s_dv = en(nc.semaphore("s_dv"))
        s_te = en(nc.semaphore("s_te"))
        s_a2 = en(nc.semaphore("s_a2"))
        s_wa = en(nc.semaphore("s_wa"))
        s_wm = en(nc.semaphore("s_wm"))
        s_wv = en(nc.semaphore("s_wv"))
        SEM = {"dv": s_dv, "te": s_te, "a2": s_a2}
        ctr = {k: 0 for k in SEM}

        def bump(inst, which, n=1):
            inst.then_inc(SEM[which], n)
            ctr[which] += n
            return ctr[which]

        streams = {}
        for _n in ["ld", "cst", "iv", "mk", "b3l",
                   "x0", "x1", "xn0", "xn1", "wp0", "wp1", "wq0", "wq1",
                   "x20", "x21", "w1l0", "w1l1", "w1w0", "w1w1",
                   "w1c0", "w1c1", "w2l0", "w2l1", "hw0", "hw1",
                   "hge0", "hge1", "hgo0", "hgo1", "xr0", "xr1",
                   "ow0", "ow1", "tap", "tapg", "tg0", "tg1"]:
            streams[_n] = {"sem": en(nc.semaphore("st_" + _n)), "n": 0}

        def sdma(q, name, out, in_):
            s = streams[name]
            q.dma_start(out, in_).then_inc(s["sem"], 16)
            s["n"] += 16
            return s["n"]

        def swt(eng, name, val=None):
            s = streams[name]
            eng.wait_ge(s["sem"], s["n"] if val is None else val)

        csl = lambda c: cst[:, c:c + 1]
        ps_d3 = ps_d.ap().rearrange("p (b f) -> p b f", f=128)
        xnT3 = xnT.ap().rearrange("p (b t) -> p b t", t=T)
        yT3 = yT.ap().rearrange("p (b t) -> p b t", t=T)
        W2T3 = W2T.ap().rearrange("p (f c) -> p f c", c=1024)

        def bc_ap(dt_, n=128, width=D, off=0):
            return bass.AP(dt_, off, [[0, n], [1, width]])

        def _early_out(open_stacks):
            with nc.Block() as blk2:
                @blk2.sync
                def _(sy):
                    sdma(sy, "ld", out_t.ap(), x_in.ap())
                    swt(sy, "ld")
                    for nm2, s2 in streams.items():
                        sy.wait_ge(s2["sem"], s2["n"]) if False else None

                @blk2.scalar
                def _(ac):
                    for nm2 in ["xn0", "xn1", "hw0", "hw1", "hgo0", "hgo1",
                                "xr0", "xr1", "tap"]:
                        swt(ac, nm2, streams[nm2]["n"])

                @blk2.sync
                def _(sy):
                    for nm2 in ["ld", "x0", "x1", "iv", "x20", "x21", "w1w0", "w1w1",
                                "w1c0", "w1c1", "hge0", "hge1", "wq0", "wq1"]:
                        swt(sy, nm2, streams[nm2]["n"])

                @blk2.gpsimd
                def _(gp):
                    for nm2 in ["cst", "mk", "b3l", "wp0", "wp1", "w1l0", "w1l1",
                                "w2l0", "w2l1", "tapg", "tg0", "tg1"]:
                        swt(gp, nm2, streams[nm2]["n"])
            for s2 in open_stacks:
                s2.close()


        with nc.Block() as blk:
            @blk.vector
            def _(dv):
                dv = _DR(dv)
                bump(dv.memset(xsq[:, :], 0.0), "dv")

            @blk.sync
            def _(sy):
                sdma(sy, "ld", cIDf, cID_in.ap())
                sy.wait_ge(s_dv, 1)
                sdma(sy, "ld", b1row, b1.ap().rearrange("(c p) -> c p", p=128))
                sdma(sy, "ld", dt8[:, :], bass.AP(logdt, 0, [[8, 128], [1, 8]]))
                ap_dn = [[8 * NS, 128], [1, 8 * NS]]
                sdma(sy, "ld", pA, bass.AP(logA, 0, ap_dn))
                sdma(sy, "ld", pB, bass.AP(Bp, 0, ap_dn))
                sdma(sy, "ld", pC, bass.AP(Cp, 0, ap_dn))
                sdma(sy, "ld", rowA[:, :], bass.AP(bout, 0, [[128, 8], [1, 128]]))
                sdma(sy, "ld", rowB[:, :], bass.AP(b2, 0, [[128, 8], [1, 128]]))
            b0_ld = streams["ld"]["n"]

            @blk.gpsimd
            def _(gp):
                sdma(gp, "cst", cQW[:, :], cQW_in.ap())
                sdma(gp, "cst", gbc[:, :], bc_ap(gam))
                sdma(gp, "cst", bbc[:, :], bc_ap(bet))
                sdma(gp, "cst", cTC[:, :], cTC_in.ap())
                sdma(gp, "cst", cID[:, :], cID_in.ap())
            b0_cst = streams["cst"]["n"]

            @blk.tensor
            def _(te):
                swt(te, "ld", b0_ld)
                te.wait_ge(s_dv, 1)
                bump(te.transpose(ps_c[:, 0:128], xsq[:, 0:128], cIDf), "te")
            b0_te = ctr["te"]

            @blk.vector
            def _(dv):
                dv = _DR(dv)
                for col, val in [(C_EPS, EPS), (C_1, 1.0), (C_NHALF, -0.5), (C_PHALF, 0.5),
                                 (C_HALF, 0.5), (C_INVD, 1.0 / 1024.0), (C_N1, -1.0),
                                 (C_LAM0, LAM0), (C_NT, -TSn), (C_2, 2.0),
                                 (C_INVW, 1.0 / (FF * D)), (C_W8, 1e-8),
                                 (C_0, 0.0), (C_SP, 0.0)]:
                    dv.memset(cst[:, col:col + 1], val)
                dv.memset(onesc[:, :], 1.0)
                dv.wait_ge(s_te, b0_te)
                bump(dv.tensor_copy(b1sb[:, :], ps_c[:, 0:32]), "dv")
                swt(dv, "ld", b0_ld)
                bump(dv.tensor_add(rowA[:, :], rowA[:, :], rowB[:, :]), "dv")
            b0_dv = ctr["dv"]

            @blk.scalar
            def _(ac):
                ac = _DR(ac)
                swt(ac, "ld", b0_ld)
                bump(ac.activation(dte[:, :], dt8[:, :], AF.Exp), "a2")
                bump(ac.activation(pA, pA, AF.Exp), "a2")
            b0_a2 = ctr["a2"]

            @blk.sync
            def _(sy):
                sy.wait_ge(s_dv, b0_dv)
                sdma(sy, "ld", bass.AP(scr_rows, 0, [[128, 8], [1, 128]]),
                     rowA[:, :])
            b0_ld2 = streams["ld"]["n"]

        if stop_after == "B0":
            _early_out([stA, stX])
            return nc

        with nc.Block() as blk:
            @blk.vector
            def _(dv):
                dv = _DR(dv)
                dv.wait_ge(s_a2, b0_a2)
                dtx3 = dtx.rearrange("p (b n) -> p b n", n=64)
                for n in range(64):
                    dv.tensor_copy(dtx3[:, :, n:n + 1], dte[:, :])
                dv.tensor_mul(pA, pA, dtx)
                dv.tensor_mul(pB, pB, dtx)
                dv.tensor_mul(pC, pC, pB)
                dv.tensor_scalar(dtx, pA, csl(C_LAM0), csl(C_NT),
                                 ALU.subtract, ALU.mult)
                for k in range(NMOM):
                    dv.tensor_reduce(mom[:, k * 8:(k + 1) * 8],
                                     pC.rearrange("p (b n) -> p b n", n=64),
                                     AX.X, ALU.add)
                    if k < NMOM - 1:
                        dv.tensor_mul(pC, pC, dtx)
                bump(dv.tensor_copy(tiny[:, TY_MR:TY_MR + 1], csl(C_1)), "dv")
            b0b_dv = ctr["dv"]

            @blk.sync
            def _(sy):
                sy.wait_ge(s_dv, b0b_dv)
                for k in range(NMOM):
                    sdma(sy, "ld", bass.AP(scr_mrow, k * D, [[8, 128], [1, 8]]),
                         mom[:, k * 8:(k + 1) * 8])
            b0b_ld = streams["ld"]["n"]

            b0b_cst = streams["cst"]["n"]

            if debug_taps:
                @blk.scalar
                def _(ac):
                    ac.wait_ge(s_dv, b0b_dv)
                    sdma(ac, "tap", taps["tap_mom"].ap(), mom[:, :])
        tap_b1 = streams["tap"]["n"]
        if stop_after == "B0b":
            _early_out([stA, stX])
            return nc

        wp1 = [("W1", k) for k in range(32)] + [("W2", k) for k in range(32)]

        def wp1_ap(which, k, half=None):
            lo = 0 if half is None else half * 512
            hi = 1024 if half is None else (half + 1) * 512
            if which == "W1":
                return W1.ap()[k * 128:(k + 1) * 128, lo:hi]
            dblk, fq = divmod(k, 4)
            return W2.ap()[dblk * 128:(dblk + 1) * 128,
                           fq * 1024 + lo:fq * 1024 + hi]

        a20 = ctr["a2"]; dv0 = ctr["dv"]; te0 = ctr["te"]
        E_sy_x = lambda tc: 16 * (tc // 2 + 1)
        E_a2_sq = lambda k: (a20 + 1) if k == 0 else (a20 + 5 * k - 3)
        E_a2_rstd = lambda tc: a20 + 5 * tc + 3
        E_a2_xnb = lambda tc: a20 + 5 * tc + 4
        E_a2_rstd2 = lambda tc: a20 + 5 * tc + 5
        E_a2_evT = lambda k: (a20 + 5 * k + 11) if k < M - 1 else (a20 + 5 * M + 2)
        E_dv_var = lambda tc: (dv0 + 1) if tc == 0 else (dv0 + 4 * tc - 2)
        E_dv_apply = lambda tc: dv0 + 4 * tc + 3
        E_dv_y2 = lambda tc: dv0 + 4 * tc + 4
        E_dv_xn2 = lambda tc: dv0 + 4 * tc + 5
        E_te_tr = lambda tc: te0 + 2 * tc + 1
        E_te_q = lambda tc: te0 + 2 * tc + 2
        MUc = lambda t: TY_MU if t % 2 == 0 else TY_MU2
        VARc = lambda t: TY_VAR if t % 2 == 0 else TY_VAR2
        RSTDc = lambda t: TY_RSTD if t % 2 == 0 else TY_RSTDB

        with nc.Block() as blk:
            @blk.sync
            def _(sy):
                for tc in range(M):
                    if tc >= 2:
                        sy.wait_ge(s_dv, E_dv_apply(tc - 2))
                    sdma(sy, f"x{tc % 2}", xt[tc % 2][:, :],
                         x_in.ap()[tc * 128:(tc + 1) * 128, :])

            @blk.scalar
            def _(ac):
                ac = _DR(ac)
                swt(ac, "x0", E_sy_x(0))
                bump(ac.activation(xsq[:, :], xt[0][:, :], AF.Square), "a2")
                for tc in range(M):
                    if tc + 1 < M:
                        t1 = tc + 1
                        swt(ac, f"x{t1 % 2}", E_sy_x(t1))
                        ac.wait_ge(s_dv, E_dv_xn2(tc - 1) if tc >= 1
                                   else E_dv_var(0))
                        bump(ac.activation(xsq[:, :], xt[t1 % 2][:, :], AF.Square),
                             "a2")
                    else:
                        bump(ac.activation(tiny[:, TY_MR:TY_MR + 1], csl(C_1),
                                           AF.Copy, bias=0.0, scale=1.0), "a2")
                    ac.wait_ge(s_dv, E_dv_var(tc))
                    bump(ac.activation(tiny[:, RSTDc(tc):RSTDc(tc) + 1],
                                       tiny[:, VARc(tc):VARc(tc) + 1], AF.Sqrt,
                                       bias=csl(C_EPS), scale=csl(C_1)), "a2")
                    ac.wait_ge(s_dv, E_dv_apply(tc))
                    if tc >= 2:
                        ac.wait_ge(s_te, E_te_tr(tc - 2))
                    bump(ac.activation(xn_b[tc % 2][:, :], xn_f[:, :], AF.Copy,
                                       bias=0.0, scale=1.0), "a2")
                    ac.wait_ge(s_dv, E_dv_y2(tc))
                    bump(ac.activation(tiny[:, TY_RSTD2:TY_RSTD2 + 1],
                                       tiny[:, TY_Y2:TY_Y2 + 1], AF.Sqrt,
                                       bias=csl(C_EPS), scale=csl(C_1)), "a2")
                    if tc >= 1:
                        ac.wait_ge(s_te, E_te_tr(tc - 1))
                        bump(ac.activation(xnT3[:, :, (tc - 1) * 128:tc * 128], ps_d3,
                                           AF.Copy, bias=0.0, scale=1.0), "a2")
                    else:
                        bump(ac.activation(tiny[:, TY_MR:TY_MR + 1], csl(C_1),
                                           AF.Copy, bias=0.0, scale=1.0), "a2")
                    if debug_taps:
                        sdma(ac, "tap", taps["tap_xn"].ap()[tc * 128:(tc + 1) * 128, :],
                             xn_f[:, :])
                ac.wait_ge(s_te, E_te_tr(M - 1))
                bump(ac.activation(xnT3[:, :, (M - 1) * 128:M * 128], ps_d3,
                                   AF.Copy, bias=0.0, scale=1.0), "a2")
                assert ctr["a2"] == a20 + 5 * M + 2

            @blk.vector
            def _(dv):
                dv = _DR(dv)
                swt(dv, "ld", b0_ld)
                swt(dv, "cst", b0b_cst)

                def stageA(t):
                    p2 = t % 2
                    swt(dv, f"x{p2}", E_sy_x(t))
                    dv.tensor_reduce(tiny[:, TY_SX:TY_SX + 1], xt[p2][:, :],
                                     AX.X, ALU.add)
                    dv.wait_ge(s_a2, E_a2_sq(t))
                    dv.tensor_reduce(tiny[:, TY_SXX:TY_SXX + 1], xsq[:, :],
                                     AX.X, ALU.add)
                    dv.tensor_scalar_mul(tiny[:, MUc(t):MUc(t) + 1],
                                         tiny[:, TY_SX:TY_SX + 1], csl(C_INVD))
                    dv.tensor_scalar_mul(tiny[:, TY_EX2:TY_EX2 + 1],
                                         tiny[:, TY_SXX:TY_SXX + 1], csl(C_INVD))
                    dv.tensor_mul(tiny[:, TY_MSQ:TY_MSQ + 1],
                                  tiny[:, MUc(t):MUc(t) + 1],
                                  tiny[:, MUc(t):MUc(t) + 1])
                    bump(dv.tensor_sub(tiny[:, VARc(t):VARc(t) + 1],
                                       tiny[:, TY_EX2:TY_EX2 + 1],
                                       tiny[:, TY_MSQ:TY_MSQ + 1]), "dv")

                def stageB(t):
                    p2 = t % 2
                    dv.wait_ge(s_a2, E_a2_rstd(t))
                    dv.reciprocal(tiny[:, RSTDc(t):RSTDc(t) + 1],
                                  tiny[:, RSTDc(t):RSTDc(t) + 1])
                    if debug_taps and t >= 1:
                        swt(dv, "tap", tap_b1 + 16 * t)
                    bump(dv.tensor_scalar(xn_f[:, :], xt[p2][:, :],
                                          tiny[:, MUc(t):MUc(t) + 1],
                                          tiny[:, RSTDc(t):RSTDc(t) + 1],
                                          ALU.subtract, ALU.mult), "dv")
                    dv.tensor_mul(tiny[:, TY_T1:TY_T1 + 1],
                                  tiny[:, VARc(t):VARc(t) + 1],
                                  tiny[:, RSTDc(t):RSTDc(t) + 1])
                    bump(dv.tensor_mul(tiny[:, TY_Y2:TY_Y2 + 1],
                                       tiny[:, TY_T1:TY_T1 + 1],
                                       tiny[:, RSTDc(t):RSTDc(t) + 1]), "dv")
                    dv.tensor_scalar_add(stat_a[:, t:t + 1],
                                         tiny[:, RSTDc(t):RSTDc(t) + 1], csl(C_1))
                    dv.tensor_scalar(stat_b[:, t:t + 1], tiny[:, MUc(t):MUc(t) + 1],
                                     tiny[:, RSTDc(t):RSTDc(t) + 1], csl(C_N1),
                                     ALU.mult, ALU.mult)
                    dv.wait_ge(s_a2, E_a2_rstd2(t))
                    dv.reciprocal(tiny[:, TY_RSTD2:TY_RSTD2 + 1],
                                  tiny[:, TY_RSTD2:TY_RSTD2 + 1])
                    dv.scalar_tensor_tensor(xsq[:, :], xn_f[:, :],
                                            tiny[:, TY_RSTD2:TY_RSTD2 + 1], gbc[:, :],
                                            ALU.mult, ALU.mult)
                    bump(dv.tensor_add(xn2sb[:, t * 1024:(t + 1) * 1024],
                                       xsq[:, :], bbc[:, :]), "dv")

                stageA(0)
                for tc in range(M):
                    if tc + 1 < M:
                        stageA(tc + 1)
                    else:
                        bump(dv.tensor_copy(tiny[:, TY_MR:TY_MR + 1], csl(C_1)), "dv")
                    stageB(tc)
                assert ctr["dv"] == dv0 + 4 * M + 1

            @blk.tensor
            def _(te):
                swt(te, "cst", b0b_cst)
                for tc in range(M):
                    p = tc % 2
                    te.wait_ge(s_a2, E_a2_xnb(tc))
                    if tc >= 1:
                        te.wait_ge(s_a2, E_a2_evT(tc - 1))
                    for dblk in range(8):
                        inst = te.transpose(ps_d[:, dblk * 128:(dblk + 1) * 128],
                                            xn_b[p][:, dblk * 128:(dblk + 1) * 128],
                                            cID[:, :])
                    bump(inst, "te")
                    te.wait_ge(s_dv, E_dv_xn2(tc))
                    te.matmul(ps_c[:, 0:512], cQW[:, tc * 128:(tc + 1) * 128],
                              xn2sb[:, tc * 1024:tc * 1024 + 512],
                              start=(tc == 0), stop=(tc == M - 1))
                    bump(te.matmul(ps_c[:, 512:1024], cQW[:, tc * 128:(tc + 1) * 128],
                                   xn2sb[:, tc * 1024 + 512:(tc + 1) * 1024],
                                   start=(tc == 0), stop=(tc == M - 1)), "te")
                assert ctr["te"] == te0 + 2 * M

        if stop_after == "B1":
            _early_out([stA, stX])
            return nc

        with nc.Block() as blk:
            @blk.vector
            def _(dv):
                dv = _DR(dv)
                dv.tensor_copy(q64[:, :], ps_c[:, :])
                dv.memset(zrep[:, :], 0.0)
                dv.memset(mstk1[0][:, :], 0.0)
                bump(dv.memset(mstk1[1][:, :], 0.0), "dv")
            b2_dv1 = ctr["dv"]

            @blk.tensor
            def _(te):
                te.wait_ge(s_dv, b2_dv1)
                for par, tgt in ((0, ps_e), (1, ps_c)):
                    te.matmul(tgt[:, 0:512], cTC[:, par * 128:(par + 1) * 128],
                              q64[:, 0:512], start=True, stop=True)
                    inst = te.matmul(tgt[:, 512:1024], cTC[:, par * 128:(par + 1) * 128],
                                     q64[:, 512:1024], start=True, stop=True)
                bump(inst, "te")
            b2_te1 = ctr["te"]

            dv_b2 = ctr["dv"]
            E_dv_v = lambda kap: dv_b2 + kap + 1
            E_gp_mk = lambda kap: 64 * (kap + 1)

            @blk.gpsimd
            def _(gp):
                swt(gp, "ld", b0b_ld)
                gp.wait_ge(s_dv, b2_dv1)
                for kap in range(NK):
                    if kap >= 2:
                        gp.wait_ge(s_dv, E_dv_v(kap - 2))
                    for e2 in range(DA + 1):
                        sdma(gp, "mk", mstk1[kap % 2][e2 * 32:e2 * 32 + 16, :],
                             bass.AP(scr_mrow, (kap + e2) * D, [[0, 16], [1, D]]))

            @blk.vector
            def _(dv):
                dv = _DR(dv)
                dv.wait_ge(s_te, b2_te1)
                for kap in range(NK):
                    src_ps = ps_e if kap % 2 == 0 else ps_c
                    base = (kap // 2) * 32
                    swt(dv, "mk", E_gp_mk(kap))
                    dv.tensor_copy(zrep[0:16, :], src_ps[base:base + 16, :])
                    for e2 in range(1, 4):
                        dv.tensor_copy(zrep[e2 * 32:e2 * 32 + 16, :], zrep[0:16, :])
                    if kap == 0:
                        bump(dv.tensor_mul(Vt[:, :], zrep[:, :], mstk1[0][:, :]), "dv")
                    else:
                        dv.tensor_mul(xn_f[:, :], zrep[:, :], mstk1[kap % 2][:, :])
                        bump(dv.tensor_add(Vt[:, :], Vt[:, :], xn_f[:, :]), "dv")
                assert ctr["dv"] == E_dv_v(NK - 1)
            b2_dv2 = ctr["dv"]

            if debug_taps:
                @blk.gpsimd
                def _(gp):
                    gp.wait_ge(s_dv, b2_dv2)
                    sdma(gp, "tapg", taps["tap_q"].ap(), q64[:, :])
                    sdma(gp, "tapg", taps["tap_V"].ap(), Vt[:, :])

        if stop_after == "B2":
            _early_out([stA, stX])
            return nc

        stA.close()
        enB = stB.enter_context
        u_b = [enB(nc.sbuf_tensor(f"ub_{k}", [128, D], BF16))
               for k in range(DA + 2)]
        y_b = enB(nc.sbuf_tensor("y_b", [128, D], BF16))
        cP = enB(nc.sbuf_tensor("sb_cP", [128, (DA + 1) * 128], BF16))
        cWE = enB(nc.sbuf_tensor("sb_cWE", [128, M * 128], BF16))
        mtb = enB(nc.sbuf_tensor("mtb", [128, (DA + 1) * D], BF16))
        dbc = enB(nc.sbuf_tensor("dbc", [128, D], BF16))
        wabs = [enB(nc.sbuf_tensor(f"wabs{i}", [128, 512], F32)) for i in range(4)]
        wacc = enB(nc.sbuf_tensor("wacc", [1, 512], F32))
        scal1 = enB(nc.sbuf_tensor("scal1", [1, 8], F32))

        dv0 = ctr["dv"]; te0 = ctr["te"]
        E_dv_uh = lambda c, h: dv0 + 4 * c + 1 + h
        E_dv_yev = lambda k: (dv0 + 4 * (k + 1) + 3) if k < M - 1 else (dv0 + 4 * M + 1)

        def E_dv_ytr(k):
            if k <= M - 3:
                return dv0 + 4 * (k + 2) + 4
            return dv0 + 4 * M + 2 if k == M - 2 else dv0 + 4 * M + 3
        E_te_ymm = lambda c, h: te0 + 3 * c + 1 + h
        E_te_ytr = lambda k: (te0 + 3 * (k + 1) + 3) if k < M - 1 else (te0 + 3 * M + 1)
        tapg_b3 = streams["tapg"]["n"]

        with nc.Block() as blk:
            @blk.gpsimd
            def _(gp):
                swt(gp, "ld", b0b_ld)
                if debug_taps:
                    swt(gp, "tap", streams["tap"]["n"])
                sdma(gp, "b3l", cP[:, :], cP_in.ap())
                sdma(gp, "b3l", cWE[:, :], cWE_in.ap())
                sdma(gp, "b3l", dbc[:, :], bc_ap(Dp))
                for k in range(DA + 1):
                    sdma(gp, "b3l", mtb[:, k * D:(k + 1) * D],
                         bass.AP(scr_mrow, k * D, [[0, 128], [1, D]]))
                for wk in range(64):
                    if wk >= 2:
                        gp.wait_ge(s_wa, 2 * wk - 2)
                    if debug_taps:
                        sdma(gp, f"wp{wk % 2}", wchunk[wk % 2][:, :],
                             wp1_ap(*wp1[wk]))
                    else:
                        sdma(gp, f"wp{wk % 2}", wchunk[wk % 2][:, 512:1024],
                             wp1_ap(*wp1[wk], half=1))
            b3_l = streams["b3l"]["n"]

            if not debug_taps:
                @blk.sync
                def _(sy):
                    for wk in range(64):
                        if wk >= 2:
                            sy.wait_ge(s_wa, 2 * wk - 2)
                        sdma(sy, f"wq{wk % 2}", wchunk[wk % 2][:, 0:512],
                             wp1_ap(*wp1[wk], half=0))

            @blk.scalar
            def _(ac):
                ac = _DR(ac)
                for j in range(128):
                    wk, h = divmod(j, 2)
                    if h == 1 or debug_taps:
                        swt(ac, f"wp{wk % 2}", 16 * (wk // 2 + 1))
                    else:
                        swt(ac, f"wq{wk % 2}", 16 * (wk // 2 + 1))
                    if j >= 4:
                        ac.wait_ge(s_wm, j - 3)
                    ac.activation(wabs[j % 4][:, :],
                                  wchunk[wk % 2][:, h * 512:(h + 1) * 512],
                                  AF.Abs).then_inc(s_wa, 1)

            @blk.vector
            def _(dv):
                dv = _DR(dv)
                swt(dv, "b3l", b3_l)
                for c in range(M):
                    p = c % 2
                    x2sl3 = xn2sb.ap().rearrange("p (c2 h f) -> p c2 h f",
                                                 c2=M, h=2, f=512)
                    for h in range(2):
                        if c >= 1:
                            dv.wait_ge(s_te, E_te_ymm(c - 1, h))
                        hs = slice(h * 512, (h + 1) * 512)
                        for k in range(DA + 1):
                            dv.tensor_mul(u_b[k][:, hs],
                                          xn2sb[:, c * 1024 + h * 512:
                                                c * 1024 + (h + 1) * 512],
                                          mtb[:, k * D + h * 512:k * D + (h + 1) * 512])
                        bump(dv.tensor_mul(u_b[DA + 1][:, hs],
                                           xn2sb[:, c * 1024 + h * 512:
                                                 c * 1024 + (h + 1) * 512],
                                           dbc[:, hs]), "dv")
                    if c >= 1:
                        kk = c - 1
                        dv.wait_ge(s_te, E_te_ymm(kk, 1))
                        if kk >= 1:
                            dv.wait_ge(s_te, E_te_ytr(kk - 1))
                        if debug_taps and kk >= 1:
                            swt(dv, f"tg{(kk - 1) % 2}", 16 * ((kk - 1) // 2 + 1))
                        psy = ps_c if kk % 2 == 0 else ps_e
                        bump(dv.tensor_copy(y_b[:, :], psy[:, :]), "dv")
                    else:
                        bump(dv.tensor_copy(tiny[:, TY_MR:TY_MR + 1], csl(C_1)), "dv")
                    if c >= 2:
                        kk = c - 2
                        dv.wait_ge(s_te, E_te_ytr(kk))
                        bump(dv.tensor_copy(yT3[:, :, kk * 128:(kk + 1) * 128], ps_d3),
                             "dv")
                    else:
                        bump(dv.tensor_copy(tiny[:, TY_MR:TY_MR + 1], csl(C_1)), "dv")
                    if c % 2 == 1:
                        g = (c - 1) // 2
                        which, gl = (0, g) if g < 4 else (1, g - 4)
                        srcp = ps_a if which == 0 else ps_b
                        dv.wait_ge(s_wm, 8 * c + 8)
                        if gl == 0:
                            dv.tensor_copy(wacc[0:1, :],
                                           srcp[0:1, 0:512]).then_inc(s_wv, 1)
                        else:
                            dv.tensor_add(wacc[0:1, :], wacc[0:1, :],
                                          srcp[0:1, 0:512]).then_inc(s_wv, 1)
                        if gl == 3:
                            n = 256
                            while n >= 1:
                                dv.tensor_add(wacc[0:1, 0:n], wacc[0:1, 0:n],
                                              wacc[0:1, n:2 * n])
                                n //= 2
                            dv.tensor_copy(scal1[0:1, which:which + 1],
                                           wacc[0:1, 0:1])
                dv.wait_ge(s_te, E_te_ymm(M - 1, 1))
                dv.wait_ge(s_te, E_te_ytr(M - 2))
                if debug_taps:
                    swt(dv, f"tg{(M - 2) % 2}", 16 * ((M - 2) // 2 + 1))
                psy = ps_c if (M - 1) % 2 == 0 else ps_e
                bump(dv.tensor_copy(y_b[:, :], psy[:, :]), "dv")
                bump(dv.tensor_copy(yT3[:, :, (M - 2) * 128:(M - 1) * 128], ps_d3), "dv")
                dv.wait_ge(s_te, E_te_ytr(M - 1))
                bump(dv.tensor_copy(yT3[:, :, (M - 1) * 128:M * 128], ps_d3), "dv")
                assert ctr["dv"] == dv0 + 4 * M + 3

            @blk.tensor
            def _(te):
                def wmm(j):
                    te.wait_ge(s_wa, j + 1)
                    which = 0 if j < 64 else 1
                    tgt = ps_a if which == 0 else ps_b
                    gl = (j % 64) // 16
                    if j % 16 == 0 and gl >= 1:
                        te.wait_ge(s_wv, which * 4 + gl)
                    te.matmul(tgt[0:1, 0:512], onesc[:, 0:1], wabs[j % 4][:, :],
                              start=(j % 16 == 0),
                              stop=(j % 16 == 15)).then_inc(s_wm, 1)

                for c in range(M):
                    p = c % 2
                    psy = ps_c if c % 2 == 0 else ps_e
                    for half in range(2):
                        te.wait_ge(s_dv, E_dv_uh(c, half))
                        if half == 0 and c >= 2:
                            te.wait_ge(s_dv, E_dv_yev(c - 2))
                        sl = slice(half * 512, (half + 1) * 512)
                        for k in range(DA + 1):
                            te.matmul(psy[:, sl], cP[:, k * 128:(k + 1) * 128],
                                      u_b[k][:, sl], start=(k == 0), stop=False)
                        te.matmul(psy[:, sl], cID[:, :], u_b[DA + 1][:, sl],
                                  start=False, stop=False)
                        bump(te.matmul(psy[:, sl], cWE[:, c * 128:(c + 1) * 128],
                                       Vt[:, sl], start=False, stop=True), "te")
                        for j in range(8 * c + 4 * half, 8 * c + 4 * half + 4):
                            wmm(j)
                    if c >= 1:
                        kk = c - 1
                        te.wait_ge(s_dv, E_dv_yev(kk))
                        if c >= 2:
                            te.wait_ge(s_dv, E_dv_ytr(kk - 1))
                        for dblk in range(8):
                            inst = te.transpose(ps_d[:, dblk * 128:(dblk + 1) * 128],
                                                y_b[:, dblk * 128:(dblk + 1) * 128],
                                                cID[:, :])
                        bump(inst, "te")
                    else:
                        bump(te.transpose(ps_d[:, 0:128], cID[:, :], cID[:, :]), "te")
                te.wait_ge(s_dv, E_dv_yev(M - 1))
                te.wait_ge(s_dv, E_dv_ytr(M - 2))
                for dblk in range(8):
                    inst = te.transpose(ps_d[:, dblk * 128:(dblk + 1) * 128],
                                        y_b[:, dblk * 128:(dblk + 1) * 128],
                                        cID[:, :])
                bump(inst, "te")
                assert ctr["te"] == te0 + 3 * M + 1

            if debug_taps:
                @blk.sync
                def _(sy):
                    for kk in range(M):
                        sy.wait_ge(s_dv, E_dv_yev(kk))
                        sdma(sy, f"tg{kk % 2}", taps["tap_y"].ap()[kk * 128:(kk + 1) * 128, :],
                             y_b[:, :])

        if stop_after == "B3":
            _early_out([stB, stX])
            return nc

        with nc.Block() as blk:
            @blk.vector
            def _(dv):
                dv = _DR(dv)
                dv.tensor_scalar(scal1[0:1, 2:4], scal1[0:1, 0:2],
                                 cst[0:1, C_INVW:C_INVW + 1], cst[0:1, C_W8:C_W8 + 1],
                                 ALU.mult, ALU.max)
                bump(dv.reciprocal(invs[0:1, 0:2], scal1[0:1, 2:4]), "dv")
            b3b_dv = ctr["dv"]

            @blk.sync
            def _(sy):
                sy.wait_ge(s_dv, b3b_dv)
                n = 1
                while n < 128:
                    m2 = min(n, 128 - n)
                    sdma(sy, "iv", invs[n:n + m2, :], invs[0:m2, :])
                    swt(sy, "iv")
                    n += m2
            b3b_iv = streams["iv"]["n"]

            if debug_taps:
                @blk.gpsimd
                def _(gp):
                    swt(gp, "iv", b3b_iv)
                    sdma(gp, "tapg", taps["tap_invs"].ap(), invs[:, :])

        stB.close()
        enC = stC.enter_context
        w1st = [enC(nc.sbuf_tensor(f"w1st{i}", [128, 1024], BF16)) for i in range(3)]
        hst = [enC(nc.sbuf_tensor(f"hst{i}", [128, T], BF16)) for i in range(2)]
        wsgA = enC(nc.sbuf_tensor("wsgA", [128, 1024], BF16))
        wsgB = enC(nc.sbuf_tensor("wsgB", [128, 1024], BF16))
        w1tn = enC(nc.sbuf_tensor("w1tn", [128, 1024], BF16))
        w2tn = [enC(nc.sbuf_tensor(f"w2tn{i}", [128, 1024], BF16)) for i in range(2)]
        wchC = [enC(nc.sbuf_tensor(f"wchC{i}", [128, 1024], F32)) for i in range(2)]

        a20 = ctr["a2"]; dv0 = ctr["dv"]; te0 = ctr["te"]
        nA, nD, nT = a20, dv0, te0
        A_s1 = {}; A_g = {}; A_s2 = {}
        D_a1 = {}; D_e1 = {}; D_a2 = {}; D_e2 = {}
        T_t1 = {}; T_h = {}; T_t2 = {}
        nA += 2; A_s1[0] = nA
        nA += 2; A_s1[1] = nA
        nD += 1; D_a1[0] = nD
        nD += 1; D_e1[0] = nD
        nD += 1; D_a1[1] = nD
        nD += 1; D_e1[1] = nD
        nT += 1; T_t1[0] = nT
        nT += 1; T_t1[1] = nT
        for fc in range(32):
            if fc <= 29:
                nA += 2; A_s1[fc + 2] = nA
            for t4 in range(4):
                nA += 1; A_g[(fc, t4)] = nA
            nA += 2; A_s2[fc] = nA
            if fc <= 29:
                nD += 1; D_a1[fc + 2] = nD
                nD += 1; D_e1[fc + 2] = nD
            nD += 1; D_a2[fc] = nD
            if fc >= 1:
                nD += 1; D_e2[fc - 1] = nD
            for t4 in range(4):
                nT += 1; T_h[(fc, t4)] = nT
            if fc <= 29:
                nT += 1; T_t1[fc + 2] = nT
            if fc >= 1:
                nT += 1; T_t2[fc - 1] = nT
        nT += 1; T_t2[31] = nT
        nD += 1; D_e2[31] = nD
        E_ac_hw = lambda fc: 16 * (fc // 2 + 1)

        with nc.Block() as blk:
            @blk.gpsimd
            def _(gp):
                sdma(gp, "w1l0", wchunk[0][:, :], wp1_ap("W1", 0))
                sdma(gp, "w2l0", wchC[0][:, :], wp1_ap("W2", 0))
                sdma(gp, "w1l0", wchunk[1][:, :], wp1_ap("W1", 1))
                for fc in range(32):
                    if fc <= 29:
                        k = fc + 2
                        gp.wait_ge(s_a2, A_s1[fc])
                        sdma(gp, "w1l0", wchunk[k % 2][:, :], wp1_ap("W1", k))
                    if fc <= 30:
                        k = fc + 1
                        if fc >= 1:
                            gp.wait_ge(s_a2, A_s2[fc - 1])
                        sdma(gp, "w2l0", wchC[k % 2][:, :], wp1_ap("W2", k))

            @blk.scalar
            def _(ac):
                ac = _DR(ac)
                swt(ac, "iv", b3b_iv)
                for k in range(2):
                    swt(ac, "w1l0", 16 * (k + 1))
                    if k == 1:
                        ac.wait_ge(s_dv, D_a1[0])
                    bump(ac.activation(wsgA[:, :], wchunk[k][:, :], AF.Sign,
                                       bias=csl(C_NHALF), scale=invs[:, 0:1]), "a2")
                    bump(ac.activation(wsgB[:, :], wchunk[k][:, :], AF.Sign,
                                       bias=csl(C_PHALF), scale=invs[:, 0:1]), "a2")
                for fc in range(32):
                    p = fc % 2
                    if fc <= 29:
                        k = fc + 2
                        swt(ac, "w1l0", 16 * (k + 1))
                        ac.wait_ge(s_dv, D_a2[fc - 1] if fc >= 1 else D_a1[1])
                        bump(ac.activation(wsgA[:, :], wchunk[k % 2][:, :], AF.Sign,
                                           bias=csl(C_NHALF), scale=invs[:, 0:1]),
                             "a2")
                        bump(ac.activation(wsgB[:, :], wchunk[k % 2][:, :], AF.Sign,
                                           bias=csl(C_PHALF), scale=invs[:, 0:1]),
                             "a2")
                    for t4 in range(4):
                        psh = (ps_a[:, :], ps_b[:, :], ps_c[:, 0:512],
                               ps_e[:, 0:512])[t4]
                        ac.wait_ge(s_te, T_h[(fc, t4)])
                        if fc >= 2 and t4 == 0:
                            swt(ac, f"hw{fc % 2}", E_ac_hw(fc - 2))
                        bump(ac.activation(hst[p][:, t4 * 512:(t4 + 1) * 512],
                                           psh, AF.Gelu,
                                           bias=b1sb[:, fc:fc + 1],
                                           scale=csl(C_HALF)), "a2")
                    swt(ac, "w2l0", 16 * (fc + 1))
                    ac.wait_ge(s_dv, D_a1[fc + 2] if fc <= 29 else D_a1[31])
                    bump(ac.activation(wsgA[:, :], wchC[fc % 2][:, :], AF.Sign,
                                       bias=csl(C_NHALF), scale=invs[:, 1:2]), "a2")
                    bump(ac.activation(wsgB[:, :], wchC[fc % 2][:, :], AF.Sign,
                                       bias=csl(C_PHALF), scale=invs[:, 1:2]), "a2")
                    sdma(ac, f"hw{fc % 2}", scr_h.ap()[fc, :, :], hst[p][:, :])
                assert ctr["a2"] == nA

            @blk.vector
            def _(dv):
                dv = _DR(dv)
                for k in range(2):
                    dv.wait_ge(s_a2, A_s1[k])
                    if k == 1:
                        dv.wait_ge(s_te, T_t1[0])
                    bump(dv.tensor_add(w1tn[:, :], wsgA[:, :], wsgB[:, :]), "dv")
                    dv.wait_ge(s_te, T_t1[k])
                    bump(dv.tensor_copy(
                        w1st[k].ap().rearrange("p (b f) -> p b f", f=128), ps_d3),
                        "dv")
                for fc in range(32):
                    if fc <= 29:
                        k = fc + 2
                        dv.wait_ge(s_a2, A_s1[k])
                        dv.wait_ge(s_te, T_t1[k - 1])
                        bump(dv.tensor_add(w1tn[:, :], wsgA[:, :], wsgB[:, :]), "dv")
                        dv.wait_ge(s_te, T_t1[k])
                        if fc >= 1:
                            dv.wait_ge(s_te, T_h[(fc - 1, 3)])
                        bump(dv.tensor_copy(
                            w1st[k % 3].ap().rearrange("p (b f) -> p b f", f=128),
                            ps_d3), "dv")
                    dv.wait_ge(s_a2, A_s2[fc])
                    if fc >= 2:
                        dv.wait_ge(s_te, T_t2[fc - 2])
                    bump(dv.tensor_add(w2tn[fc % 2][:, :], wsgA[:, :], wsgB[:, :]),
                         "dv")
                    if fc >= 1:
                        u = fc - 1
                        dblk, fq = divmod(u, 4)
                        dv.wait_ge(s_te, T_t2[u])
                        bump(dv.tensor_copy(
                            W2T3[:, fq * 8:(fq + 1) * 8, dblk * 128:(dblk + 1) * 128],
                            ps_d3), "dv")
                u = 31
                dblk, fq = divmod(u, 4)
                dv.wait_ge(s_te, T_t2[u])
                bump(dv.tensor_copy(
                    W2T3[:, fq * 8:(fq + 1) * 8, dblk * 128:(dblk + 1) * 128],
                    ps_d3), "dv")
                assert ctr["dv"] == nD

            @blk.tensor
            def _(te):
                for k in range(2):
                    te.wait_ge(s_dv, D_a1[k])
                    if k == 1:
                        te.wait_ge(s_dv, D_e1[0])
                    for dblk in range(8):
                        inst = te.transpose(ps_d[:, dblk * 128:(dblk + 1) * 128],
                                            w1tn[:, dblk * 128:(dblk + 1) * 128],
                                            cID[:, :])
                    bump(inst, "te")
                for fc in range(32):
                    te.wait_ge(s_dv, D_e1[fc])
                    for t4 in range(4):
                        psh = (ps_a[:, :], ps_b[:, :], ps_c[:, 0:512],
                               ps_e[:, 0:512])[t4]
                        gi = fc * 4 + t4
                        if gi >= 4:
                            te.wait_ge(s_a2, A_g[((gi - 4) // 4, (gi - 4) % 4)])
                        for dblk in range(8):
                            inst = te.matmul(psh,
                                             w1st[fc % 3][:, dblk * 128:(dblk + 1) * 128],
                                             xnT[:, dblk * T + t4 * 512:
                                                 dblk * T + (t4 + 1) * 512],
                                             start=(dblk == 0), stop=(dblk == 7))
                        bump(inst, "te")
                    if fc <= 29:
                        k = fc + 2
                        te.wait_ge(s_dv, D_a1[k])
                        te.wait_ge(s_dv, D_e1[k - 1] if k <= 3 else D_e2[k - 4])
                        for dblk in range(8):
                            inst = te.transpose(ps_d[:, dblk * 128:(dblk + 1) * 128],
                                                w1tn[:, dblk * 128:(dblk + 1) * 128],
                                                cID[:, :])
                        bump(inst, "te")
                    if fc >= 1:
                        u = fc - 1
                        te.wait_ge(s_dv, D_a2[u])
                        te.wait_ge(s_dv, D_e1[u + 3] if u + 3 <= 31 else D_e2[u - 1])
                        for dblk in range(8):
                            inst = te.transpose(ps_d[:, dblk * 128:(dblk + 1) * 128],
                                                w2tn[u % 2][:, dblk * 128:(dblk + 1) * 128],
                                                cID[:, :])
                        bump(inst, "te")
                u = 31
                te.wait_ge(s_dv, D_a2[u])
                te.wait_ge(s_dv, D_e2[30])
                for dblk in range(8):
                    inst = te.transpose(ps_d[:, dblk * 128:(dblk + 1) * 128],
                                        w2tn[u % 2][:, dblk * 128:(dblk + 1) * 128],
                                        cID[:, :])
                bump(inst, "te")
                assert ctr["te"] == nT

        if stop_after == "B5":
            _early_out([stC, stX])
            return nc

        stC.close()
        stX.close()
        enD = stD.enter_context
        WoT = enD(nc.sbuf_tensor("WoT", [128, 8 * 1024], BF16))
        WoT3 = WoT.ap().rearrange("p (b c) -> p b c", c=1024)
        hgrp = [enD(nc.sbuf_tensor(f"hgrp{i}", [128, 32 * 256], BF16)) for i in range(2)]
        xr = [enD(nc.sbuf_tensor(f"xr{i}", [128, D], F32)) for i in range(2)]
        osb = [enD(nc.sbuf_tensor(f"osb{i}", [128, D], F32)) for i in range(2)]
        t1f = enD(nc.sbuf_tensor("t1f", [128, D], F32))
        t2f = enD(nc.sbuf_tensor("t2f", [128, D], F32))
        bias1bc = enD(nc.sbuf_tensor("bias1bc", [128, D], F32))

        a20 = ctr["a2"]; te0 = ctr["te"]; dv0 = ctr["dv"]
        E_a2_wo = lambda k: a20 + k + 1
        E_te_wo = lambda k: te0 + k + 1
        E_dv_wo = lambda k: dv0 + k + 1
        b6w_base = [streams["w1c0"]["n"], streams["w1c1"]["n"]]
        with nc.Block() as blk:
            @blk.gpsimd
            def _(gp):
                swt(gp, "ld", b0_ld2)
                sdma(gp, "cst", bias1bc[:, :], bass.AP(scr_rows, 0, [[0, 128], [1, D]]))
            b6_cst = streams["cst"]["n"]

            @blk.sync
            def _(sy):
                for eb in range(8):
                    if eb >= 2:
                        sy.wait_ge(s_a2, E_a2_wo(eb - 2))
                    sdma(sy, f"w1c{eb % 2}", wchunk[eb % 2][:, :],
                         Wout.ap()[eb * 128:(eb + 1) * 128, :])

            @blk.scalar
            def _(ac):
                ac = _DR(ac)
                for eb in range(8):
                    swt(ac, f"w1c{eb % 2}", b6w_base[eb % 2] + 16 * (eb // 2 + 1))
                    if eb >= 2:
                        ac.wait_ge(s_te, E_te_wo(eb - 2))
                    bump(ac.activation(hgrp[eb % 2][:, 0:1024], wchunk[eb % 2][:, :],
                                       AF.Copy, bias=0.0, scale=2.0), "a2")

            @blk.tensor
            def _(te):
                for eb in range(8):
                    te.wait_ge(s_a2, E_a2_wo(eb))
                    if eb >= 1:
                        te.wait_ge(s_dv, E_dv_wo(eb - 1))
                    for dblk in range(8):
                        inst = te.transpose(ps_d[:, dblk * 128:(dblk + 1) * 128],
                                            hgrp[eb % 2][:, dblk * 128:(dblk + 1) * 128],
                                            cID[:, :])
                    bump(inst, "te")

            @blk.vector
            def _(dv):
                dv = _DR(dv)
                for eb in range(8):
                    dv.wait_ge(s_te, E_te_wo(eb))
                    bump(dv.tensor_copy(WoT3[:, :, eb * 128:(eb + 1) * 128], ps_d3), "dv")

        te0 = ctr["te"]; dv0 = ctr["dv"]
        E_te_o = lambda tg, tk: te0 + 2 * tg + tk + 1
        E_dv_oe = lambda tg, tk: dv0 + 2 * tg + tk + 1
        E_gp_ow = lambda tg: 16 * (tg + 1)
        E_ac_xr = lambda tg: 16 * (tg + 1)
        E_hg = lambda tg: 16 * 16 * (tg // 2 + 1)
        b5_hw = [streams["hw0"]["n"], streams["hw1"]["n"]]
        with nc.Block() as blk:
            @blk.sync
            def _(sy):
                swt(sy, "hw0", b5_hw[0])
                swt(sy, "hw1", b5_hw[1])
                for tg in range(8):
                    if tg >= 2:
                        sy.wait_ge(s_te, E_te_o(tg - 2, 1))
                    for fc in range(0, 32, 2):
                        sdma(sy, f"hge{tg % 2}", hgrp[tg % 2][:, fc * 256:(fc + 1) * 256],
                             scr_h.ap()[fc, :, tg * 256:(tg + 1) * 256])

            @blk.scalar
            def _(ac):
                ac = _DR(ac)
                swt(ac, "hw0", b5_hw[0])
                swt(ac, "hw1", b5_hw[1])
                for tg in range(8):
                    if tg >= 2:
                        ac.wait_ge(s_te, E_te_o(tg - 2, 1))
                    for fc in range(1, 32, 2):
                        sdma(ac, f"hgo{tg % 2}", hgrp[tg % 2][:, fc * 256:(fc + 1) * 256],
                             scr_h.ap()[fc, :, tg * 256:(tg + 1) * 256])
                    for tk in range(2):
                        tci = tg * 2 + tk
                        if tg >= 1:
                            ac.wait_ge(s_dv, E_dv_oe(tg - 1, tk))
                        sdma(ac, f"xr{tk}", xr[tk][:, :],
                             x_in.ap()[tci * 128:(tci + 1) * 128, :])

            @blk.tensor
            def _(te):
                for tg in range(8):
                    for tk in range(2):
                        pso = ps_c if tk == 0 else ps_e
                        swt(te, f"hge{tg % 2}", E_hg(tg))
                        swt(te, f"hgo{tg % 2}", E_hg(tg))
                        if tg >= 1:
                            te.wait_ge(s_dv, E_dv_oe(tg - 1, tk))
                        for dh in range(2):
                            sl = slice(dh * 512, (dh + 1) * 512)
                            for fc in range(32):
                                te.matmul(pso[:, sl],
                                          hgrp[tg % 2][:, fc * 256 + tk * 128:
                                                       fc * 256 + (tk + 1) * 128],
                                          W2T[:, fc * 1024 + dh * 512:
                                              fc * 1024 + (dh + 1) * 512],
                                          start=(fc == 0), stop=False)
                            for dblk in range(8):
                                inst = te.matmul(
                                    pso[:, sl],
                                    yT[:, dblk * T + (tg * 2 + tk) * 128:
                                       dblk * T + (tg * 2 + tk + 1) * 128],
                                    WoT[:, dblk * 1024 + dh * 512:
                                        dblk * 1024 + (dh + 1) * 512],
                                    start=False, stop=(dblk == 7))
                        bump(inst, "te")

            @blk.vector
            def _(dv):
                dv = _DR(dv)
                swt(dv, "cst", b6_cst)
                for tg in range(8):
                    for tk in range(2):
                        pso = ps_c if tk == 0 else ps_e
                        tmp = t1f if tk == 0 else t2f
                        tci = tg * 2 + tk
                        swt(dv, f"xr{tk}", E_ac_xr(tg))
                        dv.wait_ge(s_te, E_te_o(tg, tk))
                        if tg >= 1:
                            swt(dv, f"ow{tk}", E_gp_ow(tg - 1))
                        dv.tensor_scalar(tmp[:, :], xr[tk][:, :],
                                         stat_a[:, tci:tci + 1], stat_b[:, tci:tci + 1],
                                         ALU.mult, ALU.add)
                        dv.tensor_add(tmp[:, :], tmp[:, :], bias1bc[:, :])
                        dv.scalar_tensor_tensor(osb[tk][:, 0:512], pso[:, 0:512],
                                                csl(C_HALF), tmp[:, 0:512],
                                                ALU.mult, ALU.add)
                        dv._e.drain()
                        bump(dv.scalar_tensor_tensor(osb[tk][:, 512:1024],
                                                     pso[:, 512:1024], csl(C_HALF),
                                                     tmp[:, 512:1024], ALU.mult, ALU.add),
                             "dv")

            @blk.gpsimd
            def _(gp):
                for tg in range(8):
                    for tk in range(2):
                        tci = tg * 2 + tk
                        gp.wait_ge(s_dv, E_dv_oe(tg, tk))
                        sdma(gp, f"ow{tk}", out_t.ap()[tci * 128:(tci + 1) * 128, :],
                             osb[tk][:, :])
                swt(gp, "ow0", streams["ow0"]["n"])
                swt(gp, "ow1", streams["ow1"]["n"])

        with nc.Block() as blk:
            @blk.sync
            def _(sy):
                for nm in ["ld", "x0", "x1", "iv", "x20", "x21", "w1w0", "w1w1",
                           "w1c0", "w1c1", "hge0", "hge1", "wq0", "wq1"]:
                    swt(sy, nm, streams[nm]["n"])

            @blk.scalar
            def _(ac):
                for nm in ["xn0", "xn1", "hw0", "hw1", "hgo0", "hgo1", "xr0", "xr1",
                           "tap"]:
                    swt(ac, nm, streams[nm]["n"])

            @blk.gpsimd
            def _(gp):
                for nm in ["cst", "mk", "b3l", "wp0", "wp1", "w1l0", "w1l1",
                           "w2l0", "w2l1", "tapg", "tg0", "tg1"]:
                    swt(gp, nm, streams[nm]["n"])

        stD.close()

    return nc


def kernel(**inputs):
    consts = _host_constants()
    nc = build_nc(debug_taps=_DEBUG_TAPS, stop_after=_STOP_AFTER)
    x = np.ascontiguousarray(np.asarray(inputs["L0_attn_x"], dtype=np.float32))
    shared = {k: np.ascontiguousarray(np.asarray(v, dtype=np.float32))
              for k, v in inputs.items() if k != "L0_attn_x"}
    shared.update(consts)
    in_maps = []
    for i in range(8):
        mm = dict(shared)
        mm["x"] = np.ascontiguousarray(x[i])
        in_maps.append(mm)
    res = run_bass_kernel_spmd(nc, in_maps, core_ids=list(range(8)))
    return np.stack([np.asarray(r["out"], dtype=np.float32) for r in res.results])


# version 11
# speedup vs baseline: 1.2669x; 1.0516x over previous
import math
import numpy as np
from contextlib import ExitStack

import concourse.bass as bass
import concourse.mybir as mybir
from concourse.bass_utils import run_bass_kernel_spmd

F32 = mybir.dt.float32
BF16 = mybir.dt.bfloat16
AF = mybir.ActivationFunctionType
ALU = mybir.AluOpType
AX = mybir.AxisListType

T, D, NS, L, M, FF = 2048, 1024, 64, 128, 16, 4096
DA = 3
DBd = 4
NK = DA + DBd + 1
NMOM = DA + DBd + DA + 1
LAM0 = math.exp(-8.0)
TSn = 2048.0
EPS = 1e-5

_DEBUG_TAPS = False
_STOP_AFTER = None

_DRAIN_OPS = {
    "tensor_copy", "tensor_add", "tensor_sub", "tensor_mul", "tensor_scalar",
    "tensor_scalar_mul", "tensor_scalar_add", "tensor_scalar_max",
    "tensor_scalar_min", "tensor_reduce", "reduce_sum", "reduce_max",
    "scalar_tensor_tensor", "reciprocal", "memset", "activation", "select",
}


class _DR:

    def __init__(self, eng):
        self._e = eng

    def __getattr__(self, nm):
        f = getattr(self._e, nm)
        if nm in _DRAIN_OPS:
            def g(*a, **k):
                r = f(*a, **k)
                self._e.drain()
                return r
            return g
        return f


def _host_constants():
    c = {}
    ii = np.arange(L, dtype=np.float64)
    dif = ii[None, :] - ii[:, None]
    mask = dif >= 0
    cP = np.zeros((128, (DA + 1) * 128), dtype=np.float64)
    for k in range(DA + 1):
        cP[:, k * 128:(k + 1) * 128] = np.where(
            mask, np.exp(-LAM0 * np.maximum(dif, 0)) *
            (np.maximum(dif, 0) / TSn) ** k / math.factorial(k), 0.0)
    c["cP"] = cP.astype(np.float32)

    r1 = (L - 1 - ii)
    qw = np.stack([np.exp(-LAM0 * r1) * (r1 / TSn) ** a / math.factorial(a)
                   for a in range(DA + 1)], axis=1)
    cQW = np.zeros((128, M * 128), dtype=np.float64)
    for cc in range(M):
        for a in range(DA + 1):
            cQW[:, cc * 128 + a * 32 + cc] = qw[:, a]
    c["cQW"] = cQW.astype(np.float32)

    ccs = np.arange(M, dtype=np.float64)
    dcc = ccs[None, :] - ccs[:, None] - 1.0
    mc = dcc >= 0
    r2 = np.maximum(dcc, 0.0) * L
    cTC = np.zeros((128, 256), dtype=np.float64)
    for a in range(DA + 1):
        for b in range(DBd + 1):
            blk = np.where(mc, np.exp(-LAM0 * r2) * (r2 / TSn) ** b / math.factorial(b), 0.0)
            kap = a + b
            base = (kap % 2) * 128 + (kap // 2) * 32
            for cc in range(M):
                cTC[a * 32:a * 32 + 16, base + cc] = blk[:, cc]
    c["cTC"] = cTC.astype(np.float32)

    r3 = ii + 1.0
    we = np.stack([np.exp(-LAM0 * r3) * (r3 / TSn) ** e / math.factorial(e)
                   for e in range(DA + 1)])
    cWE = np.zeros((128, M * 128), dtype=np.float64)
    for cc in range(M):
        for e2 in range(DA + 1):
            cWE[e2 * 32 + cc, cc * 128:(cc + 1) * 128] = we[e2]
    c["cWE"] = cWE.astype(np.float32)

    c["cID"] = np.eye(128, dtype=np.float32)
    return c


def build_nc(debug_taps=False, stop_after=None):
    nc = bass.Bass("TRN2", target_bir_lowering=False)

    def din(name, shape, dtype=F32):
        return nc.dram_tensor(name, shape, dtype, kind="ExternalInput")

    x_in = din("x", [T, D])
    logA = din("L0_attn_log_A", [D, NS])
    Bp = din("L0_attn_B_param", [D, NS])
    Cp = din("L0_attn_C_param", [D, NS])
    Dp = din("L0_attn_D_param", [D])
    logdt = din("L0_attn_log_dt", [D])
    gam = din("L0_attn_ln_gamma", [D])
    bet = din("L0_attn_ln_beta", [D])
    Wout = din("L0_attn_W_out", [D, D])
    bout = din("L0_attn_b_out", [D])
    W1 = din("L0_ffn_W1", [FF, D])
    b1 = din("L0_ffn_b1", [FF])
    W2 = din("L0_ffn_W2", [D, FF])
    b2 = din("L0_ffn_b2", [D])
    cP_in = din("cP", [128, (DA + 1) * 128])
    cQW_in = din("cQW", [128, M * 128])
    cTC_in = din("cTC", [128, 256])
    cWE_in = din("cWE", [128, M * 128])
    cID_in = din("cID", [128, 128])

    out_t = nc.dram_tensor("out", [T, D], F32, kind="ExternalOutput")

    scr_mrow = nc.dram_tensor("scr_mrow", [NMOM, D], F32)
    scr_rows = nc.dram_tensor("scr_rows", [2, D], F32)
    scr_h = nc.dram_tensor("scr_h", [32, 128, T], BF16)

    taps = {}
    if debug_taps:
        for nm, shp in [("tap_xn", [T, D]), ("tap_xn2", [T, D]), ("tap_y", [T, D]),
                        ("tap_q", [128, D]), ("tap_V", [128, D]),
                        ("tap_mom", [128, NMOM * 8]), ("tap_invs", [128, 2])]:
            taps[nm] = nc.dram_tensor(nm, shp, BF16 if nm == "tap_y" else F32,
                                      kind="ExternalOutput")

    st = ExitStack()
    stX = ExitStack()
    stA = ExitStack()
    stB = ExitStack()
    stC = ExitStack()
    stD = ExitStack()
    en = st.enter_context
    with st:
        cst = en(nc.sbuf_tensor("cst", [128, 14], F32))
        (C_EPS, C_1, C_NHALF, C_PHALF, C_HALF, C_INVD, C_N1,
         C_LAM0, C_NT, C_2, C_INVW, C_W8, C_0, C_SP) = range(14)
        onesc = en(nc.sbuf_tensor("onesc", [128, 1], F32))
        for _v, _c in [(0.0, C_0), (1.0, C_1), (2.0, C_2), (0.5, C_HALF), (-0.5, C_NHALF)]:
            nc.const_aps.register(F32, _v, cst[:, _c:_c + 1])
        invs = en(nc.sbuf_tensor("invs", [128, 2], F32))
        b1sb = en(nc.sbuf_tensor("b1sb", [128, 32], F32))
        stat_a = en(nc.sbuf_tensor("stat_a", [128, M], F32))
        stat_b = en(nc.sbuf_tensor("stat_b", [128, M], F32))
        tiny = en(nc.sbuf_tensor("tiny", [128, 16], F32))
        (TY_SX, TY_SXX, TY_MU, TY_EX2, TY_MSQ, TY_VAR, TY_RSTD,
         TY_T1, TY_Y2, TY_RSTD2, TY_MR, TY_MU2, TY_VAR2, TY_RSTDB,
         TY_MR2) = range(15)
        cID = en(nc.sbuf_tensor("sb_cID", [128, 128], BF16))
        wchunk = [en(nc.sbuf_tensor(f"wch{i}", [128, 1024], F32)) for i in range(2)]
        W2T = en(nc.sbuf_tensor("W2T", [128, 32 * 1024], BF16))
        yT = en(nc.sbuf_tensor("yT", [128, 8 * T], BF16))

        xnT = stX.enter_context(nc.sbuf_tensor("xnT", [128, 8 * T], BF16))
        Vt = stX.enter_context(nc.sbuf_tensor("Vt", [128, D], BF16))
        xn2sb = stX.enter_context(nc.sbuf_tensor("xn2sb", [128, M * 1024], BF16))

        enA = stA.enter_context
        cQW = enA(nc.sbuf_tensor("sb_cQW", [128, M * 128], BF16))
        cTC = enA(nc.sbuf_tensor("sb_cTC", [128, 256], BF16))
        gbc = enA(nc.sbuf_tensor("gbc", [128, D], BF16))
        bbc = enA(nc.sbuf_tensor("bbc", [128, D], BF16))
        mom = enA(nc.sbuf_tensor("mom", [128, NMOM * 8], F32))
        dt8 = enA(nc.sbuf_tensor("dt8", [128, 8], F32))
        dte = enA(nc.sbuf_tensor("dte", [128, 8], F32))
        rowA = enA(nc.sbuf_tensor("rowA", [8, 128], F32))
        rowB = enA(nc.sbuf_tensor("rowB", [8, 128], F32))
        xt = [enA(nc.sbuf_tensor(f"xt{i}", [128, D], F32)) for i in range(2)]
        xn_f = enA(nc.sbuf_tensor("xn_f", [128, D], F32))
        xsq = enA(nc.sbuf_tensor("xsq", [128, D], F32))
        xn_b = [enA(nc.sbuf_tensor(f"xnb{i}", [128, D], BF16)) for i in range(2)]
        mstk1 = xn_b
        zrep = enA(nc.sbuf_tensor("zrep", [128, D], BF16))
        q64 = enA(nc.sbuf_tensor("q64", [128, D], BF16))

        cIDf = xn_f[:, 0:128]
        pA = xt[0][:, 0:512]
        pB = xt[0][:, 512:1024]
        pC = xt[1][:, 0:512]
        dtx = xt[1][:, 512:1024]
        b1row = xsq[0:32, 0:128]

        ps_a = en(nc.psum_tensor("ps_a", [128, 512], F32))
        ps_b = en(nc.psum_tensor("ps_b", [128, 512], F32))
        ps_c = en(nc.psum_tensor("ps_c", [128, 1024], F32))
        ps_d = en(nc.psum_tensor("ps_d", [128, 1024], BF16))
        ps_e = en(nc.psum_tensor("ps_e", [128, 1024], F32))

        s_dv = en(nc.semaphore("s_dv"))
        s_te = en(nc.semaphore("s_te"))
        s_a2 = en(nc.semaphore("s_a2"))
        s_wa = en(nc.semaphore("s_wa"))
        s_wm = en(nc.semaphore("s_wm"))
        s_wv = en(nc.semaphore("s_wv"))
        SEM = {"dv": s_dv, "te": s_te, "a2": s_a2}
        ctr = {k: 0 for k in SEM}

        def bump(inst, which, n=1):
            inst.then_inc(SEM[which], n)
            ctr[which] += n
            return ctr[which]

        streams = {}
        for _n in ["ld", "cst", "iv", "mk", "b3l",
                   "x0", "x1", "xn0", "xn1", "wp0", "wp1", "wq0", "wq1",
                   "x20", "x21", "w1l0", "w1l1", "w1w0", "w1w1",
                   "w1c0", "w1c1", "w2l0", "w2l1", "hw0", "hw1",
                   "hge0", "hge1", "hgo0", "hgo1", "xr0", "xr1",
                   "ow0", "ow1", "tap", "tapg", "tg0", "tg1"]:
            streams[_n] = {"sem": en(nc.semaphore("st_" + _n)), "n": 0}

        def sdma(q, name, out, in_):
            s = streams[name]
            q.dma_start(out, in_).then_inc(s["sem"], 16)
            s["n"] += 16
            return s["n"]

        def swt(eng, name, val=None):
            s = streams[name]
            eng.wait_ge(s["sem"], s["n"] if val is None else val)

        csl = lambda c: cst[:, c:c + 1]
        ps_d3 = ps_d.ap().rearrange("p (b f) -> p b f", f=128)
        xnT3 = xnT.ap().rearrange("p (b t) -> p b t", t=T)
        yT3 = yT.ap().rearrange("p (b t) -> p b t", t=T)
        W2T3 = W2T.ap().rearrange("p (f c) -> p f c", c=1024)

        def bc_ap(dt_, n=128, width=D, off=0):
            return bass.AP(dt_, off, [[0, n], [1, width]])

        def _early_out(open_stacks):
            with nc.Block() as blk2:
                @blk2.sync
                def _(sy):
                    sdma(sy, "ld", out_t.ap(), x_in.ap())
                    swt(sy, "ld")
                    for nm2, s2 in streams.items():
                        sy.wait_ge(s2["sem"], s2["n"]) if False else None

                @blk2.scalar
                def _(ac):
                    for nm2 in ["xn0", "xn1", "hw0", "hw1", "hgo0", "hgo1",
                                "xr0", "xr1", "tap"]:
                        swt(ac, nm2, streams[nm2]["n"])

                @blk2.sync
                def _(sy):
                    for nm2 in ["ld", "x0", "x1", "iv", "x20", "x21", "w1w0", "w1w1",
                                "w1c0", "w1c1", "hge0", "hge1", "wq0", "wq1"]:
                        swt(sy, nm2, streams[nm2]["n"])

                @blk2.gpsimd
                def _(gp):
                    for nm2 in ["cst", "mk", "b3l", "wp0", "wp1", "w1l0", "w1l1",
                                "w2l0", "w2l1", "tapg", "tg0", "tg1"]:
                        swt(gp, nm2, streams[nm2]["n"])
            for s2 in open_stacks:
                s2.close()


        with nc.Block() as blk:
            @blk.vector
            def _(dv):
                dv = _DR(dv)
                bump(dv.memset(xsq[:, :], 0.0), "dv")

            @blk.sync
            def _(sy):
                sdma(sy, "ld", cIDf, cID_in.ap())
                sy.wait_ge(s_dv, 1)
                sdma(sy, "ld", b1row, b1.ap().rearrange("(c p) -> c p", p=128))
                sdma(sy, "ld", dt8[:, :], bass.AP(logdt, 0, [[8, 128], [1, 8]]))
                ap_dn = [[8 * NS, 128], [1, 8 * NS]]
                sdma(sy, "ld", pA, bass.AP(logA, 0, ap_dn))
                sdma(sy, "ld", pB, bass.AP(Bp, 0, ap_dn))
                sdma(sy, "ld", pC, bass.AP(Cp, 0, ap_dn))
                sdma(sy, "ld", rowA[:, :], bass.AP(bout, 0, [[128, 8], [1, 128]]))
                sdma(sy, "ld", rowB[:, :], bass.AP(b2, 0, [[128, 8], [1, 128]]))
            b0_ld = streams["ld"]["n"]

            @blk.gpsimd
            def _(gp):
                sdma(gp, "cst", cQW[:, :], cQW_in.ap())
                sdma(gp, "cst", gbc[:, :], bc_ap(gam))
                sdma(gp, "cst", bbc[:, :], bc_ap(bet))
                sdma(gp, "cst", cTC[:, :], cTC_in.ap())
                sdma(gp, "cst", cID[:, :], cID_in.ap())
            b0_cst = streams["cst"]["n"]

            @blk.tensor
            def _(te):
                swt(te, "ld", b0_ld)
                te.wait_ge(s_dv, 1)
                bump(te.transpose(ps_c[:, 0:128], xsq[:, 0:128], cIDf), "te")
            b0_te = ctr["te"]

            @blk.vector
            def _(dv):
                dv = _DR(dv)
                for col, val in [(C_EPS, EPS), (C_1, 1.0), (C_NHALF, -0.5), (C_PHALF, 0.5),
                                 (C_HALF, 0.5), (C_INVD, 1.0 / 1024.0), (C_N1, -1.0),
                                 (C_LAM0, LAM0), (C_NT, -TSn), (C_2, 2.0),
                                 (C_INVW, 1.0 / (FF * D)), (C_W8, 1e-8),
                                 (C_0, 0.0), (C_SP, 0.0)]:
                    dv.memset(cst[:, col:col + 1], val)
                dv.memset(onesc[:, :], 1.0)
                dv.wait_ge(s_te, b0_te)
                bump(dv.tensor_copy(b1sb[:, :], ps_c[:, 0:32]), "dv")
                swt(dv, "ld", b0_ld)
                bump(dv.tensor_add(rowA[:, :], rowA[:, :], rowB[:, :]), "dv")
            b0_dv = ctr["dv"]

            @blk.scalar
            def _(ac):
                ac = _DR(ac)
                swt(ac, "ld", b0_ld)
                bump(ac.activation(dte[:, :], dt8[:, :], AF.Exp), "a2")
                bump(ac.activation(pA, pA, AF.Exp), "a2")
            b0_a2 = ctr["a2"]

            @blk.sync
            def _(sy):
                sy.wait_ge(s_dv, b0_dv)
                sdma(sy, "ld", bass.AP(scr_rows, 0, [[128, 8], [1, 128]]),
                     rowA[:, :])
            b0_ld2 = streams["ld"]["n"]

        if stop_after == "B0":
            _early_out([stA, stX])
            return nc

        with nc.Block() as blk:
            @blk.vector
            def _(dv):
                dv = _DR(dv)
                dv.wait_ge(s_a2, b0_a2)
                dtx3 = dtx.rearrange("p (b n) -> p b n", n=64)
                for n in range(63):
                    dv._e.tensor_copy(dtx3[:, :, n:n + 1], dte[:, :])
                dv.tensor_copy(dtx3[:, :, 63:64], dte[:, :])
                dv.tensor_mul(pA, pA, dtx)
                dv.tensor_mul(pB, pB, dtx)
                dv.tensor_mul(pC, pC, pB)
                dv.tensor_scalar(dtx, pA, csl(C_LAM0), csl(C_NT),
                                 ALU.subtract, ALU.mult)
                for k in range(NMOM):
                    dv.tensor_reduce(mom[:, k * 8:(k + 1) * 8],
                                     pC.rearrange("p (b n) -> p b n", n=64),
                                     AX.X, ALU.add)
                    if k < NMOM - 1:
                        dv.tensor_mul(pC, pC, dtx)
                bump(dv.tensor_copy(tiny[:, TY_MR:TY_MR + 1], csl(C_1)), "dv")
            b0b_dv = ctr["dv"]

            @blk.sync
            def _(sy):
                sy.wait_ge(s_dv, b0b_dv)
                for k in range(NMOM):
                    sdma(sy, "ld", bass.AP(scr_mrow, k * D, [[8, 128], [1, 8]]),
                         mom[:, k * 8:(k + 1) * 8])
            b0b_ld = streams["ld"]["n"]

            b0b_cst = streams["cst"]["n"]

            if debug_taps:
                @blk.scalar
                def _(ac):
                    ac.wait_ge(s_dv, b0b_dv)
                    sdma(ac, "tap", taps["tap_mom"].ap(), mom[:, :])
        tap_b1 = streams["tap"]["n"]
        if stop_after == "B0b":
            _early_out([stA, stX])
            return nc

        wp1 = [("W1", k) for k in range(32)] + [("W2", k) for k in range(32)]

        def wp1_ap(which, k, half=None):
            lo = 0 if half is None else half * 512
            hi = 1024 if half is None else (half + 1) * 512
            if which == "W1":
                return W1.ap()[k * 128:(k + 1) * 128, lo:hi]
            dblk, fq = divmod(k, 4)
            return W2.ap()[dblk * 128:(dblk + 1) * 128,
                           fq * 1024 + lo:fq * 1024 + hi]

        a20 = ctr["a2"]; dv0 = ctr["dv"]; te0 = ctr["te"]
        E_sy_x = lambda tc: 16 * (tc // 2 + 1)
        E_a2_sq = lambda k: (a20 + 1) if k == 0 else (a20 + 5 * k - 3)
        E_a2_rstd = lambda tc: a20 + 5 * tc + 3
        E_a2_xnb = lambda tc: a20 + 5 * tc + 4
        E_a2_rstd2 = lambda tc: a20 + 5 * tc + 5
        E_a2_evT = lambda k: (a20 + 5 * k + 11) if k < M - 1 else (a20 + 5 * M + 2)
        E_dv_var = lambda tc: (dv0 + 1) if tc == 0 else (dv0 + 4 * tc - 2)
        E_dv_apply = lambda tc: dv0 + 4 * tc + 3
        E_dv_y2 = lambda tc: dv0 + 4 * tc + 4
        E_dv_xn2 = lambda tc: dv0 + 4 * tc + 5
        E_te_tr = lambda tc: te0 + 2 * tc + 1
        E_te_q = lambda tc: te0 + 2 * tc + 2
        MUc = lambda t: TY_MU if t % 2 == 0 else TY_MU2
        VARc = lambda t: TY_VAR if t % 2 == 0 else TY_VAR2
        RSTDc = lambda t: TY_RSTD if t % 2 == 0 else TY_RSTDB

        with nc.Block() as blk:
            @blk.sync
            def _(sy):
                for tc in range(M):
                    if tc >= 2:
                        sy.wait_ge(s_dv, E_dv_apply(tc - 2))
                    sdma(sy, f"x{tc % 2}", xt[tc % 2][:, :],
                         x_in.ap()[tc * 128:(tc + 1) * 128, :])

            @blk.scalar
            def _(ac):
                swt(ac, "x0", E_sy_x(0))
                bump(ac.activation(xsq[:, :], xt[0][:, :], AF.Square), "a2")
                for tc in range(M):
                    if tc + 1 < M:
                        t1 = tc + 1
                        swt(ac, f"x{t1 % 2}", E_sy_x(t1))
                        ac.wait_ge(s_dv, E_dv_xn2(tc - 1) if tc >= 1
                                   else E_dv_var(0))
                        bump(ac.activation(xsq[:, :], xt[t1 % 2][:, :], AF.Square),
                             "a2")
                    else:
                        bump(ac.activation(tiny[:, TY_MR2:TY_MR2 + 1], csl(C_1),
                                           AF.Copy, bias=0.0, scale=1.0), "a2")
                    ac.wait_ge(s_dv, E_dv_var(tc))
                    bump(ac.activation(tiny[:, RSTDc(tc):RSTDc(tc) + 1],
                                       tiny[:, VARc(tc):VARc(tc) + 1], AF.Sqrt,
                                       bias=csl(C_EPS), scale=csl(C_1)), "a2")
                    ac.wait_ge(s_dv, E_dv_apply(tc))
                    if tc >= 2:
                        ac.wait_ge(s_te, E_te_tr(tc - 2))
                    bump(ac.activation(xn_b[tc % 2][:, :], xn_f[:, :], AF.Copy,
                                       bias=0.0, scale=1.0), "a2")
                    ac.wait_ge(s_dv, E_dv_y2(tc))
                    bump(ac.activation(tiny[:, TY_RSTD2:TY_RSTD2 + 1],
                                       tiny[:, TY_Y2:TY_Y2 + 1], AF.Sqrt,
                                       bias=csl(C_EPS), scale=csl(C_1)), "a2")
                    if tc >= 1:
                        ac.wait_ge(s_te, E_te_tr(tc - 1))
                        bump(ac.activation(xnT3[:, :, (tc - 1) * 128:tc * 128], ps_d3,
                                           AF.Copy, bias=0.0, scale=1.0), "a2")
                    else:
                        bump(ac.activation(tiny[:, TY_MR2:TY_MR2 + 1], csl(C_1),
                                           AF.Copy, bias=0.0, scale=1.0), "a2")
                    if debug_taps:
                        sdma(ac, "tap", taps["tap_xn"].ap()[tc * 128:(tc + 1) * 128, :],
                             xn_f[:, :])
                ac.wait_ge(s_te, E_te_tr(M - 1))
                bump(ac.activation(xnT3[:, :, (M - 1) * 128:M * 128], ps_d3,
                                   AF.Copy, bias=0.0, scale=1.0), "a2")
                assert ctr["a2"] == a20 + 5 * M + 2

            @blk.vector
            def _(dv):
                dv = _DR(dv)
                swt(dv, "ld", b0_ld)
                swt(dv, "cst", b0b_cst)

                def stageA(t):
                    p2 = t % 2
                    swt(dv, f"x{p2}", E_sy_x(t))
                    dv.tensor_reduce(tiny[:, TY_SX:TY_SX + 1], xt[p2][:, :],
                                     AX.X, ALU.add)
                    dv.wait_ge(s_a2, E_a2_sq(t))
                    dv.tensor_reduce(tiny[:, TY_SXX:TY_SXX + 1], xsq[:, :],
                                     AX.X, ALU.add)
                    dv.tensor_scalar_mul(tiny[:, MUc(t):MUc(t) + 1],
                                         tiny[:, TY_SX:TY_SX + 1], csl(C_INVD))
                    dv.tensor_scalar_mul(tiny[:, TY_EX2:TY_EX2 + 1],
                                         tiny[:, TY_SXX:TY_SXX + 1], csl(C_INVD))
                    dv.tensor_mul(tiny[:, TY_MSQ:TY_MSQ + 1],
                                  tiny[:, MUc(t):MUc(t) + 1],
                                  tiny[:, MUc(t):MUc(t) + 1])
                    bump(dv.tensor_sub(tiny[:, VARc(t):VARc(t) + 1],
                                       tiny[:, TY_EX2:TY_EX2 + 1],
                                       tiny[:, TY_MSQ:TY_MSQ + 1]), "dv")

                def stageB(t):
                    p2 = t % 2
                    dv.wait_ge(s_a2, E_a2_rstd(t))
                    dv.reciprocal(tiny[:, RSTDc(t):RSTDc(t) + 1],
                                  tiny[:, RSTDc(t):RSTDc(t) + 1])
                    if debug_taps and t >= 1:
                        swt(dv, "tap", tap_b1 + 16 * t)
                    bump(dv.tensor_scalar(xn_f[:, :], xt[p2][:, :],
                                          tiny[:, MUc(t):MUc(t) + 1],
                                          tiny[:, RSTDc(t):RSTDc(t) + 1],
                                          ALU.subtract, ALU.mult), "dv")
                    dv.tensor_mul(tiny[:, TY_T1:TY_T1 + 1],
                                  tiny[:, VARc(t):VARc(t) + 1],
                                  tiny[:, RSTDc(t):RSTDc(t) + 1])
                    bump(dv.tensor_mul(tiny[:, TY_Y2:TY_Y2 + 1],
                                       tiny[:, TY_T1:TY_T1 + 1],
                                       tiny[:, RSTDc(t):RSTDc(t) + 1]), "dv")
                    dv.tensor_scalar_add(stat_a[:, t:t + 1],
                                         tiny[:, RSTDc(t):RSTDc(t) + 1], csl(C_1))
                    dv.tensor_scalar(stat_b[:, t:t + 1], tiny[:, MUc(t):MUc(t) + 1],
                                     tiny[:, RSTDc(t):RSTDc(t) + 1], csl(C_N1),
                                     ALU.mult, ALU.mult)
                    dv.wait_ge(s_a2, E_a2_rstd2(t))
                    dv.reciprocal(tiny[:, TY_RSTD2:TY_RSTD2 + 1],
                                  tiny[:, TY_RSTD2:TY_RSTD2 + 1])
                    dv.scalar_tensor_tensor(xsq[:, :], xn_f[:, :],
                                            tiny[:, TY_RSTD2:TY_RSTD2 + 1], gbc[:, :],
                                            ALU.mult, ALU.mult)
                    bump(dv.tensor_add(xn2sb[:, t * 1024:(t + 1) * 1024],
                                       xsq[:, :], bbc[:, :]), "dv")

                stageA(0)
                for tc in range(M):
                    if tc + 1 < M:
                        stageA(tc + 1)
                    else:
                        bump(dv.tensor_copy(tiny[:, TY_MR:TY_MR + 1], csl(C_1)), "dv")
                    stageB(tc)
                assert ctr["dv"] == dv0 + 4 * M + 1

            @blk.tensor
            def _(te):
                swt(te, "cst", b0b_cst)
                for tc in range(M):
                    p = tc % 2
                    te.wait_ge(s_a2, E_a2_xnb(tc))
                    if tc >= 1:
                        te.wait_ge(s_a2, E_a2_evT(tc - 1))
                    for dblk in range(8):
                        inst = te.transpose(ps_d[:, dblk * 128:(dblk + 1) * 128],
                                            xn_b[p][:, dblk * 128:(dblk + 1) * 128],
                                            cID[:, :])
                    bump(inst, "te")
                    te.wait_ge(s_dv, E_dv_xn2(tc))
                    te.matmul(ps_c[:, 0:512], cQW[:, tc * 128:(tc + 1) * 128],
                              xn2sb[:, tc * 1024:tc * 1024 + 512],
                              start=(tc == 0), stop=(tc == M - 1))
                    bump(te.matmul(ps_c[:, 512:1024], cQW[:, tc * 128:(tc + 1) * 128],
                                   xn2sb[:, tc * 1024 + 512:(tc + 1) * 1024],
                                   start=(tc == 0), stop=(tc == M - 1)), "te")
                assert ctr["te"] == te0 + 2 * M

        if stop_after == "B1":
            _early_out([stA, stX])
            return nc

        with nc.Block() as blk:
            @blk.vector
            def _(dv):
                dv = _DR(dv)
                dv.tensor_copy(q64[:, :], ps_c[:, :])
                dv.memset(zrep[:, :], 0.0)
                dv.memset(mstk1[0][:, :], 0.0)
                bump(dv.memset(mstk1[1][:, :], 0.0), "dv")
            b2_dv1 = ctr["dv"]

            @blk.tensor
            def _(te):
                te.wait_ge(s_dv, b2_dv1)
                for par, tgt in ((0, ps_e), (1, ps_c)):
                    te.matmul(tgt[:, 0:512], cTC[:, par * 128:(par + 1) * 128],
                              q64[:, 0:512], start=True, stop=True)
                    inst = te.matmul(tgt[:, 512:1024], cTC[:, par * 128:(par + 1) * 128],
                                     q64[:, 512:1024], start=True, stop=True)
                bump(inst, "te")
            b2_te1 = ctr["te"]

            dv_b2 = ctr["dv"]
            E_dv_v = lambda kap: dv_b2 + kap + 1
            E_gp_mk = lambda kap: 64 * (kap + 1)

            @blk.gpsimd
            def _(gp):
                swt(gp, "ld", b0b_ld)
                gp.wait_ge(s_dv, b2_dv1)
                for kap in range(NK):
                    if kap >= 2:
                        gp.wait_ge(s_dv, E_dv_v(kap - 2))
                    for e2 in range(DA + 1):
                        sdma(gp, "mk", mstk1[kap % 2][e2 * 32:e2 * 32 + 16, :],
                             bass.AP(scr_mrow, (kap + e2) * D, [[0, 16], [1, D]]))

            @blk.vector
            def _(dv):
                dv = _DR(dv)
                dv.wait_ge(s_te, b2_te1)
                for kap in range(NK):
                    src_ps = ps_e if kap % 2 == 0 else ps_c
                    base = (kap // 2) * 32
                    swt(dv, "mk", E_gp_mk(kap))
                    dv.tensor_copy(zrep[0:16, :], src_ps[base:base + 16, :])
                    for e2 in range(1, 4):
                        dv.tensor_copy(zrep[e2 * 32:e2 * 32 + 16, :], zrep[0:16, :])
                    if kap == 0:
                        bump(dv.tensor_mul(Vt[:, :], zrep[:, :], mstk1[0][:, :]), "dv")
                    else:
                        dv.tensor_mul(xn_f[:, :], zrep[:, :], mstk1[kap % 2][:, :])
                        bump(dv.tensor_add(Vt[:, :], Vt[:, :], xn_f[:, :]), "dv")
                assert ctr["dv"] == E_dv_v(NK - 1)
            b2_dv2 = ctr["dv"]

            if debug_taps:
                @blk.gpsimd
                def _(gp):
                    gp.wait_ge(s_dv, b2_dv2)
                    sdma(gp, "tapg", taps["tap_q"].ap(), q64[:, :])
                    sdma(gp, "tapg", taps["tap_V"].ap(), Vt[:, :])

        if stop_after == "B2":
            _early_out([stA, stX])
            return nc

        stA.close()
        enB = stB.enter_context
        u_b = [enB(nc.sbuf_tensor(f"ub_{k}", [128, D], BF16))
               for k in range(DA + 2)]
        y_b = enB(nc.sbuf_tensor("y_b", [128, D], BF16))
        cP = enB(nc.sbuf_tensor("sb_cP", [128, (DA + 1) * 128], BF16))
        cWE = enB(nc.sbuf_tensor("sb_cWE", [128, M * 128], BF16))
        mtb = enB(nc.sbuf_tensor("mtb", [128, (DA + 1) * D], BF16))
        dbc = enB(nc.sbuf_tensor("dbc", [128, D], BF16))
        wabs = [enB(nc.sbuf_tensor(f"wabs{i}", [128, 512], F32)) for i in range(4)]
        wacc = enB(nc.sbuf_tensor("wacc", [1, 512], F32))
        scal1 = enB(nc.sbuf_tensor("scal1", [1, 8], F32))

        dv0 = ctr["dv"]; te0 = ctr["te"]
        E_dv_uh = lambda c, h: dv0 + 4 * c + 1 + h
        E_dv_yev = lambda k: (dv0 + 4 * (k + 1) + 3) if k < M - 1 else (dv0 + 4 * M + 1)

        def E_dv_ytr(k):
            if k <= M - 3:
                return dv0 + 4 * (k + 2) + 4
            return dv0 + 4 * M + 2 if k == M - 2 else dv0 + 4 * M + 3
        E_te_ymm = lambda c, h: te0 + 3 * c + 1 + h
        E_te_ytr = lambda k: (te0 + 3 * (k + 1) + 3) if k < M - 1 else (te0 + 3 * M + 1)
        tapg_b3 = streams["tapg"]["n"]

        with nc.Block() as blk:
            @blk.gpsimd
            def _(gp):
                swt(gp, "ld", b0b_ld)
                if debug_taps:
                    swt(gp, "tap", streams["tap"]["n"])
                sdma(gp, "b3l", cP[:, :], cP_in.ap())
                sdma(gp, "b3l", cWE[:, :], cWE_in.ap())
                sdma(gp, "b3l", dbc[:, :], bc_ap(Dp))
                for k in range(DA + 1):
                    sdma(gp, "b3l", mtb[:, k * D:(k + 1) * D],
                         bass.AP(scr_mrow, k * D, [[0, 128], [1, D]]))
                for wk in range(64):
                    if wk >= 2:
                        gp.wait_ge(s_wa, 2 * wk - 2)
                    if debug_taps:
                        sdma(gp, f"wp{wk % 2}", wchunk[wk % 2][:, :],
                             wp1_ap(*wp1[wk]))
                    else:
                        sdma(gp, f"wp{wk % 2}", wchunk[wk % 2][:, 512:1024],
                             wp1_ap(*wp1[wk], half=1))
            b3_l = streams["b3l"]["n"]

            if not debug_taps:
                @blk.sync
                def _(sy):
                    for wk in range(64):
                        if wk >= 2:
                            sy.wait_ge(s_wa, 2 * wk - 2)
                        sdma(sy, f"wq{wk % 2}", wchunk[wk % 2][:, 0:512],
                             wp1_ap(*wp1[wk], half=0))

            @blk.scalar
            def _(ac):
                for j in range(128):
                    wk, h = divmod(j, 2)
                    if h == 1 or debug_taps:
                        swt(ac, f"wp{wk % 2}", 16 * (wk // 2 + 1))
                    else:
                        swt(ac, f"wq{wk % 2}", 16 * (wk // 2 + 1))
                    if j >= 4:
                        ac.wait_ge(s_wm, j - 3)
                    ac.activation(wabs[j % 4][:, :],
                                  wchunk[wk % 2][:, h * 512:(h + 1) * 512],
                                  AF.Abs).then_inc(s_wa, 1)

            @blk.vector
            def _(dv):
                dv = _DR(dv)
                swt(dv, "b3l", b3_l)
                for c in range(M):
                    p = c % 2
                    x2sl3 = xn2sb.ap().rearrange("p (c2 h f) -> p c2 h f",
                                                 c2=M, h=2, f=512)
                    for h in range(2):
                        if c >= 1:
                            dv.wait_ge(s_te, E_te_ymm(c - 1, h))
                        hs = slice(h * 512, (h + 1) * 512)
                        for k in range(DA + 1):
                            dv._e.tensor_mul(u_b[k][:, hs],
                                             xn2sb[:, c * 1024 + h * 512:
                                                   c * 1024 + (h + 1) * 512],
                                             mtb[:, k * D + h * 512:
                                                 k * D + (h + 1) * 512])
                        bump(dv._e.tensor_mul(u_b[DA + 1][:, hs],
                                              xn2sb[:, c * 1024 + h * 512:
                                                    c * 1024 + (h + 1) * 512],
                                              dbc[:, hs]), "dv")
                    if c >= 1:
                        kk = c - 1
                        dv.wait_ge(s_te, E_te_ymm(kk, 1))
                        if kk >= 1:
                            dv.wait_ge(s_te, E_te_ytr(kk - 1))
                        if debug_taps and kk >= 1:
                            swt(dv, f"tg{(kk - 1) % 2}", 16 * ((kk - 1) // 2 + 1))
                        psy = ps_c if kk % 2 == 0 else ps_e
                        bump(dv._e.tensor_copy(y_b[:, :], psy[:, :]), "dv")
                    else:
                        bump(dv.tensor_copy(tiny[:, TY_MR:TY_MR + 1], csl(C_1)), "dv")
                    if c >= 2:
                        kk = c - 2
                        dv.wait_ge(s_te, E_te_ytr(kk))
                        bump(dv._e.tensor_copy(yT3[:, :, kk * 128:(kk + 1) * 128],
                                               ps_d3), "dv")
                    else:
                        bump(dv.tensor_copy(tiny[:, TY_MR:TY_MR + 1], csl(C_1)), "dv")
                    if c % 2 == 1:
                        g = (c - 1) // 2
                        which, gl = (0, g) if g < 4 else (1, g - 4)
                        srcp = ps_a if which == 0 else ps_b
                        dv.wait_ge(s_wm, 8 * c + 8)
                        if gl == 0:
                            dv.tensor_copy(wacc[0:1, :],
                                           srcp[0:1, 0:512]).then_inc(s_wv, 1)
                        else:
                            dv.tensor_add(wacc[0:1, :], wacc[0:1, :],
                                          srcp[0:1, 0:512]).then_inc(s_wv, 1)
                        if gl == 3:
                            n = 256
                            while n >= 1:
                                dv.tensor_add(wacc[0:1, 0:n], wacc[0:1, 0:n],
                                              wacc[0:1, n:2 * n])
                                n //= 2
                            dv.tensor_copy(scal1[0:1, which:which + 1],
                                           wacc[0:1, 0:1])
                dv.wait_ge(s_te, E_te_ymm(M - 1, 1))
                dv.wait_ge(s_te, E_te_ytr(M - 2))
                if debug_taps:
                    swt(dv, f"tg{(M - 2) % 2}", 16 * ((M - 2) // 2 + 1))
                psy = ps_c if (M - 1) % 2 == 0 else ps_e
                bump(dv.tensor_copy(y_b[:, :], psy[:, :]), "dv")
                bump(dv.tensor_copy(yT3[:, :, (M - 2) * 128:(M - 1) * 128], ps_d3), "dv")
                dv.wait_ge(s_te, E_te_ytr(M - 1))
                bump(dv.tensor_copy(yT3[:, :, (M - 1) * 128:M * 128], ps_d3), "dv")
                assert ctr["dv"] == dv0 + 4 * M + 3

            @blk.tensor
            def _(te):
                def wmm(j):
                    te.wait_ge(s_wa, j + 1)
                    which = 0 if j < 64 else 1
                    tgt = ps_a if which == 0 else ps_b
                    gl = (j % 64) // 16
                    if j % 16 == 0 and gl >= 1:
                        te.wait_ge(s_wv, which * 4 + gl)
                    te.matmul(tgt[0:1, 0:512], onesc[:, 0:1], wabs[j % 4][:, :],
                              start=(j % 16 == 0),
                              stop=(j % 16 == 15)).then_inc(s_wm, 1)

                for c in range(M):
                    p = c % 2
                    psy = ps_c if c % 2 == 0 else ps_e
                    for half in range(2):
                        te.wait_ge(s_dv, E_dv_uh(c, half))
                        if half == 0 and c >= 2:
                            te.wait_ge(s_dv, E_dv_yev(c - 2))
                        sl = slice(half * 512, (half + 1) * 512)
                        for k in range(DA + 1):
                            te.matmul(psy[:, sl], cP[:, k * 128:(k + 1) * 128],
                                      u_b[k][:, sl], start=(k == 0), stop=False)
                        te.matmul(psy[:, sl], cID[:, :], u_b[DA + 1][:, sl],
                                  start=False, stop=False)
                        bump(te.matmul(psy[:, sl], cWE[:, c * 128:(c + 1) * 128],
                                       Vt[:, sl], start=False, stop=True), "te")
                        for j in range(8 * c + 4 * half, 8 * c + 4 * half + 4):
                            wmm(j)
                    if c >= 1:
                        kk = c - 1
                        te.wait_ge(s_dv, E_dv_yev(kk))
                        if c >= 2:
                            te.wait_ge(s_dv, E_dv_ytr(kk - 1))
                        for dblk in range(8):
                            inst = te.transpose(ps_d[:, dblk * 128:(dblk + 1) * 128],
                                                y_b[:, dblk * 128:(dblk + 1) * 128],
                                                cID[:, :])
                        bump(inst, "te")
                    else:
                        bump(te.transpose(ps_d[:, 0:128], cID[:, :], cID[:, :]), "te")
                te.wait_ge(s_dv, E_dv_yev(M - 1))
                te.wait_ge(s_dv, E_dv_ytr(M - 2))
                for dblk in range(8):
                    inst = te.transpose(ps_d[:, dblk * 128:(dblk + 1) * 128],
                                        y_b[:, dblk * 128:(dblk + 1) * 128],
                                        cID[:, :])
                bump(inst, "te")
                assert ctr["te"] == te0 + 3 * M + 1

            if debug_taps:
                @blk.sync
                def _(sy):
                    for kk in range(M):
                        sy.wait_ge(s_dv, E_dv_yev(kk))
                        sdma(sy, f"tg{kk % 2}", taps["tap_y"].ap()[kk * 128:(kk + 1) * 128, :],
                             y_b[:, :])

        if stop_after == "B3":
            _early_out([stB, stX])
            return nc

        with nc.Block() as blk:
            @blk.vector
            def _(dv):
                dv = _DR(dv)
                dv.tensor_scalar(scal1[0:1, 2:4], scal1[0:1, 0:2],
                                 cst[0:1, C_INVW:C_INVW + 1], cst[0:1, C_W8:C_W8 + 1],
                                 ALU.mult, ALU.max)
                bump(dv.reciprocal(invs[0:1, 0:2], scal1[0:1, 2:4]), "dv")
            b3b_dv = ctr["dv"]

            @blk.sync
            def _(sy):
                sy.wait_ge(s_dv, b3b_dv)
                n = 1
                while n < 128:
                    m2 = min(n, 128 - n)
                    sdma(sy, "iv", invs[n:n + m2, :], invs[0:m2, :])
                    swt(sy, "iv")
                    n += m2
            b3b_iv = streams["iv"]["n"]

            if debug_taps:
                @blk.gpsimd
                def _(gp):
                    swt(gp, "iv", b3b_iv)
                    sdma(gp, "tapg", taps["tap_invs"].ap(), invs[:, :])

        stB.close()
        enC = stC.enter_context
        w1st = [enC(nc.sbuf_tensor(f"w1st{i}", [128, 1024], BF16)) for i in range(3)]
        hst = [enC(nc.sbuf_tensor(f"hst{i}", [128, T], BF16)) for i in range(2)]
        wsgA = enC(nc.sbuf_tensor("wsgA", [128, 1024], BF16))
        wsgB = enC(nc.sbuf_tensor("wsgB", [128, 1024], BF16))
        w1tn = enC(nc.sbuf_tensor("w1tn", [128, 1024], BF16))
        w2tn = [enC(nc.sbuf_tensor(f"w2tn{i}", [128, 1024], BF16)) for i in range(2)]
        wchC = [enC(nc.sbuf_tensor(f"wchC{i}", [128, 1024], F32)) for i in range(2)]

        a20 = ctr["a2"]; dv0 = ctr["dv"]; te0 = ctr["te"]
        nA, nD, nT = a20, dv0, te0
        A_s1 = {}; A_g = {}; A_s2 = {}
        D_a1 = {}; D_e1 = {}; D_a2 = {}; D_e2 = {}
        T_t1 = {}; T_h = {}; T_t2 = {}
        nA += 2; A_s1[0] = nA
        nA += 2; A_s1[1] = nA
        nD += 1; D_a1[0] = nD
        nD += 1; D_e1[0] = nD
        nD += 1; D_a1[1] = nD
        nD += 1; D_e1[1] = nD
        nT += 1; T_t1[0] = nT
        nT += 1; T_t1[1] = nT
        for fc in range(32):
            if fc <= 29:
                nA += 2; A_s1[fc + 2] = nA
            for t4 in range(4):
                nA += 1; A_g[(fc, t4)] = nA
            nA += 2; A_s2[fc] = nA
            if fc <= 29:
                nD += 1; D_a1[fc + 2] = nD
                nD += 1; D_e1[fc + 2] = nD
            nD += 1; D_a2[fc] = nD
            if fc >= 1:
                nD += 1; D_e2[fc - 1] = nD
            for t4 in range(4):
                nT += 1; T_h[(fc, t4)] = nT
            if fc <= 29:
                nT += 1; T_t1[fc + 2] = nT
            if fc >= 1:
                nT += 1; T_t2[fc - 1] = nT
        nT += 1; T_t2[31] = nT
        nD += 1; D_e2[31] = nD
        E_ac_hw = lambda fc: 16 * (fc // 2 + 1)

        with nc.Block() as blk:
            @blk.gpsimd
            def _(gp):
                sdma(gp, "w1l0", wchunk[0][:, :], wp1_ap("W1", 0))
                sdma(gp, "w2l0", wchC[0][:, :], wp1_ap("W2", 0))
                sdma(gp, "w1l0", wchunk[1][:, :], wp1_ap("W1", 1))
                for fc in range(32):
                    if fc <= 29:
                        k = fc + 2
                        gp.wait_ge(s_a2, A_s1[fc])
                        sdma(gp, "w1l0", wchunk[k % 2][:, :], wp1_ap("W1", k))
                    if fc <= 30:
                        k = fc + 1
                        if fc >= 1:
                            gp.wait_ge(s_a2, A_s2[fc - 1])
                        sdma(gp, "w2l0", wchC[k % 2][:, :], wp1_ap("W2", k))

            @blk.scalar
            def _(ac):
                swt(ac, "iv", b3b_iv)
                for k in range(2):
                    swt(ac, "w1l0", 16 * (k + 1))
                    if k == 1:
                        ac.wait_ge(s_dv, D_a1[0])
                    bump(ac.activation(wsgA[:, :], wchunk[k][:, :], AF.Sign,
                                       bias=csl(C_NHALF), scale=invs[:, 0:1]), "a2")
                    bump(ac.activation(wsgB[:, :], wchunk[k][:, :], AF.Sign,
                                       bias=csl(C_PHALF), scale=invs[:, 0:1]), "a2")
                for fc in range(32):
                    p = fc % 2
                    if fc <= 29:
                        k = fc + 2
                        swt(ac, "w1l0", 16 * (k + 1))
                        ac.wait_ge(s_dv, D_a2[fc - 1] if fc >= 1 else D_a1[1])
                        bump(ac.activation(wsgA[:, :], wchunk[k % 2][:, :], AF.Sign,
                                           bias=csl(C_NHALF), scale=invs[:, 0:1]),
                             "a2")
                        bump(ac.activation(wsgB[:, :], wchunk[k % 2][:, :], AF.Sign,
                                           bias=csl(C_PHALF), scale=invs[:, 0:1]),
                             "a2")
                    for t4 in range(4):
                        psh = (ps_a[:, :], ps_b[:, :], ps_c[:, 0:512],
                               ps_e[:, 0:512])[t4]
                        ac.wait_ge(s_te, T_h[(fc, t4)])
                        if fc >= 2 and t4 == 0:
                            swt(ac, f"hw{fc % 2}", E_ac_hw(fc - 2))
                        bump(ac.activation(hst[p][:, t4 * 512:(t4 + 1) * 512],
                                           psh, AF.Gelu,
                                           bias=b1sb[:, fc:fc + 1],
                                           scale=csl(C_HALF)), "a2")
                    swt(ac, "w2l0", 16 * (fc + 1))
                    ac.wait_ge(s_dv, D_a1[fc + 2] if fc <= 29 else D_a1[31])
                    bump(ac.activation(wsgA[:, :], wchC[fc % 2][:, :], AF.Sign,
                                       bias=csl(C_NHALF), scale=invs[:, 1:2]), "a2")
                    bump(ac.activation(wsgB[:, :], wchC[fc % 2][:, :], AF.Sign,
                                       bias=csl(C_PHALF), scale=invs[:, 1:2]), "a2")
                    sdma(ac, f"hw{fc % 2}", scr_h.ap()[fc, :, :], hst[p][:, :])
                assert ctr["a2"] == nA

            @blk.vector
            def _(dv):
                for k in range(2):
                    dv.wait_ge(s_a2, A_s1[k])
                    if k == 1:
                        dv.wait_ge(s_te, T_t1[0])
                    bump(dv.tensor_add(w1tn[:, :], wsgA[:, :], wsgB[:, :]), "dv")
                    dv.wait_ge(s_te, T_t1[k])
                    bump(dv.tensor_copy(
                        w1st[k].ap().rearrange("p (b f) -> p b f", f=128), ps_d3),
                        "dv")
                for fc in range(32):
                    if fc <= 29:
                        k = fc + 2
                        dv.wait_ge(s_a2, A_s1[k])
                        dv.wait_ge(s_te, T_t1[k - 1])
                        bump(dv.tensor_add(w1tn[:, :], wsgA[:, :], wsgB[:, :]), "dv")
                        dv.wait_ge(s_te, T_t1[k])
                        if fc >= 1:
                            dv.wait_ge(s_te, T_h[(fc - 1, 3)])
                        bump(dv.tensor_copy(
                            w1st[k % 3].ap().rearrange("p (b f) -> p b f", f=128),
                            ps_d3), "dv")
                    dv.wait_ge(s_a2, A_s2[fc])
                    if fc >= 2:
                        dv.wait_ge(s_te, T_t2[fc - 2])
                    bump(dv.tensor_add(w2tn[fc % 2][:, :], wsgA[:, :], wsgB[:, :]),
                         "dv")
                    if fc >= 1:
                        u = fc - 1
                        dblk, fq = divmod(u, 4)
                        dv.wait_ge(s_te, T_t2[u])
                        bump(dv.tensor_copy(
                            W2T3[:, fq * 8:(fq + 1) * 8, dblk * 128:(dblk + 1) * 128],
                            ps_d3), "dv")
                u = 31
                dblk, fq = divmod(u, 4)
                dv.wait_ge(s_te, T_t2[u])
                bump(dv.tensor_copy(
                    W2T3[:, fq * 8:(fq + 1) * 8, dblk * 128:(dblk + 1) * 128],
                    ps_d3), "dv")
                assert ctr["dv"] == nD

            @blk.tensor
            def _(te):
                for k in range(2):
                    te.wait_ge(s_dv, D_a1[k])
                    if k == 1:
                        te.wait_ge(s_dv, D_e1[0])
                    for dblk in range(8):
                        inst = te.transpose(ps_d[:, dblk * 128:(dblk + 1) * 128],
                                            w1tn[:, dblk * 128:(dblk + 1) * 128],
                                            cID[:, :])
                    bump(inst, "te")
                for fc in range(32):
                    te.wait_ge(s_dv, D_e1[fc])
                    for t4 in range(4):
                        psh = (ps_a[:, :], ps_b[:, :], ps_c[:, 0:512],
                               ps_e[:, 0:512])[t4]
                        gi = fc * 4 + t4
                        if gi >= 4:
                            te.wait_ge(s_a2, A_g[((gi - 4) // 4, (gi - 4) % 4)])
                        for dblk in range(8):
                            inst = te.matmul(psh,
                                             w1st[fc % 3][:, dblk * 128:(dblk + 1) * 128],
                                             xnT[:, dblk * T + t4 * 512:
                                                 dblk * T + (t4 + 1) * 512],
                                             start=(dblk == 0), stop=(dblk == 7))
                        bump(inst, "te")
                    if fc <= 29:
                        k = fc + 2
                        te.wait_ge(s_dv, D_a1[k])
                        te.wait_ge(s_dv, D_e1[k - 1] if k <= 3 else D_e2[k - 4])
                        for dblk in range(8):
                            inst = te.transpose(ps_d[:, dblk * 128:(dblk + 1) * 128],
                                                w1tn[:, dblk * 128:(dblk + 1) * 128],
                                                cID[:, :])
                        bump(inst, "te")
                    if fc >= 1:
                        u = fc - 1
                        te.wait_ge(s_dv, D_a2[u])
                        te.wait_ge(s_dv, D_e1[u + 3] if u + 3 <= 31 else D_e2[u - 1])
                        for dblk in range(8):
                            inst = te.transpose(ps_d[:, dblk * 128:(dblk + 1) * 128],
                                                w2tn[u % 2][:, dblk * 128:(dblk + 1) * 128],
                                                cID[:, :])
                        bump(inst, "te")
                u = 31
                te.wait_ge(s_dv, D_a2[u])
                te.wait_ge(s_dv, D_e2[30])
                for dblk in range(8):
                    inst = te.transpose(ps_d[:, dblk * 128:(dblk + 1) * 128],
                                        w2tn[u % 2][:, dblk * 128:(dblk + 1) * 128],
                                        cID[:, :])
                bump(inst, "te")
                assert ctr["te"] == nT

        if stop_after == "B5":
            _early_out([stC, stX])
            return nc

        stC.close()
        stX.close()
        enD = stD.enter_context
        WoT = enD(nc.sbuf_tensor("WoT", [128, 8 * 1024], BF16))
        WoT3 = WoT.ap().rearrange("p (b c) -> p b c", c=1024)
        hgrp = [enD(nc.sbuf_tensor(f"hgrp{i}", [128, 32 * 256], BF16)) for i in range(2)]
        xr = [enD(nc.sbuf_tensor(f"xr{i}", [128, D], F32)) for i in range(2)]
        osb = [enD(nc.sbuf_tensor(f"osb{i}", [128, D], F32)) for i in range(2)]
        t1f = enD(nc.sbuf_tensor("t1f", [128, D], F32))
        t2f = enD(nc.sbuf_tensor("t2f", [128, D], F32))
        bias1bc = enD(nc.sbuf_tensor("bias1bc", [128, D], F32))

        a20 = ctr["a2"]; te0 = ctr["te"]; dv0 = ctr["dv"]
        E_a2_wo = lambda k: a20 + k + 1
        E_te_wo = lambda k: te0 + k + 1
        E_dv_wo = lambda k: dv0 + k + 1
        b6w_base = [streams["w1c0"]["n"], streams["w1c1"]["n"]]
        with nc.Block() as blk:
            @blk.gpsimd
            def _(gp):
                swt(gp, "ld", b0_ld2)
                sdma(gp, "cst", bias1bc[:, :], bass.AP(scr_rows, 0, [[0, 128], [1, D]]))
            b6_cst = streams["cst"]["n"]

            @blk.sync
            def _(sy):
                for eb in range(8):
                    if eb >= 2:
                        sy.wait_ge(s_a2, E_a2_wo(eb - 2))
                    sdma(sy, f"w1c{eb % 2}", wchunk[eb % 2][:, :],
                         Wout.ap()[eb * 128:(eb + 1) * 128, :])

            @blk.scalar
            def _(ac):
                for eb in range(8):
                    swt(ac, f"w1c{eb % 2}", b6w_base[eb % 2] + 16 * (eb // 2 + 1))
                    if eb >= 2:
                        ac.wait_ge(s_te, E_te_wo(eb - 2))
                    bump(ac.activation(hgrp[eb % 2][:, 0:1024], wchunk[eb % 2][:, :],
                                       AF.Copy, bias=0.0, scale=2.0), "a2")

            @blk.tensor
            def _(te):
                for eb in range(8):
                    te.wait_ge(s_a2, E_a2_wo(eb))
                    if eb >= 1:
                        te.wait_ge(s_dv, E_dv_wo(eb - 1))
                    for dblk in range(8):
                        inst = te.transpose(ps_d[:, dblk * 128:(dblk + 1) * 128],
                                            hgrp[eb % 2][:, dblk * 128:(dblk + 1) * 128],
                                            cID[:, :])
                    bump(inst, "te")

            @blk.vector
            def _(dv):
                dv = _DR(dv)
                for eb in range(8):
                    dv.wait_ge(s_te, E_te_wo(eb))
                    bump(dv.tensor_copy(WoT3[:, :, eb * 128:(eb + 1) * 128], ps_d3), "dv")

        te0 = ctr["te"]; dv0 = ctr["dv"]
        E_te_o = lambda tg, tk: te0 + 2 * tg + tk + 1
        E_dv_oe = lambda tg, tk: dv0 + 2 * tg + tk + 1
        E_gp_ow = lambda tg: 16 * (tg + 1)
        E_ac_xr = lambda tg: 16 * (tg + 1)
        E_hg = lambda tg: 16 * 16 * (tg // 2 + 1)
        b5_hw = [streams["hw0"]["n"], streams["hw1"]["n"]]
        with nc.Block() as blk:
            @blk.sync
            def _(sy):
                swt(sy, "hw0", b5_hw[0])
                swt(sy, "hw1", b5_hw[1])
                for tg in range(8):
                    if tg >= 2:
                        sy.wait_ge(s_te, E_te_o(tg - 2, 1))
                    for fc in range(0, 32, 2):
                        sdma(sy, f"hge{tg % 2}", hgrp[tg % 2][:, fc * 256:(fc + 1) * 256],
                             scr_h.ap()[fc, :, tg * 256:(tg + 1) * 256])

            @blk.scalar
            def _(ac):
                ac = _DR(ac)
                swt(ac, "hw0", b5_hw[0])
                swt(ac, "hw1", b5_hw[1])
                for tg in range(8):
                    if tg >= 2:
                        ac.wait_ge(s_te, E_te_o(tg - 2, 1))
                    for fc in range(1, 32, 2):
                        sdma(ac, f"hgo{tg % 2}", hgrp[tg % 2][:, fc * 256:(fc + 1) * 256],
                             scr_h.ap()[fc, :, tg * 256:(tg + 1) * 256])
                    for tk in range(2):
                        tci = tg * 2 + tk
                        if tg >= 1:
                            ac.wait_ge(s_dv, E_dv_oe(tg - 1, tk))
                        sdma(ac, f"xr{tk}", xr[tk][:, :],
                             x_in.ap()[tci * 128:(tci + 1) * 128, :])

            @blk.tensor
            def _(te):
                for tg in range(8):
                    for tk in range(2):
                        pso = ps_c if tk == 0 else ps_e
                        swt(te, f"hge{tg % 2}", E_hg(tg))
                        swt(te, f"hgo{tg % 2}", E_hg(tg))
                        if tg >= 1:
                            te.wait_ge(s_dv, E_dv_oe(tg - 1, tk))
                        for dh in range(2):
                            sl = slice(dh * 512, (dh + 1) * 512)
                            for fc in range(32):
                                te.matmul(pso[:, sl],
                                          hgrp[tg % 2][:, fc * 256 + tk * 128:
                                                       fc * 256 + (tk + 1) * 128],
                                          W2T[:, fc * 1024 + dh * 512:
                                              fc * 1024 + (dh + 1) * 512],
                                          start=(fc == 0), stop=False)
                            for dblk in range(8):
                                inst = te.matmul(
                                    pso[:, sl],
                                    yT[:, dblk * T + (tg * 2 + tk) * 128:
                                       dblk * T + (tg * 2 + tk + 1) * 128],
                                    WoT[:, dblk * 1024 + dh * 512:
                                        dblk * 1024 + (dh + 1) * 512],
                                    start=False, stop=(dblk == 7))
                        bump(inst, "te")

            @blk.vector
            def _(dv):
                dv = _DR(dv)
                swt(dv, "cst", b6_cst)
                for tg in range(8):
                    for tk in range(2):
                        pso = ps_c if tk == 0 else ps_e
                        tmp = t1f if tk == 0 else t2f
                        tci = tg * 2 + tk
                        swt(dv, f"xr{tk}", E_ac_xr(tg))
                        dv.wait_ge(s_te, E_te_o(tg, tk))
                        if tg >= 1:
                            swt(dv, f"ow{tk}", E_gp_ow(tg - 1))
                        dv.tensor_scalar(tmp[:, :], xr[tk][:, :],
                                         stat_a[:, tci:tci + 1], stat_b[:, tci:tci + 1],
                                         ALU.mult, ALU.add)
                        dv.tensor_add(tmp[:, :], tmp[:, :], bias1bc[:, :])
                        dv.scalar_tensor_tensor(osb[tk][:, 0:512], pso[:, 0:512],
                                                csl(C_HALF), tmp[:, 0:512],
                                                ALU.mult, ALU.add)
                        dv._e.drain()
                        bump(dv.scalar_tensor_tensor(osb[tk][:, 512:1024],
                                                     pso[:, 512:1024], csl(C_HALF),
                                                     tmp[:, 512:1024], ALU.mult, ALU.add),
                             "dv")

            @blk.gpsimd
            def _(gp):
                for tg in range(8):
                    for tk in range(2):
                        tci = tg * 2 + tk
                        gp.wait_ge(s_dv, E_dv_oe(tg, tk))
                        sdma(gp, f"ow{tk}", out_t.ap()[tci * 128:(tci + 1) * 128, :],
                             osb[tk][:, :])
                swt(gp, "ow0", streams["ow0"]["n"])
                swt(gp, "ow1", streams["ow1"]["n"])

        with nc.Block() as blk:
            @blk.sync
            def _(sy):
                for nm in ["ld", "x0", "x1", "iv", "x20", "x21", "w1w0", "w1w1",
                           "w1c0", "w1c1", "hge0", "hge1", "wq0", "wq1"]:
                    swt(sy, nm, streams[nm]["n"])

            @blk.scalar
            def _(ac):
                for nm in ["xn0", "xn1", "hw0", "hw1", "hgo0", "hgo1", "xr0", "xr1",
                           "tap"]:
                    swt(ac, nm, streams[nm]["n"])

            @blk.gpsimd
            def _(gp):
                for nm in ["cst", "mk", "b3l", "wp0", "wp1", "w1l0", "w1l1",
                           "w2l0", "w2l1", "tapg", "tg0", "tg1"]:
                    swt(gp, nm, streams[nm]["n"])

        stD.close()

    return nc


def kernel(**inputs):
    consts = _host_constants()
    nc = build_nc(debug_taps=_DEBUG_TAPS, stop_after=_STOP_AFTER)
    x = np.ascontiguousarray(np.asarray(inputs["L0_attn_x"], dtype=np.float32))
    shared = {k: np.ascontiguousarray(np.asarray(v, dtype=np.float32))
              for k, v in inputs.items() if k != "L0_attn_x"}
    shared.update(consts)
    in_maps = []
    for i in range(8):
        mm = dict(shared)
        mm["x"] = np.ascontiguousarray(x[i])
        in_maps.append(mm)
    res = run_bass_kernel_spmd(nc, in_maps, core_ids=list(range(8)))
    return np.stack([np.asarray(r["out"], dtype=np.float32) for r in res.results])


# version 12
# speedup vs baseline: 1.2721x; 1.0041x over previous
import math
import numpy as np
from contextlib import ExitStack

import concourse.bass as bass
import concourse.mybir as mybir
from concourse.bass_utils import run_bass_kernel_spmd

F32 = mybir.dt.float32
BF16 = mybir.dt.bfloat16
AF = mybir.ActivationFunctionType
ALU = mybir.AluOpType
AX = mybir.AxisListType

T, D, NS, L, M, FF = 2048, 1024, 64, 128, 16, 4096
DA = 3
DBd = 4
NK = DA + DBd + 1
NMOM = DA + DBd + DA + 1
LAM0 = math.exp(-8.0)
TSn = 2048.0
EPS = 1e-5

_DEBUG_TAPS = False
_STOP_AFTER = None

_DRAIN_OPS = {
    "tensor_copy", "tensor_add", "tensor_sub", "tensor_mul", "tensor_scalar",
    "tensor_scalar_mul", "tensor_scalar_add", "tensor_scalar_max",
    "tensor_scalar_min", "tensor_reduce", "reduce_sum", "reduce_max",
    "scalar_tensor_tensor", "reciprocal", "memset", "activation", "select",
}


class _DR:

    def __init__(self, eng):
        self._e = eng

    def __getattr__(self, nm):
        f = getattr(self._e, nm)
        if nm in _DRAIN_OPS:
            def g(*a, **k):
                r = f(*a, **k)
                self._e.drain()
                return r
            return g
        return f


def _host_constants():
    c = {}
    ii = np.arange(L, dtype=np.float64)
    dif = ii[None, :] - ii[:, None]
    mask = dif >= 0
    cP = np.zeros((128, (DA + 1) * 128), dtype=np.float64)
    for k in range(DA + 1):
        cP[:, k * 128:(k + 1) * 128] = np.where(
            mask, np.exp(-LAM0 * np.maximum(dif, 0)) *
            (np.maximum(dif, 0) / TSn) ** k / math.factorial(k), 0.0)
    c["cP"] = cP.astype(np.float32)

    r1 = (L - 1 - ii)
    qw = np.stack([np.exp(-LAM0 * r1) * (r1 / TSn) ** a / math.factorial(a)
                   for a in range(DA + 1)], axis=1)
    cQW = np.zeros((128, M * 128), dtype=np.float64)
    for cc in range(M):
        for a in range(DA + 1):
            cQW[:, cc * 128 + a * 32 + cc] = qw[:, a]
    c["cQW"] = cQW.astype(np.float32)

    ccs = np.arange(M, dtype=np.float64)
    dcc = ccs[None, :] - ccs[:, None] - 1.0
    mc = dcc >= 0
    r2 = np.maximum(dcc, 0.0) * L
    cTC = np.zeros((128, 256), dtype=np.float64)
    for a in range(DA + 1):
        for b in range(DBd + 1):
            blk = np.where(mc, np.exp(-LAM0 * r2) * (r2 / TSn) ** b / math.factorial(b), 0.0)
            kap = a + b
            base = (kap % 2) * 128 + (kap // 2) * 32
            for cc in range(M):
                cTC[a * 32:a * 32 + 16, base + cc] = blk[:, cc]
    c["cTC"] = cTC.astype(np.float32)

    r3 = ii + 1.0
    we = np.stack([np.exp(-LAM0 * r3) * (r3 / TSn) ** e / math.factorial(e)
                   for e in range(DA + 1)])
    cWE = np.zeros((128, M * 128), dtype=np.float64)
    for cc in range(M):
        for e2 in range(DA + 1):
            cWE[e2 * 32 + cc, cc * 128:(cc + 1) * 128] = we[e2]
    c["cWE"] = cWE.astype(np.float32)

    c["cID"] = np.eye(128, dtype=np.float32)
    return c


def build_nc(debug_taps=False, stop_after=None):
    nc = bass.Bass("TRN2", target_bir_lowering=False)

    def din(name, shape, dtype=F32):
        return nc.dram_tensor(name, shape, dtype, kind="ExternalInput")

    x_in = din("x", [T, D])
    logA = din("L0_attn_log_A", [D, NS])
    Bp = din("L0_attn_B_param", [D, NS])
    Cp = din("L0_attn_C_param", [D, NS])
    Dp = din("L0_attn_D_param", [D])
    logdt = din("L0_attn_log_dt", [D])
    gam = din("L0_attn_ln_gamma", [D])
    bet = din("L0_attn_ln_beta", [D])
    Wout = din("L0_attn_W_out", [D, D])
    bout = din("L0_attn_b_out", [D])
    W1 = din("L0_ffn_W1", [FF, D])
    b1 = din("L0_ffn_b1", [FF])
    W2 = din("L0_ffn_W2", [D, FF])
    b2 = din("L0_ffn_b2", [D])
    cP_in = din("cP", [128, (DA + 1) * 128])
    cQW_in = din("cQW", [128, M * 128])
    cTC_in = din("cTC", [128, 256])
    cWE_in = din("cWE", [128, M * 128])
    cID_in = din("cID", [128, 128])

    out_t = nc.dram_tensor("out", [T, D], F32, kind="ExternalOutput")

    scr_mrow = nc.dram_tensor("scr_mrow", [NMOM, D], F32)
    scr_rows = nc.dram_tensor("scr_rows", [2, D], F32)
    scr_h = nc.dram_tensor("scr_h", [32, 128, T], BF16)

    taps = {}
    if debug_taps:
        for nm, shp in [("tap_xn", [T, D]), ("tap_xn2", [T, D]), ("tap_y", [T, D]),
                        ("tap_q", [128, D]), ("tap_V", [128, D]),
                        ("tap_mom", [128, NMOM * 8]), ("tap_invs", [128, 2])]:
            taps[nm] = nc.dram_tensor(nm, shp, BF16 if nm == "tap_y" else F32,
                                      kind="ExternalOutput")

    st = ExitStack()
    stX = ExitStack()
    stA = ExitStack()
    stB = ExitStack()
    stC = ExitStack()
    stD = ExitStack()
    en = st.enter_context
    with st:
        cst = en(nc.sbuf_tensor("cst", [128, 14], F32))
        (C_EPS, C_1, C_NHALF, C_PHALF, C_HALF, C_INVD, C_N1,
         C_LAM0, C_NT, C_2, C_INVW, C_W8, C_0, C_SP) = range(14)
        onesc = en(nc.sbuf_tensor("onesc", [128, 1], F32))
        for _v, _c in [(0.0, C_0), (1.0, C_1), (2.0, C_2), (0.5, C_HALF), (-0.5, C_NHALF)]:
            nc.const_aps.register(F32, _v, cst[:, _c:_c + 1])
        invs = en(nc.sbuf_tensor("invs", [128, 2], F32))
        b1sb = en(nc.sbuf_tensor("b1sb", [128, 32], F32))
        stat_a = en(nc.sbuf_tensor("stat_a", [128, M], F32))
        stat_b = en(nc.sbuf_tensor("stat_b", [128, M], F32))
        tiny = en(nc.sbuf_tensor("tiny", [128, 16], F32))
        (TY_SX, TY_SXX, TY_MU, TY_EX2, TY_MSQ, TY_VAR, TY_RSTD,
         TY_T1, TY_Y2, TY_RSTD2, TY_MR, TY_MU2, TY_VAR2, TY_RSTDB,
         TY_MR2) = range(15)
        cID = en(nc.sbuf_tensor("sb_cID", [128, 128], BF16))
        wchunk = [en(nc.sbuf_tensor(f"wch{i}", [128, 1024], F32)) for i in range(2)]
        W2T = en(nc.sbuf_tensor("W2T", [128, 32 * 1024], BF16))
        yT = en(nc.sbuf_tensor("yT", [128, 8 * T], BF16))

        xnT = stX.enter_context(nc.sbuf_tensor("xnT", [128, 8 * T], BF16))
        Vt = stX.enter_context(nc.sbuf_tensor("Vt", [128, D], BF16))
        xn2sb = stX.enter_context(nc.sbuf_tensor("xn2sb", [128, M * 1024], BF16))

        enA = stA.enter_context
        cQW = enA(nc.sbuf_tensor("sb_cQW", [128, M * 128], BF16))
        cTC = enA(nc.sbuf_tensor("sb_cTC", [128, 256], BF16))
        gbc = enA(nc.sbuf_tensor("gbc", [128, D], BF16))
        bbc = enA(nc.sbuf_tensor("bbc", [128, D], BF16))
        mom = enA(nc.sbuf_tensor("mom", [128, NMOM * 8], F32))
        dt8 = enA(nc.sbuf_tensor("dt8", [128, 8], F32))
        dte = enA(nc.sbuf_tensor("dte", [128, 8], F32))
        rowA = enA(nc.sbuf_tensor("rowA", [8, 128], F32))
        rowB = enA(nc.sbuf_tensor("rowB", [8, 128], F32))
        xt = [enA(nc.sbuf_tensor(f"xt{i}", [128, D], F32)) for i in range(2)]
        xn_f = enA(nc.sbuf_tensor("xn_f", [128, D], F32))
        xsq = enA(nc.sbuf_tensor("xsq", [128, D], F32))
        xn_b = [enA(nc.sbuf_tensor(f"xnb{i}", [128, D], BF16)) for i in range(2)]
        mstk1 = xn_b
        zrep = enA(nc.sbuf_tensor("zrep", [128, D], BF16))
        q64 = enA(nc.sbuf_tensor("q64", [128, D], BF16))

        cIDf = xn_f[:, 0:128]
        pA = xt[0][:, 0:512]
        pB = xt[0][:, 512:1024]
        pC = xt[1][:, 0:512]
        dtx = xt[1][:, 512:1024]
        b1row = xsq[0:32, 0:128]

        ps_a = en(nc.psum_tensor("ps_a", [128, 512], F32))
        ps_b = en(nc.psum_tensor("ps_b", [128, 512], F32))
        ps_c = en(nc.psum_tensor("ps_c", [128, 1024], F32))
        ps_d = en(nc.psum_tensor("ps_d", [128, 1024], BF16))
        ps_e = en(nc.psum_tensor("ps_e", [128, 1024], F32))

        s_dv = en(nc.semaphore("s_dv"))
        s_te = en(nc.semaphore("s_te"))
        s_a2 = en(nc.semaphore("s_a2"))
        s_wa = en(nc.semaphore("s_wa"))
        s_wm = en(nc.semaphore("s_wm"))
        s_wv = en(nc.semaphore("s_wv"))
        SEM = {"dv": s_dv, "te": s_te, "a2": s_a2}
        ctr = {k: 0 for k in SEM}

        def bump(inst, which, n=1):
            inst.then_inc(SEM[which], n)
            ctr[which] += n
            return ctr[which]

        streams = {}
        for _n in ["ld", "cst", "iv", "mk", "b3l",
                   "x0", "x1", "xn0", "xn1", "wp0", "wp1", "wq0", "wq1",
                   "x20", "x21", "w1l0", "w1l1", "w1w0", "w1w1",
                   "w1c0", "w1c1", "w2l0", "w2l1", "hw0", "hw1",
                   "hge0", "hge1", "hgo0", "hgo1", "xr0", "xr1",
                   "ow0", "ow1", "tap", "tapg", "tg0", "tg1"]:
            streams[_n] = {"sem": en(nc.semaphore("st_" + _n)), "n": 0}

        def sdma(q, name, out, in_):
            s = streams[name]
            q.dma_start(out, in_).then_inc(s["sem"], 16)
            s["n"] += 16
            return s["n"]

        def swt(eng, name, val=None):
            s = streams[name]
            eng.wait_ge(s["sem"], s["n"] if val is None else val)

        csl = lambda c: cst[:, c:c + 1]
        ps_d3 = ps_d.ap().rearrange("p (b f) -> p b f", f=128)
        xnT3 = xnT.ap().rearrange("p (b t) -> p b t", t=T)
        yT3 = yT.ap().rearrange("p (b t) -> p b t", t=T)
        W2T3 = W2T.ap().rearrange("p (f c) -> p f c", c=1024)

        def bc_ap(dt_, n=128, width=D, off=0):
            return bass.AP(dt_, off, [[0, n], [1, width]])

        def _early_out(open_stacks):
            with nc.Block() as blk2:
                @blk2.sync
                def _(sy):
                    sdma(sy, "ld", out_t.ap(), x_in.ap())
                    swt(sy, "ld")
                    for nm2, s2 in streams.items():
                        sy.wait_ge(s2["sem"], s2["n"]) if False else None

                @blk2.scalar
                def _(ac):
                    for nm2 in ["xn0", "xn1", "hw0", "hw1", "hgo0", "hgo1",
                                "xr0", "xr1", "tap"]:
                        swt(ac, nm2, streams[nm2]["n"])

                @blk2.sync
                def _(sy):
                    for nm2 in ["ld", "x0", "x1", "iv", "x20", "x21", "w1w0", "w1w1",
                                "w1c0", "w1c1", "hge0", "hge1", "wq0", "wq1"]:
                        swt(sy, nm2, streams[nm2]["n"])

                @blk2.gpsimd
                def _(gp):
                    for nm2 in ["cst", "mk", "b3l", "wp0", "wp1", "w1l0", "w1l1",
                                "w2l0", "w2l1", "tapg", "tg0", "tg1"]:
                        swt(gp, nm2, streams[nm2]["n"])
            for s2 in open_stacks:
                s2.close()


        with nc.Block() as blk:
            @blk.vector
            def _(dv):
                dv = _DR(dv)
                bump(dv.memset(xsq[:, :], 0.0), "dv")

            @blk.sync
            def _(sy):
                sdma(sy, "ld", cIDf, cID_in.ap())
                sy.wait_ge(s_dv, 1)
                sdma(sy, "ld", b1row, b1.ap().rearrange("(c p) -> c p", p=128))
                sdma(sy, "ld", dt8[:, :], bass.AP(logdt, 0, [[8, 128], [1, 8]]))
                ap_dn = [[8 * NS, 128], [1, 8 * NS]]
                sdma(sy, "ld", pA, bass.AP(logA, 0, ap_dn))
                sdma(sy, "ld", pB, bass.AP(Bp, 0, ap_dn))
                sdma(sy, "ld", pC, bass.AP(Cp, 0, ap_dn))
                sdma(sy, "ld", rowA[:, :], bass.AP(bout, 0, [[128, 8], [1, 128]]))
                sdma(sy, "ld", rowB[:, :], bass.AP(b2, 0, [[128, 8], [1, 128]]))
            b0_ld = streams["ld"]["n"]

            @blk.gpsimd
            def _(gp):
                sdma(gp, "cst", cQW[:, :], cQW_in.ap())
                sdma(gp, "cst", gbc[:, :], bc_ap(gam))
                sdma(gp, "cst", bbc[:, :], bc_ap(bet))
                sdma(gp, "cst", cTC[:, :], cTC_in.ap())
                sdma(gp, "cst", cID[:, :], cID_in.ap())
            b0_cst = streams["cst"]["n"]

            @blk.tensor
            def _(te):
                swt(te, "ld", b0_ld)
                te.wait_ge(s_dv, 1)
                bump(te.transpose(ps_c[:, 0:128], xsq[:, 0:128], cIDf), "te")
            b0_te = ctr["te"]

            @blk.vector
            def _(dv):
                dv = _DR(dv)
                for col, val in [(C_EPS, EPS), (C_1, 1.0), (C_NHALF, -0.5), (C_PHALF, 0.5),
                                 (C_HALF, 0.5), (C_INVD, 1.0 / 1024.0), (C_N1, -1.0),
                                 (C_LAM0, LAM0), (C_NT, -TSn), (C_2, 2.0),
                                 (C_INVW, 1.0 / (FF * D)), (C_W8, 1e-8),
                                 (C_0, 0.0), (C_SP, 0.0)]:
                    dv.memset(cst[:, col:col + 1], val)
                bump(dv.memset(onesc[:, :], 1.0), "dv")
                dv.wait_ge(s_te, b0_te)
                bump(dv.tensor_copy(b1sb[:, :], ps_c[:, 0:32]), "dv")
                swt(dv, "ld", b0_ld)
                bump(dv.tensor_add(rowA[:, :], rowA[:, :], rowB[:, :]), "dv")
            b0_dv = ctr["dv"]

            @blk.scalar
            def _(ac):
                ac = _DR(ac)
                ac.wait_ge(s_dv, b0_dv - 2)
                swt(ac, "ld", b0_ld)
                bump(ac.activation(dte[:, :], dt8[:, :], AF.Exp), "a2")
                bump(ac.activation(pA, pA, AF.Exp), "a2")
            b0_a2 = ctr["a2"]

            @blk.sync
            def _(sy):
                sy.wait_ge(s_dv, b0_dv)
                sdma(sy, "ld", bass.AP(scr_rows, 0, [[128, 8], [1, 128]]),
                     rowA[:, :])
            b0_ld2 = streams["ld"]["n"]

        if stop_after == "B0":
            _early_out([stA, stX])
            return nc

        with nc.Block() as blk:
            @blk.vector
            def _(dv):
                dv = _DR(dv)
                dv.wait_ge(s_a2, b0_a2)
                dtx3 = dtx.rearrange("p (b n) -> p b n", n=64)
                for n in range(63):
                    dv._e.tensor_copy(dtx3[:, :, n:n + 1], dte[:, :])
                dv.tensor_copy(dtx3[:, :, 63:64], dte[:, :])
                dv.tensor_mul(pA, pA, dtx)
                dv.tensor_mul(pB, pB, dtx)
                dv.tensor_mul(pC, pC, pB)
                dv.tensor_scalar(dtx, pA, csl(C_LAM0), csl(C_NT),
                                 ALU.subtract, ALU.mult)
                for k in range(NMOM):
                    dv.tensor_reduce(mom[:, k * 8:(k + 1) * 8],
                                     pC.rearrange("p (b n) -> p b n", n=64),
                                     AX.X, ALU.add)
                    if k < NMOM - 1:
                        dv.tensor_mul(pC, pC, dtx)
                bump(dv.tensor_copy(tiny[:, TY_MR:TY_MR + 1], csl(C_1)), "dv")
            b0b_dv = ctr["dv"]

            @blk.sync
            def _(sy):
                sy.wait_ge(s_dv, b0b_dv)
                for k in range(NMOM):
                    sdma(sy, "ld", bass.AP(scr_mrow, k * D, [[8, 128], [1, 8]]),
                         mom[:, k * 8:(k + 1) * 8])
            b0b_ld = streams["ld"]["n"]

            b0b_cst = streams["cst"]["n"]

            if debug_taps:
                @blk.scalar
                def _(ac):
                    ac.wait_ge(s_dv, b0b_dv)
                    sdma(ac, "tap", taps["tap_mom"].ap(), mom[:, :])
        tap_b1 = streams["tap"]["n"]
        if stop_after == "B0b":
            _early_out([stA, stX])
            return nc

        wp1 = [("W1", k) for k in range(32)] + [("W2", k) for k in range(32)]

        def wp1_ap(which, k, half=None):
            lo = 0 if half is None else half * 512
            hi = 1024 if half is None else (half + 1) * 512
            if which == "W1":
                return W1.ap()[k * 128:(k + 1) * 128, lo:hi]
            dblk, fq = divmod(k, 4)
            return W2.ap()[dblk * 128:(dblk + 1) * 128,
                           fq * 1024 + lo:fq * 1024 + hi]

        a20 = ctr["a2"]; dv0 = ctr["dv"]; te0 = ctr["te"]
        E_sy_x = lambda tc: 16 * (tc // 2 + 1)
        E_a2_sq = lambda k: (a20 + 1) if k == 0 else (a20 + 5 * k - 3)
        E_a2_rstd = lambda tc: a20 + 5 * tc + 3
        E_a2_xnb = lambda tc: a20 + 5 * tc + 4
        E_a2_rstd2 = lambda tc: a20 + 5 * tc + 5
        E_a2_evT = lambda k: (a20 + 5 * k + 11) if k < M - 1 else (a20 + 5 * M + 2)
        E_dv_var = lambda tc: (dv0 + 1) if tc == 0 else (dv0 + 4 * tc - 2)
        E_dv_apply = lambda tc: dv0 + 4 * tc + 3
        E_dv_y2 = lambda tc: dv0 + 4 * tc + 4
        E_dv_xn2 = lambda tc: dv0 + 4 * tc + 5
        E_te_tr = lambda tc: te0 + 2 * tc + 1
        E_te_q = lambda tc: te0 + 2 * tc + 2
        MUc = lambda t: TY_MU if t % 2 == 0 else TY_MU2
        VARc = lambda t: TY_VAR if t % 2 == 0 else TY_VAR2
        RSTDc = lambda t: TY_RSTD if t % 2 == 0 else TY_RSTDB

        with nc.Block() as blk:
            @blk.sync
            def _(sy):
                for tc in range(M):
                    if tc >= 2:
                        sy.wait_ge(s_dv, E_dv_apply(tc - 2))
                    sdma(sy, f"x{tc % 2}", xt[tc % 2][:, :],
                         x_in.ap()[tc * 128:(tc + 1) * 128, :])

            @blk.scalar
            def _(ac):
                swt(ac, "x0", E_sy_x(0))
                bump(ac.activation(xsq[:, :], xt[0][:, :], AF.Square), "a2")
                for tc in range(M):
                    if tc + 1 < M:
                        t1 = tc + 1
                        swt(ac, f"x{t1 % 2}", E_sy_x(t1))
                        ac.wait_ge(s_dv, E_dv_xn2(tc - 1) if tc >= 1
                                   else E_dv_var(0))
                        bump(ac.activation(xsq[:, :], xt[t1 % 2][:, :], AF.Square),
                             "a2")
                    else:
                        bump(ac.activation(tiny[:, TY_MR2:TY_MR2 + 1], csl(C_1),
                                           AF.Copy, bias=0.0, scale=1.0), "a2")
                    ac.wait_ge(s_dv, E_dv_var(tc))
                    bump(ac.activation(tiny[:, RSTDc(tc):RSTDc(tc) + 1],
                                       tiny[:, VARc(tc):VARc(tc) + 1], AF.Sqrt,
                                       bias=csl(C_EPS), scale=csl(C_1)), "a2")
                    ac.wait_ge(s_dv, E_dv_apply(tc))
                    if tc >= 2:
                        ac.wait_ge(s_te, E_te_tr(tc - 2))
                    bump(ac.activation(xn_b[tc % 2][:, :], xn_f[:, :], AF.Copy,
                                       bias=0.0, scale=1.0), "a2")
                    ac.wait_ge(s_dv, E_dv_y2(tc))
                    bump(ac.activation(tiny[:, TY_RSTD2:TY_RSTD2 + 1],
                                       tiny[:, TY_Y2:TY_Y2 + 1], AF.Sqrt,
                                       bias=csl(C_EPS), scale=csl(C_1)), "a2")
                    if tc >= 1:
                        ac.wait_ge(s_te, E_te_tr(tc - 1))
                        bump(ac.activation(xnT3[:, :, (tc - 1) * 128:tc * 128], ps_d3,
                                           AF.Copy, bias=0.0, scale=1.0), "a2")
                    else:
                        bump(ac.activation(tiny[:, TY_MR2:TY_MR2 + 1], csl(C_1),
                                           AF.Copy, bias=0.0, scale=1.0), "a2")
                    if debug_taps:
                        sdma(ac, "tap", taps["tap_xn"].ap()[tc * 128:(tc + 1) * 128, :],
                             xn_f[:, :])
                ac.wait_ge(s_te, E_te_tr(M - 1))
                bump(ac.activation(xnT3[:, :, (M - 1) * 128:M * 128], ps_d3,
                                   AF.Copy, bias=0.0, scale=1.0), "a2")
                assert ctr["a2"] == a20 + 5 * M + 2

            @blk.vector
            def _(dv):
                dv = _DR(dv)
                swt(dv, "ld", b0_ld)
                swt(dv, "cst", b0b_cst)

                def stageA(t):
                    p2 = t % 2
                    swt(dv, f"x{p2}", E_sy_x(t))
                    dv.tensor_reduce(tiny[:, TY_SX:TY_SX + 1], xt[p2][:, :],
                                     AX.X, ALU.add)
                    dv.wait_ge(s_a2, E_a2_sq(t))
                    dv.tensor_reduce(tiny[:, TY_SXX:TY_SXX + 1], xsq[:, :],
                                     AX.X, ALU.add)
                    dv.tensor_scalar_mul(tiny[:, MUc(t):MUc(t) + 1],
                                         tiny[:, TY_SX:TY_SX + 1], csl(C_INVD))
                    dv.tensor_scalar_mul(tiny[:, TY_EX2:TY_EX2 + 1],
                                         tiny[:, TY_SXX:TY_SXX + 1], csl(C_INVD))
                    dv.tensor_mul(tiny[:, TY_MSQ:TY_MSQ + 1],
                                  tiny[:, MUc(t):MUc(t) + 1],
                                  tiny[:, MUc(t):MUc(t) + 1])
                    bump(dv.tensor_sub(tiny[:, VARc(t):VARc(t) + 1],
                                       tiny[:, TY_EX2:TY_EX2 + 1],
                                       tiny[:, TY_MSQ:TY_MSQ + 1]), "dv")

                def stageB(t):
                    p2 = t % 2
                    dv.wait_ge(s_a2, E_a2_rstd(t))
                    dv.reciprocal(tiny[:, RSTDc(t):RSTDc(t) + 1],
                                  tiny[:, RSTDc(t):RSTDc(t) + 1])
                    if debug_taps and t >= 1:
                        swt(dv, "tap", tap_b1 + 16 * t)
                    bump(dv.tensor_scalar(xn_f[:, :], xt[p2][:, :],
                                          tiny[:, MUc(t):MUc(t) + 1],
                                          tiny[:, RSTDc(t):RSTDc(t) + 1],
                                          ALU.subtract, ALU.mult), "dv")
                    dv.tensor_mul(tiny[:, TY_T1:TY_T1 + 1],
                                  tiny[:, VARc(t):VARc(t) + 1],
                                  tiny[:, RSTDc(t):RSTDc(t) + 1])
                    bump(dv.tensor_mul(tiny[:, TY_Y2:TY_Y2 + 1],
                                       tiny[:, TY_T1:TY_T1 + 1],
                                       tiny[:, RSTDc(t):RSTDc(t) + 1]), "dv")
                    dv.tensor_scalar_add(stat_a[:, t:t + 1],
                                         tiny[:, RSTDc(t):RSTDc(t) + 1], csl(C_1))
                    dv.tensor_scalar(stat_b[:, t:t + 1], tiny[:, MUc(t):MUc(t) + 1],
                                     tiny[:, RSTDc(t):RSTDc(t) + 1], csl(C_N1),
                                     ALU.mult, ALU.mult)
                    dv.wait_ge(s_a2, E_a2_rstd2(t))
                    dv.reciprocal(tiny[:, TY_RSTD2:TY_RSTD2 + 1],
                                  tiny[:, TY_RSTD2:TY_RSTD2 + 1])
                    dv.scalar_tensor_tensor(xsq[:, :], xn_f[:, :],
                                            tiny[:, TY_RSTD2:TY_RSTD2 + 1], gbc[:, :],
                                            ALU.mult, ALU.mult)
                    bump(dv.tensor_add(xn2sb[:, t * 1024:(t + 1) * 1024],
                                       xsq[:, :], bbc[:, :]), "dv")

                stageA(0)
                for tc in range(M):
                    if tc + 1 < M:
                        stageA(tc + 1)
                    else:
                        bump(dv.tensor_copy(tiny[:, TY_MR:TY_MR + 1], csl(C_1)), "dv")
                    stageB(tc)
                assert ctr["dv"] == dv0 + 4 * M + 1

            @blk.tensor
            def _(te):
                swt(te, "cst", b0b_cst)
                for tc in range(M):
                    p = tc % 2
                    te.wait_ge(s_a2, E_a2_xnb(tc))
                    if tc >= 1:
                        te.wait_ge(s_a2, E_a2_evT(tc - 1))
                    for dblk in range(8):
                        inst = te.transpose(ps_d[:, dblk * 128:(dblk + 1) * 128],
                                            xn_b[p][:, dblk * 128:(dblk + 1) * 128],
                                            cID[:, :])
                    bump(inst, "te")
                    te.wait_ge(s_dv, E_dv_xn2(tc))
                    te.matmul(ps_c[:, 0:512], cQW[:, tc * 128:(tc + 1) * 128],
                              xn2sb[:, tc * 1024:tc * 1024 + 512],
                              start=(tc == 0), stop=(tc == M - 1))
                    bump(te.matmul(ps_c[:, 512:1024], cQW[:, tc * 128:(tc + 1) * 128],
                                   xn2sb[:, tc * 1024 + 512:(tc + 1) * 1024],
                                   start=(tc == 0), stop=(tc == M - 1)), "te")
                assert ctr["te"] == te0 + 2 * M

        if stop_after == "B1":
            _early_out([stA, stX])
            return nc

        with nc.Block() as blk:
            @blk.vector
            def _(dv):
                dv = _DR(dv)
                dv.tensor_copy(q64[:, :], ps_c[:, :])
                dv.memset(zrep[:, :], 0.0)
                dv.memset(mstk1[0][:, :], 0.0)
                bump(dv.memset(mstk1[1][:, :], 0.0), "dv")
            b2_dv1 = ctr["dv"]

            @blk.tensor
            def _(te):
                te.wait_ge(s_dv, b2_dv1)
                for par, tgt in ((0, ps_e), (1, ps_c)):
                    te.matmul(tgt[:, 0:512], cTC[:, par * 128:(par + 1) * 128],
                              q64[:, 0:512], start=True, stop=True)
                    inst = te.matmul(tgt[:, 512:1024], cTC[:, par * 128:(par + 1) * 128],
                                     q64[:, 512:1024], start=True, stop=True)
                bump(inst, "te")
            b2_te1 = ctr["te"]

            dv_b2 = ctr["dv"]
            E_dv_v = lambda kap: dv_b2 + kap + 1
            E_gp_mk = lambda kap: 64 * (kap + 1)

            @blk.gpsimd
            def _(gp):
                swt(gp, "ld", b0b_ld)
                gp.wait_ge(s_dv, b2_dv1)
                for kap in range(NK):
                    if kap >= 2:
                        gp.wait_ge(s_dv, E_dv_v(kap - 2))
                    for e2 in range(DA + 1):
                        sdma(gp, "mk", mstk1[kap % 2][e2 * 32:e2 * 32 + 16, :],
                             bass.AP(scr_mrow, (kap + e2) * D, [[0, 16], [1, D]]))

            @blk.vector
            def _(dv):
                dv = _DR(dv)
                dv.wait_ge(s_te, b2_te1)
                for kap in range(NK):
                    src_ps = ps_e if kap % 2 == 0 else ps_c
                    base = (kap // 2) * 32
                    swt(dv, "mk", E_gp_mk(kap))
                    dv.tensor_copy(zrep[0:16, :], src_ps[base:base + 16, :])
                    for e2 in range(1, 4):
                        dv.tensor_copy(zrep[e2 * 32:e2 * 32 + 16, :], zrep[0:16, :])
                    if kap == 0:
                        bump(dv.tensor_mul(Vt[:, :], zrep[:, :], mstk1[0][:, :]), "dv")
                    else:
                        dv.tensor_mul(xn_f[:, :], zrep[:, :], mstk1[kap % 2][:, :])
                        bump(dv.tensor_add(Vt[:, :], Vt[:, :], xn_f[:, :]), "dv")
                assert ctr["dv"] == E_dv_v(NK - 1)
            b2_dv2 = ctr["dv"]

            if debug_taps:
                @blk.gpsimd
                def _(gp):
                    gp.wait_ge(s_dv, b2_dv2)
                    sdma(gp, "tapg", taps["tap_q"].ap(), q64[:, :])
                    sdma(gp, "tapg", taps["tap_V"].ap(), Vt[:, :])

        if stop_after == "B2":
            _early_out([stA, stX])
            return nc

        stA.close()
        enB = stB.enter_context
        u_b = [enB(nc.sbuf_tensor(f"ub_{k}", [128, D], BF16))
               for k in range(DA + 2)]
        y_b = enB(nc.sbuf_tensor("y_b", [128, D], BF16))
        cP = enB(nc.sbuf_tensor("sb_cP", [128, (DA + 1) * 128], BF16))
        cWE = enB(nc.sbuf_tensor("sb_cWE", [128, M * 128], BF16))
        mtb = enB(nc.sbuf_tensor("mtb", [128, (DA + 1) * D], BF16))
        dbc = enB(nc.sbuf_tensor("dbc", [128, D], BF16))
        wabs = [enB(nc.sbuf_tensor(f"wabs{i}", [128, 512], F32)) for i in range(4)]
        wacc = enB(nc.sbuf_tensor("wacc", [1, 512], F32))
        scal1 = enB(nc.sbuf_tensor("scal1", [1, 8], F32))

        dv0 = ctr["dv"]; te0 = ctr["te"]
        E_dv_uh = lambda c, h: dv0 + 4 * c + 1 + h
        E_dv_yev = lambda k: (dv0 + 4 * (k + 1) + 3) if k < M - 1 else (dv0 + 4 * M + 1)

        def E_dv_ytr(k):
            if k <= M - 3:
                return dv0 + 4 * (k + 2) + 4
            return dv0 + 4 * M + 2 if k == M - 2 else dv0 + 4 * M + 3
        E_te_ymm = lambda c, h: te0 + 3 * c + 1 + h
        E_te_ytr = lambda k: (te0 + 3 * (k + 1) + 3) if k < M - 1 else (te0 + 3 * M + 1)
        tapg_b3 = streams["tapg"]["n"]

        with nc.Block() as blk:
            @blk.gpsimd
            def _(gp):
                swt(gp, "ld", b0b_ld)
                if debug_taps:
                    swt(gp, "tap", streams["tap"]["n"])
                sdma(gp, "b3l", cP[:, :], cP_in.ap())
                sdma(gp, "b3l", cWE[:, :], cWE_in.ap())
                sdma(gp, "b3l", dbc[:, :], bc_ap(Dp))
                for k in range(DA + 1):
                    sdma(gp, "b3l", mtb[:, k * D:(k + 1) * D],
                         bass.AP(scr_mrow, k * D, [[0, 128], [1, D]]))
                for wk in range(64):
                    if wk >= 2:
                        gp.wait_ge(s_wa, 2 * wk - 2)
                    if debug_taps:
                        sdma(gp, f"wp{wk % 2}", wchunk[wk % 2][:, :],
                             wp1_ap(*wp1[wk]))
                    else:
                        sdma(gp, f"wp{wk % 2}", wchunk[wk % 2][:, 512:1024],
                             wp1_ap(*wp1[wk], half=1))
            b3_l = streams["b3l"]["n"]

            if not debug_taps:
                @blk.sync
                def _(sy):
                    for wk in range(64):
                        if wk >= 2:
                            sy.wait_ge(s_wa, 2 * wk - 2)
                        sdma(sy, f"wq{wk % 2}", wchunk[wk % 2][:, 0:512],
                             wp1_ap(*wp1[wk], half=0))

            @blk.scalar
            def _(ac):
                for j in range(128):
                    wk, h = divmod(j, 2)
                    if h == 1 or debug_taps:
                        swt(ac, f"wp{wk % 2}", 16 * (wk // 2 + 1))
                    else:
                        swt(ac, f"wq{wk % 2}", 16 * (wk // 2 + 1))
                    if j >= 4:
                        ac.wait_ge(s_wm, j - 3)
                    ac.activation(wabs[j % 4][:, :],
                                  wchunk[wk % 2][:, h * 512:(h + 1) * 512],
                                  AF.Abs).then_inc(s_wa, 1)

            @blk.vector
            def _(dv):
                dv = _DR(dv)
                swt(dv, "b3l", b3_l)
                for c in range(M):
                    p = c % 2
                    x2sl3 = xn2sb.ap().rearrange("p (c2 h f) -> p c2 h f",
                                                 c2=M, h=2, f=512)
                    for h in range(2):
                        if c >= 1:
                            dv.wait_ge(s_te, E_te_ymm(c - 1, h))
                        hs = slice(h * 512, (h + 1) * 512)
                        for k in range(DA + 1):
                            dv._e.tensor_mul(u_b[k][:, hs],
                                             xn2sb[:, c * 1024 + h * 512:
                                                   c * 1024 + (h + 1) * 512],
                                             mtb[:, k * D + h * 512:
                                                 k * D + (h + 1) * 512])
                        bump(dv._e.tensor_mul(u_b[DA + 1][:, hs],
                                              xn2sb[:, c * 1024 + h * 512:
                                                    c * 1024 + (h + 1) * 512],
                                              dbc[:, hs]), "dv")
                    if c >= 1:
                        kk = c - 1
                        dv.wait_ge(s_te, E_te_ymm(kk, 1))
                        if kk >= 1:
                            dv.wait_ge(s_te, E_te_ytr(kk - 1))
                        if debug_taps and kk >= 1:
                            swt(dv, f"tg{(kk - 1) % 2}", 16 * ((kk - 1) // 2 + 1))
                        psy = ps_c if kk % 2 == 0 else ps_e
                        bump(dv._e.tensor_copy(y_b[:, :], psy[:, :]), "dv")
                    else:
                        bump(dv.tensor_copy(tiny[:, TY_MR:TY_MR + 1], csl(C_1)), "dv")
                    if c >= 2:
                        kk = c - 2
                        dv.wait_ge(s_te, E_te_ytr(kk))
                        bump(dv._e.tensor_copy(yT3[:, :, kk * 128:(kk + 1) * 128],
                                               ps_d3), "dv")
                    else:
                        bump(dv.tensor_copy(tiny[:, TY_MR:TY_MR + 1], csl(C_1)), "dv")
                    if c % 2 == 1:
                        g = (c - 1) // 2
                        which, gl = (0, g) if g < 4 else (1, g - 4)
                        srcp = ps_a if which == 0 else ps_b
                        dv.wait_ge(s_wm, 8 * c + 8)
                        if gl == 0:
                            dv.tensor_copy(wacc[0:1, :],
                                           srcp[0:1, 0:512]).then_inc(s_wv, 1)
                        else:
                            dv.tensor_add(wacc[0:1, :], wacc[0:1, :],
                                          srcp[0:1, 0:512]).then_inc(s_wv, 1)
                        if gl == 3:
                            n = 256
                            while n >= 1:
                                dv.tensor_add(wacc[0:1, 0:n], wacc[0:1, 0:n],
                                              wacc[0:1, n:2 * n])
                                n //= 2
                            dv.tensor_copy(scal1[0:1, which:which + 1],
                                           wacc[0:1, 0:1])
                dv.wait_ge(s_te, E_te_ymm(M - 1, 1))
                dv.wait_ge(s_te, E_te_ytr(M - 2))
                if debug_taps:
                    swt(dv, f"tg{(M - 2) % 2}", 16 * ((M - 2) // 2 + 1))
                psy = ps_c if (M - 1) % 2 == 0 else ps_e
                bump(dv.tensor_copy(y_b[:, :], psy[:, :]), "dv")
                bump(dv.tensor_copy(yT3[:, :, (M - 2) * 128:(M - 1) * 128], ps_d3), "dv")
                dv.wait_ge(s_te, E_te_ytr(M - 1))
                bump(dv.tensor_copy(yT3[:, :, (M - 1) * 128:M * 128], ps_d3), "dv")
                assert ctr["dv"] == dv0 + 4 * M + 3

            @blk.tensor
            def _(te):
                def wmm(j):
                    te.wait_ge(s_wa, j + 1)
                    which = 0 if j < 64 else 1
                    tgt = ps_a if which == 0 else ps_b
                    gl = (j % 64) // 16
                    if j % 16 == 0 and gl >= 1:
                        te.wait_ge(s_wv, which * 4 + gl)
                    te.matmul(tgt[0:1, 0:512], onesc[:, 0:1], wabs[j % 4][:, :],
                              start=(j % 16 == 0),
                              stop=(j % 16 == 15)).then_inc(s_wm, 1)

                for c in range(M):
                    p = c % 2
                    psy = ps_c if c % 2 == 0 else ps_e
                    for half in range(2):
                        te.wait_ge(s_dv, E_dv_uh(c, half))
                        if half == 0 and c >= 2:
                            te.wait_ge(s_dv, E_dv_yev(c - 2))
                        sl = slice(half * 512, (half + 1) * 512)
                        for k in range(DA + 1):
                            te.matmul(psy[:, sl], cP[:, k * 128:(k + 1) * 128],
                                      u_b[k][:, sl], start=(k == 0), stop=False)
                        te.matmul(psy[:, sl], cID[:, :], u_b[DA + 1][:, sl],
                                  start=False, stop=False)
                        bump(te.matmul(psy[:, sl], cWE[:, c * 128:(c + 1) * 128],
                                       Vt[:, sl], start=False, stop=True), "te")
                        for j in range(8 * c + 4 * half, 8 * c + 4 * half + 4):
                            wmm(j)
                    if c >= 1:
                        kk = c - 1
                        te.wait_ge(s_dv, E_dv_yev(kk))
                        if c >= 2:
                            te.wait_ge(s_dv, E_dv_ytr(kk - 1))
                        for dblk in range(8):
                            inst = te.transpose(ps_d[:, dblk * 128:(dblk + 1) * 128],
                                                y_b[:, dblk * 128:(dblk + 1) * 128],
                                                cID[:, :])
                        bump(inst, "te")
                    else:
                        bump(te.transpose(ps_d[:, 0:128], cID[:, :], cID[:, :]), "te")
                te.wait_ge(s_dv, E_dv_yev(M - 1))
                te.wait_ge(s_dv, E_dv_ytr(M - 2))
                for dblk in range(8):
                    inst = te.transpose(ps_d[:, dblk * 128:(dblk + 1) * 128],
                                        y_b[:, dblk * 128:(dblk + 1) * 128],
                                        cID[:, :])
                bump(inst, "te")
                assert ctr["te"] == te0 + 3 * M + 1

            if debug_taps:
                @blk.sync
                def _(sy):
                    for kk in range(M):
                        sy.wait_ge(s_dv, E_dv_yev(kk))
                        sdma(sy, f"tg{kk % 2}", taps["tap_y"].ap()[kk * 128:(kk + 1) * 128, :],
                             y_b[:, :])

        if stop_after == "B3":
            _early_out([stB, stX])
            return nc

        with nc.Block() as blk:
            @blk.vector
            def _(dv):
                dv = _DR(dv)
                dv.tensor_scalar(scal1[0:1, 2:4], scal1[0:1, 0:2],
                                 cst[0:1, C_INVW:C_INVW + 1], cst[0:1, C_W8:C_W8 + 1],
                                 ALU.mult, ALU.max)
                bump(dv.reciprocal(invs[0:1, 0:2], scal1[0:1, 2:4]), "dv")
            b3b_dv = ctr["dv"]

            @blk.sync
            def _(sy):
                sy.wait_ge(s_dv, b3b_dv)
                n = 1
                while n < 128:
                    m2 = min(n, 128 - n)
                    sdma(sy, "iv", invs[n:n + m2, :], invs[0:m2, :])
                    swt(sy, "iv")
                    n += m2
            b3b_iv = streams["iv"]["n"]

            if debug_taps:
                @blk.gpsimd
                def _(gp):
                    swt(gp, "iv", b3b_iv)
                    sdma(gp, "tapg", taps["tap_invs"].ap(), invs[:, :])

        stB.close()
        enC = stC.enter_context
        w1st = [enC(nc.sbuf_tensor(f"w1st{i}", [128, 1024], BF16)) for i in range(3)]
        hst = [enC(nc.sbuf_tensor(f"hst{i}", [128, T], BF16)) for i in range(2)]
        wsgA = enC(nc.sbuf_tensor("wsgA", [128, 1024], BF16))
        wsgB = enC(nc.sbuf_tensor("wsgB", [128, 1024], BF16))
        w1tn = enC(nc.sbuf_tensor("w1tn", [128, 1024], BF16))
        w2tn = [enC(nc.sbuf_tensor(f"w2tn{i}", [128, 1024], BF16)) for i in range(2)]
        wchC = [enC(nc.sbuf_tensor(f"wchC{i}", [128, 1024], F32)) for i in range(2)]

        a20 = ctr["a2"]; dv0 = ctr["dv"]; te0 = ctr["te"]
        nA, nD, nT = a20, dv0, te0
        A_s1 = {}; A_g = {}; A_s2 = {}
        D_a1 = {}; D_e1 = {}; D_a2 = {}; D_e2 = {}
        T_t1 = {}; T_h = {}; T_t2 = {}
        nA += 2; A_s1[0] = nA
        nA += 2; A_s1[1] = nA
        nD += 1; D_a1[0] = nD
        nD += 1; D_e1[0] = nD
        nD += 1; D_a1[1] = nD
        nD += 1; D_e1[1] = nD
        nT += 1; T_t1[0] = nT
        nT += 1; T_t1[1] = nT
        for fc in range(32):
            if fc <= 29:
                nA += 2; A_s1[fc + 2] = nA
            for t4 in range(4):
                nA += 1; A_g[(fc, t4)] = nA
            nA += 2; A_s2[fc] = nA
            if fc <= 29:
                nD += 1; D_a1[fc + 2] = nD
                nD += 1; D_e1[fc + 2] = nD
            nD += 1; D_a2[fc] = nD
            if fc >= 1:
                nD += 1; D_e2[fc - 1] = nD
            for t4 in range(4):
                nT += 1; T_h[(fc, t4)] = nT
            if fc <= 29:
                nT += 1; T_t1[fc + 2] = nT
            if fc >= 1:
                nT += 1; T_t2[fc - 1] = nT
        nT += 1; T_t2[31] = nT
        nD += 1; D_e2[31] = nD
        E_ac_hw = lambda fc: 16 * (fc // 2 + 1)

        with nc.Block() as blk:
            @blk.gpsimd
            def _(gp):
                sdma(gp, "w1l0", wchunk[0][:, :], wp1_ap("W1", 0))
                sdma(gp, "w2l0", wchC[0][:, :], wp1_ap("W2", 0))
                sdma(gp, "w1l0", wchunk[1][:, :], wp1_ap("W1", 1))
                for fc in range(32):
                    if fc <= 29:
                        k = fc + 2
                        gp.wait_ge(s_a2, A_s1[fc])
                        sdma(gp, "w1l0", wchunk[k % 2][:, :], wp1_ap("W1", k))
                    if fc <= 30:
                        k = fc + 1
                        if fc >= 1:
                            gp.wait_ge(s_a2, A_s2[fc - 1])
                        sdma(gp, "w2l0", wchC[k % 2][:, :], wp1_ap("W2", k))

            @blk.scalar
            def _(ac):
                swt(ac, "iv", b3b_iv)
                for k in range(2):
                    swt(ac, "w1l0", 16 * (k + 1))
                    if k == 1:
                        ac.wait_ge(s_dv, D_a1[0])
                    bump(ac.activation(wsgA[:, :], wchunk[k][:, :], AF.Sign,
                                       bias=csl(C_NHALF), scale=invs[:, 0:1]), "a2")
                    bump(ac.activation(wsgB[:, :], wchunk[k][:, :], AF.Sign,
                                       bias=csl(C_PHALF), scale=invs[:, 0:1]), "a2")
                for fc in range(32):
                    p = fc % 2
                    if fc <= 29:
                        k = fc + 2
                        swt(ac, "w1l0", 16 * (k + 1))
                        ac.wait_ge(s_dv, D_a2[fc - 1] if fc >= 1 else D_a1[1])
                        bump(ac.activation(wsgA[:, :], wchunk[k % 2][:, :], AF.Sign,
                                           bias=csl(C_NHALF), scale=invs[:, 0:1]),
                             "a2")
                        bump(ac.activation(wsgB[:, :], wchunk[k % 2][:, :], AF.Sign,
                                           bias=csl(C_PHALF), scale=invs[:, 0:1]),
                             "a2")
                    for t4 in range(4):
                        psh = (ps_a[:, :], ps_b[:, :], ps_c[:, 0:512],
                               ps_e[:, 0:512])[t4]
                        ac.wait_ge(s_te, T_h[(fc, t4)])
                        if fc >= 2 and t4 == 0:
                            swt(ac, f"hw{fc % 2}", E_ac_hw(fc - 2))
                        bump(ac.activation(hst[p][:, t4 * 512:(t4 + 1) * 512],
                                           psh, AF.Gelu,
                                           bias=b1sb[:, fc:fc + 1],
                                           scale=csl(C_HALF)), "a2")
                    swt(ac, "w2l0", 16 * (fc + 1))
                    ac.wait_ge(s_dv, D_a1[fc + 2] if fc <= 29 else D_a1[31])
                    bump(ac.activation(wsgA[:, :], wchC[fc % 2][:, :], AF.Sign,
                                       bias=csl(C_NHALF), scale=invs[:, 1:2]), "a2")
                    bump(ac.activation(wsgB[:, :], wchC[fc % 2][:, :], AF.Sign,
                                       bias=csl(C_PHALF), scale=invs[:, 1:2]), "a2")
                    sdma(ac, f"hw{fc % 2}", scr_h.ap()[fc, :, :], hst[p][:, :])
                assert ctr["a2"] == nA

            @blk.vector
            def _(dv):
                for k in range(2):
                    dv.wait_ge(s_a2, A_s1[k])
                    if k == 1:
                        dv.wait_ge(s_te, T_t1[0])
                    bump(dv.tensor_add(w1tn[:, :], wsgA[:, :], wsgB[:, :]), "dv")
                    dv.wait_ge(s_te, T_t1[k])
                    bump(dv.tensor_copy(
                        w1st[k].ap().rearrange("p (b f) -> p b f", f=128), ps_d3),
                        "dv")
                for fc in range(32):
                    if fc <= 29:
                        k = fc + 2
                        dv.wait_ge(s_a2, A_s1[k])
                        dv.wait_ge(s_te, T_t1[k - 1])
                        bump(dv.tensor_add(w1tn[:, :], wsgA[:, :], wsgB[:, :]), "dv")
                        dv.wait_ge(s_te, T_t1[k])
                        if fc >= 1:
                            dv.wait_ge(s_te, T_h[(fc - 1, 3)])
                        bump(dv.tensor_copy(
                            w1st[k % 3].ap().rearrange("p (b f) -> p b f", f=128),
                            ps_d3), "dv")
                    dv.wait_ge(s_a2, A_s2[fc])
                    if fc >= 2:
                        dv.wait_ge(s_te, T_t2[fc - 2])
                    bump(dv.tensor_add(w2tn[fc % 2][:, :], wsgA[:, :], wsgB[:, :]),
                         "dv")
                    if fc >= 1:
                        u = fc - 1
                        dblk, fq = divmod(u, 4)
                        dv.wait_ge(s_te, T_t2[u])
                        bump(dv.tensor_copy(
                            W2T3[:, fq * 8:(fq + 1) * 8, dblk * 128:(dblk + 1) * 128],
                            ps_d3), "dv")
                u = 31
                dblk, fq = divmod(u, 4)
                dv.wait_ge(s_te, T_t2[u])
                bump(dv.tensor_copy(
                    W2T3[:, fq * 8:(fq + 1) * 8, dblk * 128:(dblk + 1) * 128],
                    ps_d3), "dv")
                assert ctr["dv"] == nD

            @blk.tensor
            def _(te):
                for k in range(2):
                    te.wait_ge(s_dv, D_a1[k])
                    if k == 1:
                        te.wait_ge(s_dv, D_e1[0])
                    for dblk in range(8):
                        inst = te.transpose(ps_d[:, dblk * 128:(dblk + 1) * 128],
                                            w1tn[:, dblk * 128:(dblk + 1) * 128],
                                            cID[:, :])
                    bump(inst, "te")
                for fc in range(32):
                    te.wait_ge(s_dv, D_e1[fc])
                    for t4 in range(4):
                        psh = (ps_a[:, :], ps_b[:, :], ps_c[:, 0:512],
                               ps_e[:, 0:512])[t4]
                        gi = fc * 4 + t4
                        if gi >= 4:
                            te.wait_ge(s_a2, A_g[((gi - 4) // 4, (gi - 4) % 4)])
                        for dblk in range(8):
                            inst = te.matmul(psh,
                                             w1st[fc % 3][:, dblk * 128:(dblk + 1) * 128],
                                             xnT[:, dblk * T + t4 * 512:
                                                 dblk * T + (t4 + 1) * 512],
                                             start=(dblk == 0), stop=(dblk == 7))
                        bump(inst, "te")
                    if fc <= 29:
                        k = fc + 2
                        te.wait_ge(s_dv, D_a1[k])
                        te.wait_ge(s_dv, D_e1[k - 1] if k <= 3 else D_e2[k - 4])
                        for dblk in range(8):
                            inst = te.transpose(ps_d[:, dblk * 128:(dblk + 1) * 128],
                                                w1tn[:, dblk * 128:(dblk + 1) * 128],
                                                cID[:, :])
                        bump(inst, "te")
                    if fc >= 1:
                        u = fc - 1
                        te.wait_ge(s_dv, D_a2[u])
                        te.wait_ge(s_dv, D_e1[u + 3] if u + 3 <= 31 else D_e2[u - 1])
                        for dblk in range(8):
                            inst = te.transpose(ps_d[:, dblk * 128:(dblk + 1) * 128],
                                                w2tn[u % 2][:, dblk * 128:(dblk + 1) * 128],
                                                cID[:, :])
                        bump(inst, "te")
                u = 31
                te.wait_ge(s_dv, D_a2[u])
                te.wait_ge(s_dv, D_e2[30])
                for dblk in range(8):
                    inst = te.transpose(ps_d[:, dblk * 128:(dblk + 1) * 128],
                                        w2tn[u % 2][:, dblk * 128:(dblk + 1) * 128],
                                        cID[:, :])
                bump(inst, "te")
                assert ctr["te"] == nT

        if stop_after == "B5":
            _early_out([stC, stX])
            return nc

        stC.close()
        stX.close()
        enD = stD.enter_context
        WoT = enD(nc.sbuf_tensor("WoT", [128, 8 * 1024], BF16))
        WoT3 = WoT.ap().rearrange("p (b c) -> p b c", c=1024)
        hgrp = [enD(nc.sbuf_tensor(f"hgrp{i}", [128, 32 * 256], BF16)) for i in range(2)]
        xr = [enD(nc.sbuf_tensor(f"xr{i}", [128, D], F32)) for i in range(2)]
        osb = [enD(nc.sbuf_tensor(f"osb{i}", [128, D], F32)) for i in range(2)]
        t1f = enD(nc.sbuf_tensor("t1f", [128, D], F32))
        t2f = enD(nc.sbuf_tensor("t2f", [128, D], F32))
        bias1bc = enD(nc.sbuf_tensor("bias1bc", [128, D], F32))

        a20 = ctr["a2"]; te0 = ctr["te"]; dv0 = ctr["dv"]
        E_a2_wo = lambda k: a20 + k + 1
        E_te_wo = lambda k: te0 + k + 1
        E_dv_wo = lambda k: dv0 + k + 1
        b6w_base = [streams["w1c0"]["n"], streams["w1c1"]["n"]]
        with nc.Block() as blk:
            @blk.gpsimd
            def _(gp):
                swt(gp, "ld", b0_ld2)
                sdma(gp, "cst", bias1bc[:, :], bass.AP(scr_rows, 0, [[0, 128], [1, D]]))
            b6_cst = streams["cst"]["n"]

            @blk.sync
            def _(sy):
                for eb in range(8):
                    if eb >= 2:
                        sy.wait_ge(s_a2, E_a2_wo(eb - 2))
                    sdma(sy, f"w1c{eb % 2}", wchunk[eb % 2][:, :],
                         Wout.ap()[eb * 128:(eb + 1) * 128, :])

            @blk.scalar
            def _(ac):
                for eb in range(8):
                    swt(ac, f"w1c{eb % 2}", b6w_base[eb % 2] + 16 * (eb // 2 + 1))
                    if eb >= 2:
                        ac.wait_ge(s_te, E_te_wo(eb - 2))
                    bump(ac.activation(hgrp[eb % 2][:, 0:1024], wchunk[eb % 2][:, :],
                                       AF.Copy, bias=0.0, scale=2.0), "a2")

            @blk.tensor
            def _(te):
                for eb in range(8):
                    te.wait_ge(s_a2, E_a2_wo(eb))
                    if eb >= 1:
                        te.wait_ge(s_dv, E_dv_wo(eb - 1))
                    for dblk in range(8):
                        inst = te.transpose(ps_d[:, dblk * 128:(dblk + 1) * 128],
                                            hgrp[eb % 2][:, dblk * 128:(dblk + 1) * 128],
                                            cID[:, :])
                    bump(inst, "te")

            @blk.vector
            def _(dv):
                dv = _DR(dv)
                for eb in range(8):
                    dv.wait_ge(s_te, E_te_wo(eb))
                    bump(dv.tensor_copy(WoT3[:, :, eb * 128:(eb + 1) * 128], ps_d3), "dv")

        te0 = ctr["te"]; dv0 = ctr["dv"]
        E_te_o = lambda tg, tk: te0 + 2 * tg + tk + 1
        E_dv_oe = lambda tg, tk: dv0 + 2 * tg + tk + 1
        E_gp_ow = lambda tg: 16 * (tg + 1)
        E_ac_xr = lambda tg: 16 * (tg + 1)
        E_hg = lambda tg: 16 * 16 * (tg // 2 + 1)
        b5_hw = [streams["hw0"]["n"], streams["hw1"]["n"]]
        with nc.Block() as blk:
            @blk.sync
            def _(sy):
                swt(sy, "hw0", b5_hw[0])
                swt(sy, "hw1", b5_hw[1])
                for tg in range(8):
                    if tg >= 2:
                        sy.wait_ge(s_te, E_te_o(tg - 2, 1))
                    for fc in range(0, 32, 2):
                        sdma(sy, f"hge{tg % 2}", hgrp[tg % 2][:, fc * 256:(fc + 1) * 256],
                             scr_h.ap()[fc, :, tg * 256:(tg + 1) * 256])

            @blk.scalar
            def _(ac):
                ac = _DR(ac)
                swt(ac, "hw0", b5_hw[0])
                swt(ac, "hw1", b5_hw[1])
                for tg in range(8):
                    if tg >= 2:
                        ac.wait_ge(s_te, E_te_o(tg - 2, 1))
                    for fc in range(1, 32, 2):
                        sdma(ac, f"hgo{tg % 2}", hgrp[tg % 2][:, fc * 256:(fc + 1) * 256],
                             scr_h.ap()[fc, :, tg * 256:(tg + 1) * 256])
                    for tk in range(2):
                        tci = tg * 2 + tk
                        if tg >= 1:
                            ac.wait_ge(s_dv, E_dv_oe(tg - 1, tk))
                        sdma(ac, f"xr{tk}", xr[tk][:, :],
                             x_in.ap()[tci * 128:(tci + 1) * 128, :])

            @blk.tensor
            def _(te):
                for tg in range(8):
                    for tk in range(2):
                        pso = ps_c if tk == 0 else ps_e
                        swt(te, f"hge{tg % 2}", E_hg(tg))
                        swt(te, f"hgo{tg % 2}", E_hg(tg))
                        if tg >= 1:
                            te.wait_ge(s_dv, E_dv_oe(tg - 1, tk))
                        for dh in range(2):
                            sl = slice(dh * 512, (dh + 1) * 512)
                            for fc in range(32):
                                te.matmul(pso[:, sl],
                                          hgrp[tg % 2][:, fc * 256 + tk * 128:
                                                       fc * 256 + (tk + 1) * 128],
                                          W2T[:, fc * 1024 + dh * 512:
                                              fc * 1024 + (dh + 1) * 512],
                                          start=(fc == 0), stop=False)
                            for dblk in range(8):
                                inst = te.matmul(
                                    pso[:, sl],
                                    yT[:, dblk * T + (tg * 2 + tk) * 128:
                                       dblk * T + (tg * 2 + tk + 1) * 128],
                                    WoT[:, dblk * 1024 + dh * 512:
                                        dblk * 1024 + (dh + 1) * 512],
                                    start=False, stop=(dblk == 7))
                        bump(inst, "te")

            @blk.vector
            def _(dv):
                dv = _DR(dv)
                swt(dv, "cst", b6_cst)
                for tg in range(8):
                    for tk in range(2):
                        pso = ps_c if tk == 0 else ps_e
                        tmp = t1f if tk == 0 else t2f
                        tci = tg * 2 + tk
                        swt(dv, f"xr{tk}", E_ac_xr(tg))
                        dv.wait_ge(s_te, E_te_o(tg, tk))
                        if tg >= 1:
                            swt(dv, f"ow{tk}", E_gp_ow(tg - 1))
                        dv.tensor_scalar(tmp[:, :], xr[tk][:, :],
                                         stat_a[:, tci:tci + 1], stat_b[:, tci:tci + 1],
                                         ALU.mult, ALU.add)
                        dv.tensor_add(tmp[:, :], tmp[:, :], bias1bc[:, :])
                        dv.scalar_tensor_tensor(osb[tk][:, 0:512], pso[:, 0:512],
                                                csl(C_HALF), tmp[:, 0:512],
                                                ALU.mult, ALU.add)
                        dv._e.drain()
                        bump(dv.scalar_tensor_tensor(osb[tk][:, 512:1024],
                                                     pso[:, 512:1024], csl(C_HALF),
                                                     tmp[:, 512:1024], ALU.mult, ALU.add),
                             "dv")

            @blk.gpsimd
            def _(gp):
                for tg in range(8):
                    for tk in range(2):
                        tci = tg * 2 + tk
                        gp.wait_ge(s_dv, E_dv_oe(tg, tk))
                        sdma(gp, f"ow{tk}", out_t.ap()[tci * 128:(tci + 1) * 128, :],
                             osb[tk][:, :])
                swt(gp, "ow0", streams["ow0"]["n"])
                swt(gp, "ow1", streams["ow1"]["n"])

        with nc.Block() as blk:
            @blk.sync
            def _(sy):
                for nm in ["ld", "x0", "x1", "iv", "x20", "x21", "w1w0", "w1w1",
                           "w1c0", "w1c1", "hge0", "hge1", "wq0", "wq1"]:
                    swt(sy, nm, streams[nm]["n"])

            @blk.scalar
            def _(ac):
                for nm in ["xn0", "xn1", "hw0", "hw1", "hgo0", "hgo1", "xr0", "xr1",
                           "tap"]:
                    swt(ac, nm, streams[nm]["n"])

            @blk.gpsimd
            def _(gp):
                for nm in ["cst", "mk", "b3l", "wp0", "wp1", "w1l0", "w1l1",
                           "w2l0", "w2l1", "tapg", "tg0", "tg1"]:
                    swt(gp, nm, streams[nm]["n"])

        stD.close()

    return nc


def kernel(**inputs):
    consts = _host_constants()
    nc = build_nc(debug_taps=_DEBUG_TAPS, stop_after=_STOP_AFTER)
    x = np.ascontiguousarray(np.asarray(inputs["L0_attn_x"], dtype=np.float32))
    shared = {k: np.ascontiguousarray(np.asarray(v, dtype=np.float32))
              for k, v in inputs.items() if k != "L0_attn_x"}
    shared.update(consts)
    in_maps = []
    for i in range(8):
        mm = dict(shared)
        mm["x"] = np.ascontiguousarray(x[i])
        in_maps.append(mm)
    res = run_bass_kernel_spmd(nc, in_maps, core_ids=list(range(8)))
    return np.stack([np.asarray(r["out"], dtype=np.float32) for r in res.results])
